# version 15
# baseline (speedup 1.0000x reference)
import contextlib
import numpy as np
import ml_dtypes
import concourse.bass as bass
import concourse.mybir as mybir
from concourse.bass_utils import run_bass_kernel_spmd

F32 = mybir.dt.float32
BF16 = mybir.dt.bfloat16
AF = mybir.ActivationFunctionType
ALU = mybir.AluOpType
AX = mybir.AxisListType

T = 4096
D = 1024
NC = 8
H = 16
DH = 64
DFF = 4096
NEXP = 8
HID = 1408
NHC = 11
LN_EPS = 1e-5
MAGIC = 12582912.0
NEG = -3.0e38

SCRATCH_KIND = "Internal"
PHASES = None


class Prog:

    ENG = ("t", "v", "s", "g", "y")

    def __init__(self, nc):
        self.nc = nc
        self.eng = {"t": nc.tensor, "v": nc.vector, "s": nc.scalar,
                    "g": nc.gpsimd, "y": nc.sync}
        self.semkeys = ["t", "v", "s", "g", "yd", "gd", "sd"]
        self.sem = {k: nc.alloc_semaphore("sem_" + k) for k in self.semkeys}
        self.cnt = {k: 0 for k in self.semkeys}
        self.writer = {}
        self.readers = {}
        self.hwm = {e: {} for e in self.ENG}
        self.fence_vals = None

    def guard(self, cm):
        pr = self

        class _G:
            def __enter__(self):
                return cm.__enter__()

            def __exit__(self, *e):
                r = cm.__exit__(*e)
                pr.fence()
                return r

        return _G()

    def scope(self):
        pr = self

        class _S(contextlib.ExitStack):
            def __exit__(self, *exc):
                r = super().__exit__(*exc)
                pr.fence()
                return r

        return _S()

    def fence(self):
        self.fence_vals = dict(self.cnt)

    @staticmethod
    def key(ap):
        t = getattr(ap, "tensor", ap)
        return getattr(t, "name", str(t))

    def _wait(self, ek, semkey, val):
        if val <= 0:
            return
        if self.hwm[ek].get(semkey, 0) >= val:
            return
        self.eng[ek].wait_ge(self.sem[semkey], val)
        self.hwm[ek][semkey] = val

    def _dval(self, sk, v):
        return self.cnt[sk] if sk.endswith("d") else v

    def _edges(self, ek, ins, outs, skip_tensor_internal):
        if self.fence_vals is not None:
            for sk, v in self.fence_vals.items():
                self._wait(ek, sk, v)
        for ap in ins:
            w = self.writer.get(self.key(ap))
            if w and not (skip_tensor_internal and w[0] == "t"):
                self._wait(ek, w[0], self._dval(w[0], w[1]))
        for ap in outs:
            k = self.key(ap)
            for sk, v in self.readers.get(k, {}).items():
                if skip_tensor_internal and sk == "t":
                    continue
                self._wait(ek, sk, self._dval(sk, v))
            w = self.writer.get(k)
            if w and not (skip_tensor_internal and w[0] == "t"):
                self._wait(ek, w[0], self._dval(w[0], w[1]))

    def _update(self, semkey, val, ins, outs):
        for ap in outs:
            self.writer[self.key(ap)] = (semkey, val)
            self.readers[self.key(ap)] = {}
        for ap in ins:
            self.readers.setdefault(self.key(ap), {})[semkey] = val

    def do(self, ek, opname, *args, ins=(), outs=(), **kw):
        skip = ek == "t"
        self._edges(ek, ins, outs, skip)
        inst = getattr(self.eng[ek], opname)(*args, **kw)
        self.cnt[ek] += 1
        inst.then_inc(self.sem[ek], 1)
        v = self.cnt[ek]
        self._update(ek, v, ins, outs)
        return inst

    def dma(self, qk, out, in_, **kw):
        semkey = qk + "d"
        self._edges(qk, [in_], [out], False)
        inst = self.eng[qk].dma_start(out=out, in_=in_, **kw)
        self.cnt[semkey] += 16
        inst.then_inc(self.sem[semkey], 16)
        self._update(semkey, self.cnt[semkey], [in_], [out])
        return inst

    def barrier_on(self, ek, semkeys=None):
        for sk in (semkeys or self.semkeys):
            self._wait(ek, sk, self.cnt[sk])



def sb_alloc(nc, stack, *specs):
    return tuple(stack.enter_context(nc.sbuf_tensor(n, s, t, allow_name_mangling=True)) for n, s, t in specs)


def ps_alloc(nc, stack, *specs):
    return tuple(stack.enter_context(nc.psum_tensor(n, s, t, allow_name_mangling=True)) for n, s, t in specs)


def _bf(x):
    return np.ascontiguousarray(x, dtype=ml_dtypes.bfloat16)


def _f32(x):
    return np.ascontiguousarray(x, dtype=np.float32)


def _fmvec(v, nchunk=NC):
    return _f32(np.asarray(v, np.float32).reshape(nchunk, 128).T)


def host_constants():
    c = {}
    c["eye_bf"] = _bf(np.eye(128))
    c["eye_f32"] = _f32(np.eye(128))
    c["ones_row_bf"] = _bf(np.ones((1, 128)))
    c["ones_row_f32"] = _f32(np.ones((1, 128)))
    c["ones_col_bf"] = _bf(np.ones((128, 1)))
    c["pos_row"] = _f32(np.arange(T, dtype=np.float32)[None, :])
    c["upos"] = _f32(np.arange(128)[:, None] + 128 * np.arange(32)[None, :])
    ul = np.arange(128)[:, None]
    j = np.arange(512)[None, :]
    c["cdelta"] = _f32(np.stack([np.maximum(0.0, ul - j + d0) * 2.0
                                 for d0 in (0, 128, 256, 384)]))
    jj = np.arange(128)[None, :]
    c["tri"] = _f32(np.where(ul > jj, NEG, 0.0))
    hh = np.arange(16)[None, :]
    c["headmask"] = _f32(np.where(ul < hh, NEG, 0.0))
    mA = np.zeros((1, 128), np.float32); mA[0, :64] = 1.0
    mB = np.zeros((1, 128), np.float32); mB[0, 64:] = 1.0
    c["epscol"] = _f32(np.full((128, 1), 1e-5))
    c["negbig"] = _f32(np.full((128, 1), -1e30))
    c["maskA_bf"] = _bf(mA)
    c["maskB_bf"] = _bf(mB)
    c["maskA_f32"] = _f32(mA)
    c["maskB_f32"] = _f32(mB)
    return c


def prep_host(inputs):
    a = {k: np.asarray(v) for k, v in inputs.items()}
    h = {}
    for w in ("Wq", "Wk", "Wv", "Wo"):
        h["L0_" + w] = _f32(a["L0_attn_" + w])
    for b in ("bq", "bk", "bv", "bo"):
        h["L0_" + b] = _fmvec(a["L0_attn_" + b])
    h["L0_ds"] = _f32(a["L0_attn_dist_scale"].reshape(1, 1))
    h["L0_W1"] = _f32(a["L0_attn_W1"])
    h["L0_b1"] = _fmvec(a["L0_attn_b1"], DFF // 128)
    h["L0_W2"] = _f32(a["L0_attn_W2"])
    h["L0_b2"] = _fmvec(a["L0_attn_b2"])
    h["L0_ln1g"] = _fmvec(a["L0_attn_ln1_g"])
    h["L0_ln1b"] = _fmvec(a["L0_attn_ln1_b"])
    h["L0_ln2g"] = _fmvec(a["L0_attn_ln2_g"])
    h["L0_ln2b"] = _fmvec(a["L0_attn_ln2_b"])
    h["L0_ln2g_row"] = _f32(np.asarray(a["L0_attn_ln2_g"], np.float32).reshape(1, D))
    h["L0_ln2b_row"] = _f32(np.asarray(a["L0_attn_ln2_b"], np.float32).reshape(1, D))
    h["L0_fk"] = _f32(a["L0_ffn_kernel"])
    h["L0_fb"] = _fmvec(a["L0_ffn_bias"])
    h["L0_fa"] = _f32(a["L0_ffn_a"])
    h["L0_fbb"] = _f32(a["L0_ffn_b"])
    h["L1_wqkv"] = _f32(a["L1_attn_w_qkv"].T)
    h["L1_wout"] = _f32(a["L1_attn_w_out"].T)
    h["L1_gate"] = _f32(a["L1_ffn_gate_w"].T)
    h["L1_ebias"] = _f32(a["L1_ffn_expert_bias"].reshape(1, 8))
    h["L1_wup"] = _f32(np.ascontiguousarray(a["L1_ffn_w_up"].transpose(0, 2, 1)))
    h["L1_wdn"] = _f32(np.ascontiguousarray(a["L1_ffn_w_down"].transpose(0, 2, 1)))
    h["L1_swup"] = _f32(a["L1_ffn_shared_w_up"].T)
    h["L1_swdn"] = _f32(a["L1_ffn_shared_w_down"].T)
    h.update(host_constants())
    return h


HOST_SPECS = None


def declare_io(nc, host_shapes):
    d = {}
    d["x0"] = nc.dram_tensor("x0", [T, D], F32, kind="ExternalInput").ap()
    for name, (shape, dt) in host_shapes.items():
        d[name] = nc.dram_tensor(name, list(shape), dt, kind="ExternalInput").ap()
    d["out"] = nc.dram_tensor("out", [T, D], F32, kind="ExternalOutput").ap()
    sk = SCRATCH_KIND
    for name, shape, dt in [
        ("xn_fm", [D, T], BF16),
        ("g_nat", [T, D], F32),
        ("o_fm", [D, T], BF16),
        ("x1_fm", [D, T], BF16),
        ("xres_nat", [T, D], F32),
        ("w2t", [8, T], F32),
    ]:
        d[name] = nc.dram_tensor(name, shape, dt, kind=sk).ap()
    return d


def fm3(ap_dram, t0, tw, c0=0, ncs=NC):
    return ap_dram.rearrange("(c p) t -> p c t", p=128)[:, c0:c0 + ncs, t0:t0 + tw]


class Consts:

    def __init__(self, pr, d):
        nc = pr.nc
        self.eye_bf = nc.alloc_sbuf_tensor("c_eye_bf", [128, 128], BF16)
        self.eye_f32 = nc.alloc_sbuf_tensor("c_eye_f32", [128, 128], F32)
        self.ones_row_bf = nc.alloc_sbuf_tensor("c_ones_row", [1, 128], BF16)
        self.ones_row_f32 = nc.alloc_sbuf_tensor("c_ones_row_f32", [1, 128], F32)
        self.ones_col_bf = nc.alloc_sbuf_tensor("c_ones_col", [128, 1], BF16)
        self.pos_row = nc.alloc_sbuf_tensor("c_pos_row", [1, T], F32)
        self.upos = nc.alloc_sbuf_tensor("c_upos", [128, 32], F32)
        self.cdelta = nc.alloc_sbuf_tensor("c_cdelta", [128, 4 * 512], F32)
        self.tri = nc.alloc_sbuf_tensor("c_tri", [128, 128], F32)
        self.headmask = nc.alloc_sbuf_tensor("c_headmask", [128, 16], F32)
        self.epscol = nc.alloc_sbuf_tensor("c_epscol", [128, 1], F32)
        self.negbig = nc.alloc_sbuf_tensor("c_negbig", [128, 1], F32)
        self.maskA = nc.alloc_sbuf_tensor("c_maskA", [1, 128], BF16)
        self.maskB = nc.alloc_sbuf_tensor("c_maskB", [1, 128], BF16)
        self.maskAf = nc.alloc_sbuf_tensor("c_maskAf", [1, 128], F32)
        self.maskBf = nc.alloc_sbuf_tensor("c_maskBf", [1, 128], F32)
        pr.dma("y", self.eye_bf[:], d["eye_bf"])
        pr.dma("y", self.eye_f32[:], d["eye_f32"])
        pr.dma("y", self.ones_row_bf[:], d["ones_row_bf"])
        pr.dma("y", self.ones_row_f32[:], d["ones_row_f32"])
        pr.dma("y", self.ones_col_bf[:], d["ones_col_bf"])
        pr.dma("y", self.pos_row[:], d["pos_row"])
        pr.dma("y", self.upos[:], d["upos"])
        for k in range(4):
            pr.dma("y", self.cdelta[:, k * 512:(k + 1) * 512], d["cdelta"][k])
        pr.dma("y", self.tri[:], d["tri"])
        pr.dma("y", self.headmask[:], d["headmask"])
        pr.dma("y", self.epscol[:], d["epscol"])
        pr.dma("y", self.negbig[:], d["negbig"])
        pr.dma("y", self.maskA[:], d["maskA_bf"])
        pr.dma("y", self.maskB[:], d["maskB_bf"])
        pr.dma("y", self.maskAf[:], d["maskA_f32"])
        pr.dma("y", self.maskBf[:], d["maskB_f32"])


def emit_ln_to_fm(pr, C, src_nat_dram, dst_fm_dram, gain_sb=None, bias_sb=None,
                  src2_info=None, tag=""):
    pr.fence()
    """dst_fm = LN(src_nat [+ src2]) transposed to FM (bf16), plain LN with
    optional per-feature affine applied in FM after the transpose.

    gain_sb/bias_sb: SBUF [128, NC] FM vectors or None.
    src2_info: unused for now.
    """
    nc = pr.nc
    with (
        pr.guard(nc.sbuf_tensor(f"ln{tag}_x0", [128, D], F32, allow_name_mangling=True)) as xt0,
        pr.guard(nc.sbuf_tensor(f"ln{tag}_x1", [128, D], F32, allow_name_mangling=True)) as xt1,
        pr.guard(nc.sbuf_tensor(f"ln{tag}_xn0", [128, D], BF16, allow_name_mangling=True)) as xn0,
        pr.guard(nc.sbuf_tensor(f"ln{tag}_xn1", [128, D], BF16, allow_name_mangling=True)) as xn1,
        pr.guard(nc.sbuf_tensor(f"ln{tag}_st", [128, 8], F32, allow_name_mangling=True)) as stt0,
        pr.guard(nc.sbuf_tensor(f"ln{tag}_st1", [128, 8], F32, allow_name_mangling=True)) as stt1,
        pr.guard(nc.sbuf_tensor(f"ln{tag}_sq", [128, D], F32, allow_name_mangling=True)) as sq0,
        pr.guard(nc.sbuf_tensor(f"ln{tag}_sq1", [128, D], F32, allow_name_mangling=True)) as sq1,
        pr.guard(nc.sbuf_tensor(f"ln{tag}_o0", [128, 1024], BF16, allow_name_mangling=True)) as o0,
        pr.guard(nc.sbuf_tensor(f"ln{tag}_o1", [128, 1024], BF16, allow_name_mangling=True)) as o1,
        pr.guard(nc.psum_tensor(f"ln{tag}_p0", [128, 1024], BF16, allow_name_mangling=True)) as p0,
        pr.guard(nc.psum_tensor(f"ln{tag}_p1", [128, 1024], BF16, allow_name_mangling=True)) as p1,
    ):
        xts, xns, os_, ps = [xt0, xt1], [xn0, xn1], [o0, o1], [p0, p1]
        for i in range(T // 128):
            t0 = i * 128
            xt, xn, ot, pt = xts[i % 2], xns[i % 2], os_[i % 2], ps[i % 2]
            stt, sq = (stt0, sq0) if i % 2 == 0 else (stt1, sq1)
            pr.dma("y", xt[:], src_nat_dram[t0:t0 + 128, :])
            pr.do("v", "reduce_sum", stt[:, 0:1], xt[:], ins=[xt], outs=[stt], axis=AX.X)
            pr.do("s", "activation", sq[:], xt[:], AF.Square, ins=[xt], outs=[sq, stt],
                  accum_out=stt[:, 1:2])
            pr.do("v", "tensor_scalar", stt[:, 2:4], stt[:, 0:2], 1.0 / D, None,
                  ALU.mult, ins=[stt, sq], outs=[stt])
            pr.do("v", "tensor_mul", stt[:, 4:5], stt[:, 2:3], stt[:, 2:3],
                  ins=[stt], outs=[stt])
            pr.do("v", "tensor_sub", stt[:, 4:5], stt[:, 3:4], stt[:, 4:5],
                  ins=[stt], outs=[stt])
            pr.do("s", "activation", stt[:, 5:6], stt[:, 4:5], AF.Sqrt,
                  bias=C.epscol[:], ins=[stt, C.epscol], outs=[stt])
            pr.do("v", "reciprocal", stt[:, 6:7], stt[:, 5:6], ins=[stt], outs=[stt])
            pr.do("v", "tensor_scalar", xn[:], xt[:], stt[:, 2:3], stt[:, 6:7],
                  ALU.subtract, ALU.mult, ins=[xt, stt], outs=[xn])
            for c in range(NC):
                pr.do("t", "transpose", pt[:, c * 128:(c + 1) * 128],
                      xn[:, c * 128:(c + 1) * 128], C.eye_bf[:],
                      ins=[xn, C.eye_bf], outs=[pt])
            if gain_sb is not None:
                for c in range(NC):
                    pr.do("v", "tensor_scalar", ot[:, c * 128:(c + 1) * 128],
                          pt[:, c * 128:(c + 1) * 128],
                          gain_sb[:, c:c + 1], bias_sb[:, c:c + 1],
                          ALU.mult, ALU.add, ins=[pt, gain_sb, bias_sb], outs=[ot])
            else:
                pr.do("v", "tensor_copy", ot[:], pt[:], ins=[pt], outs=[ot])
            pr.dma("y", fm3(dst_fm_dram, t0, 128),
                   ot[:].rearrange("p (c t) -> p c t", c=NC))
    return


def load_fm_block(pr, qk, sb, fm_dram, t0, tw, ncs=NC, c0=0):
    pr.dma(qk, sb[:].rearrange("p (c t) -> p c t", c=ncs), fm3(fm_dram, t0, tw, c0, ncs))


def emit_quant(pr, C, d, wq_sb):
    pr.fence()
    """Quantize L0_ffn kernel int4-style into wq_sb [128, NC*1024] bf16."""
    nc = pr.nc
    with (
        pr.guard(nc.sbuf_tensor("qk_w", [128, NC * 1024], F32, allow_name_mangling=True)) as wk,
        pr.guard(nc.sbuf_tensor("qk_am", [128, 9], F32, allow_name_mangling=True)) as am,
        pr.guard(nc.sbuf_tensor("qk_amr", [1, 128], F32, allow_name_mangling=True)) as amr,
        pr.guard(nc.sbuf_tensor("qk_sc", [1, 4], F32, allow_name_mangling=True)) as sc,
        pr.guard(nc.sbuf_tensor("qk_scb", [128, 2], F32, allow_name_mangling=True)) as scb,
        pr.guard(nc.sbuf_tensor("qk_q", [128, 1024], F32, allow_name_mangling=True)) as qq,
        pr.guard(nc.psum_tensor("qk_p", [128, 512], F32, allow_name_mangling=True)) as pp,
        pr.guard(nc.psum_tensor("qk_pb", [128, 512], F32, allow_name_mangling=True)) as pb,
    ):
        load_fm_block(pr, "y", wk, d["L0_fk"], 0, 1024)
        for c in range(NC):
            pr.do("v", "reduce_max", am[:, c:c + 1], wk[:, c * 1024:(c + 1) * 1024],
                  ins=[wk], outs=[am], axis=AX.X, apply_absolute_value=True)
        pr.do("v", "reduce_max", am[:, 8:9], am[:, 0:8], ins=[am], outs=[am],
              axis=AX.X, apply_absolute_value=True)
        pr.do("t", "transpose", pp[0:1, 0:128], am[:, 8:9],
              C.eye_f32[:], ins=[am, C.eye_f32], outs=[pp])
        pr.do("v", "reduce_max", sc[:, 0:1], pp[0:1, 0:128], ins=[pp], outs=[sc],
              axis=AX.X, apply_absolute_value=True)
        pr.do("v", "tensor_scalar_mul", sc[:, 1:2], sc[:, 0:1], 1.0 / 7.0,
              ins=[sc], outs=[sc])
        pr.do("v", "reciprocal", sc[:, 2:3], sc[:, 1:2], ins=[sc], outs=[sc])
        pr.do("t", "matmul", pb[:, 0:2], C.ones_row_f32[:], sc[:, 1:3],
              ins=[C.ones_row_f32, sc], outs=[pb], start=True, stop=True)
        pr.do("v", "tensor_copy", scb[:], pb[:, 0:2], ins=[pb], outs=[scb])
        for c in range(NC):
            w_c = wk[:, c * 1024:(c + 1) * 1024]
            pr.do("v", "tensor_scalar", qq[:], w_c, scb[:, 1:2], MAGIC,
                  ALU.mult, ALU.add, ins=[wk, scb], outs=[qq])
            pr.do("v", "tensor_scalar", wq_sb[:, c * 1024:(c + 1) * 1024], qq[:],
                  MAGIC, scb[:, 0:1], ALU.subtract, ALU.mult,
                  ins=[qq, scb], outs=[wq_sb])


def emit_gffn_l0(pr, C, d):
    pr.fence()
    """g = silu(xn @ quant(kernel) + bias + (xn@a)@b), FM f32 -> g_fm dram."""
    nc = pr.nc
    with (
        pr.guard(nc.sbuf_tensor("gf_wq", [128, NC * 1024], BF16, allow_name_mangling=True)) as wq,
        pr.guard(nc.sbuf_tensor("gf_a", [128, NC * 16], BF16, allow_name_mangling=True)) as a_sb,
        pr.guard(nc.sbuf_tensor("gf_b", [16, 1024], BF16, allow_name_mangling=True)) as b_sb,
        pr.guard(nc.sbuf_tensor("gf_bias", [128, NC], F32, allow_name_mangling=True)) as bias_sb,
    ):
        emit_quant(pr, C, d, wq)
        pr.fence()
        pr.dma("g", a_sb[:].rearrange("p (c n) -> p c n", c=NC),
               d["L0_fa"].rearrange("(c p) n -> p c n", p=128))
        pr.dma("g", b_sb[:], d["L0_fbb"])
        pr.dma("y", bias_sb[:], d["L0_fb"])
        with (
            pr.guard(nc.sbuf_tensor("gf_xn0", [128, NC * 512], BF16, allow_name_mangling=True)) as xnb0,
            pr.guard(nc.sbuf_tensor("gf_xn1", [128, NC * 512], BF16, allow_name_mangling=True)) as xnb1,
            pr.guard(nc.sbuf_tensor("gf_l1", [16, 512], BF16, allow_name_mangling=True)) as l1,
            pr.guard(nc.sbuf_tensor("gf_gfm", [128, NC * 512], F32, allow_name_mangling=True)) as gfm,
            pr.guard(nc.sbuf_tensor("gf_gn0", [128, 1024], F32, allow_name_mangling=True)) as gn0,
            pr.guard(nc.sbuf_tensor("gf_gn1", [128, 1024], F32, allow_name_mangling=True)) as gn1,
            pr.guard(nc.psum_tensor("gf_pl", [16, 512], F32, allow_name_mangling=True)) as pl,
            pr.guard(nc.psum_tensor("gf_p0", [128, 512], F32, allow_name_mangling=True)) as pp0,
            pr.guard(nc.psum_tensor("gf_p1", [128, 512], F32, allow_name_mangling=True)) as pp1,
            pr.guard(nc.psum_tensor("gf_pn0", [128, 1024], F32, allow_name_mangling=True)) as pn0,
            pr.guard(nc.psum_tensor("gf_pn1", [128, 1024], F32, allow_name_mangling=True)) as pn1,
        ):
            xnbs, pps = [xnb0, xnb1], [pp0, pp1]
            gns, pns = [gn0, gn1], [pn0, pn1]
            for it in range(T // 512):
                t0 = it * 512
                xnb = xnbs[it % 2]
                load_fm_block(pr, "y", xnb, d["xn_fm"], t0, 512)
                for c in range(NC):
                    pr.do("t", "matmul", pl[:], a_sb[:, c * 16:(c + 1) * 16],
                          xnb[:, c * 512:(c + 1) * 512], ins=[a_sb, xnb], outs=[pl],
                          start=(c == 0), stop=(c == NC - 1))
                pr.do("v", "tensor_copy", l1[:], pl[:], ins=[pl], outs=[l1])
                for do in range(NC):
                    pp = pps[do % 2]
                    for c in range(NC):
                        pr.do("t", "matmul", pp[:],
                              wq[:, c * 1024 + do * 128: c * 1024 + do * 128 + 128],
                              xnb[:, c * 512:(c + 1) * 512], ins=[wq, xnb], outs=[pp],
                              start=(c == 0), stop=False)
                    pr.do("t", "matmul", pp[:], b_sb[:, do * 128:(do + 1) * 128],
                          l1[:], ins=[b_sb, l1], outs=[pp], start=False, stop=True)
                    pr.do("s", "activation", gfm[:, do * 512:(do + 1) * 512], pp[:],
                          AF.Silu, bias=bias_sb[:, do:do + 1], ins=[pp, bias_sb],
                          outs=[gfm])
                for sub in range(4):
                    pn = pns[sub % 2]
                    for do in range(NC):
                        pr.do("t", "transpose", pn[:, do * 128:(do + 1) * 128],
                              gfm[:, do * 512 + sub * 128: do * 512 + sub * 128 + 128],
                              C.eye_f32[:], ins=[gfm, C.eye_f32], outs=[pn])
                    gn = gns[sub % 2]
                    pr.do("v", "tensor_copy", gn[:], pn[:], ins=[pn], outs=[gn])
                    pr.dma("g", d["g_nat"][t0 + sub * 128: t0 + sub * 128 + 128, :], gn[:])



def emit_moe(pr, C, d):
    nc = pr.nc
    pr.fence()
    with pr.scope() as _st:
        with pr.scope() as _sg:
            gsb, ebb, xnt, scn, bn, b2n, m1, msk, msk2, w8 = sb_alloc(
                nc, _sg,
                ("mo_gsb", [128, NC * 8], BF16), ("mo_ebb", [128, 8], F32),
                ("mo_xnt", [128, NC * 128], BF16), ("mo_scn", [128, 8], F32),
                ("mo_bn", [128, 8], F32), ("mo_b2n", [128, 8], F32),
                ("mo_m1", [128, 2], F32), ("mo_msk", [128, 8], F32),
                ("mo_msk2", [128, 8], F32), ("mo_w8", [8, 128], F32))
            (ebr,) = sb_alloc(nc, _sg, ("mo_ebr", [1, 8], F32))
            pg, ptr, peb = ps_alloc(nc, _sg, ("mo_pg", [128, 512], F32),
                                    ("mo_ptr", [8, 512], F32),
                                    ("mo_peb", [128, 512], F32))
            pr.dma("g", gsb[:].rearrange("p (c n) -> p c n", c=NC),
                   d["L1_gate"].rearrange("(c p) n -> p c n", p=128))
            pr.dma("y", ebr[:], d["L1_ebias"])
            pr.do("t", "matmul", peb[:, 0:8], C.ones_row_f32[:], ebr[:],
                  ins=[C.ones_row_f32, ebr], outs=[peb], start=True, stop=True)
            pr.do("v", "tensor_copy", ebb[:], peb[:, 0:8], ins=[peb], outs=[ebb])
            for tc in range(T // 128):
                t0 = tc * 128
                load_fm_block(pr, "y", xnt, d["xn_fm"], t0, 128)
                for c in range(NC):
                    pr.do("t", "matmul", pg[:, 0:8], xnt[:, c * 128:(c + 1) * 128],
                          gsb[:, c * 8:(c + 1) * 8], ins=[xnt, gsb], outs=[pg],
                          start=(c == 0), stop=(c == NC - 1))
                pr.do("s", "activation", scn[:], pg[:, 0:8], AF.Sigmoid,
                      ins=[pg], outs=[scn])
                pr.do("v", "tensor_add", bn[:], scn[:], ebb[:],
                      ins=[scn, ebb], outs=[bn])
                pr.do("v", "reduce_max", m1[:, 0:1], bn[:], ins=[bn], outs=[m1],
                      axis=AX.X)
                pr.do("v", "tensor_scalar", msk[:], bn[:], m1[:, 0:1], None,
                      ALU.is_ge, ins=[bn, m1], outs=[msk])
                pr.do("v", "scalar_tensor_tensor", b2n[:], msk[:], C.negbig[:],
                      bn[:], ALU.mult, ALU.add, ins=[msk, C.negbig, bn], outs=[b2n])
                pr.do("v", "reduce_max", m1[:, 1:2], b2n[:], ins=[b2n], outs=[m1],
                      axis=AX.X)
                pr.do("v", "tensor_scalar", msk2[:], b2n[:], m1[:, 1:2], None,
                      ALU.is_ge, ins=[b2n, m1], outs=[msk2])
                pr.do("v", "tensor_add", msk[:], msk[:], msk2[:],
                      ins=[msk, msk2], outs=[msk])
                pr.do("v", "tensor_mul", msk[:], msk[:], scn[:],
                      ins=[msk, scn], outs=[msk])
                pr.do("v", "tensor_mul", msk[:], msk[:], msk[:],
                      ins=[msk], outs=[msk])
                pr.do("t", "transpose", ptr[0:8, 0:128], msk[:], C.eye_f32[:],
                      ins=[msk, C.eye_f32], outs=[ptr])
                pr.do("v", "tensor_copy", w8[:], ptr[0:8, 0:128],
                      ins=[ptr], outs=[w8])
                pr.dma("g", d["w2t"][:, t0:t0 + 128], w8[:])
        pr.fence()
        with pr.scope() as _se:
            (macc,) = sb_alloc(nc, _se, ("mo_macc", [128, NC * 2048], F32))
            for th in range(2):
                h0 = th * 2048
                with pr.scope() as _se2:
                    wup, wdn, xnb0, xnb1, h2, rr0, rr1, tt_, wb, w2r = sb_alloc(
                        nc, _se2,
                        ("mo_wup", [128, NC * HID], BF16),
                        ("mo_wdn", [128, NHC * 1024], BF16),
                        ("mo_xnb0", [128, NC * 512], BF16),
                        ("mo_xnb1", [128, NC * 512], BF16),
                        ("mo_h2", [128, NHC * 512], BF16),
                        ("mo_rr0", [128, 512], BF16), ("mo_rr1", [128, 512], BF16),
                        ("mo_tt", [128, 512], BF16), ("mo_wb", [128, 512], BF16),
                        ("mo_w2r", [1, 512], F32))
                    pwb, ph0, ph1, py0, py1 = ps_alloc(
                        nc, _se2, ("mo_pwb", [128, 512], F32),
                        ("mo_ph0", [128, 512], F32), ("mo_ph1", [128, 512], F32),
                        ("mo_py0", [128, 512], F32), ("mo_py1", [128, 512], F32))
                    xnbs, phs, pys, rrs = [xnb0, xnb1], [ph0, ph1], [py0, py1], [rr0, rr1]
                    for e in range(NEXP + 1):
                        shared = e == NEXP
                        if shared:
                            up_src = d["L1_swup"].rearrange("(c p) n -> p c n", p=128)
                            dn_src = d["L1_swdn"].rearrange("(c p) n -> p c n", p=128)
                        else:
                            up_src = d["L1_wup"][e].rearrange("(c p) n -> p c n", p=128)
                            dn_src = d["L1_wdn"][e].rearrange("(c p) n -> p c n", p=128)
                        pr.dma("g", wup[:].rearrange("p (c n) -> p c n", c=NC), up_src)
                        pr.dma("g", wdn[:].rearrange("p (c n) -> p c n", c=NHC), dn_src)
                        for tb in range(4):
                            t0 = h0 + tb * 512
                            xnb = xnbs[tb % 2]
                            load_fm_block(pr, "y", xnb, d["xn_fm"], t0, 512)
                            if not shared:
                                pr.dma("y", w2r[:], d["w2t"][e:e + 1, t0:t0 + 512])
                                pr.do("t", "matmul", pwb[:], C.ones_row_f32[:],
                                      w2r[:], ins=[C.ones_row_f32, w2r], outs=[pwb],
                                      start=True, stop=True)
                                pr.do("v", "tensor_copy", wb[:], pwb[:],
                                      ins=[pwb], outs=[wb])
                            for ns in range(NHC):
                                ph = phs[ns % 2]
                                rr = rrs[ns % 2]
                                for c in range(NC):
                                    pr.do("t", "matmul", ph[:],
                                          wup[:, c * HID + ns * 128:
                                              c * HID + ns * 128 + 128],
                                          xnb[:, c * 512:(c + 1) * 512],
                                          ins=[wup, xnb], outs=[ph],
                                          start=(c == 0), stop=(c == NC - 1))
                                pr.do("s", "activation", rr[:], ph[:], AF.Relu,
                                      ins=[ph], outs=[rr])
                                h2s = h2[:, ns * 512:(ns + 1) * 512]
                                if shared:
                                    pr.do("v", "tensor_mul", h2s, rr[:], rr[:],
                                          ins=[rr], outs=[h2])
                                else:
                                    pr.do("v", "tensor_mul", tt_[:], rr[:], wb[:],
                                          ins=[rr, wb], outs=[tt_])
                                    pr.do("v", "tensor_mul", h2s, tt_[:], rr[:],
                                          ins=[tt_, rr], outs=[h2])
                            for do in range(NC):
                                py = pys[do % 2]
                                for ns in range(NHC):
                                    pr.do("t", "matmul", py[:],
                                          wdn[:, ns * 1024 + do * 128:
                                              ns * 1024 + do * 128 + 128],
                                          h2[:, ns * 512:(ns + 1) * 512],
                                          ins=[wdn, h2], outs=[py],
                                          start=(ns == 0), stop=(ns == NHC - 1))
                                msl = macc[:, do * 2048 + tb * 512:
                                           do * 2048 + (tb + 1) * 512]
                                if e == 0:
                                    pr.do("v", "tensor_copy", msl, py[:],
                                          ins=[py], outs=[macc])
                                else:
                                    pr.do("v", "tensor_tensor", msl, msl, py[:],
                                          ALU.add, ins=[py, macc], outs=[macc])
                pr.fence()
                with pr.scope() as _so:
                    (gn0, gn1) = sb_alloc(nc, _so, ("mo_gn0", [128, D], F32),
                                          ("mo_gn1", [128, D], F32))
                    pn0, pn1 = ps_alloc(nc, _so, ("mo_pn0", [128, 1024], F32),
                                        ("mo_pn1", [128, 1024], F32))
                    for do in range(NC):
                        sl = macc[:, do * 2048:(do + 1) * 2048]
                        pr.do("s", "activation", sl, sl, AF.Silu,
                              ins=[macc], outs=[macc])
                    for tc in range(16):
                        t0 = h0 + tc * 128
                        pn, gn = (pn0, gn0) if tc % 2 == 0 else (pn1, gn1)
                        for do in range(NC):
                            pr.do("t", "transpose", pn[:, do * 128:(do + 1) * 128],
                                  macc[:, do * 2048 + tc * 128:
                                       do * 2048 + tc * 128 + 128],
                                  C.eye_f32[:], ins=[macc, C.eye_f32], outs=[pn])
                        pr.do("v", "tensor_copy", gn[:], pn[:], ins=[pn], outs=[gn])
                        pr.dma("g", d["g_nat"][t0:t0 + 128, :], gn[:])
                pr.fence()


def emit_out_l1(pr, C, d):
    nc = pr.nc
    pr.fence()
    with pr.scope() as _st:
        (wo,) = sb_alloc(nc, _st, ("ol_wo", [128, NC * D], BF16))
        pr.dma("g", wo[:].rearrange("p (c n) -> p c n", c=NC),
               d["L1_wout"].rearrange("(c p) n -> p c n", p=128))
        ob0, ob1, s2fm0, s2fm1, s2n0, s2n1, gt0, gt1, xt0, xt1, out0, out1 = sb_alloc(
            nc, _st,
            ("ol_ob", [128, NC * 512], BF16), ("ol_ob1", [128, NC * 512], BF16),
            ("ol_s2fm", [128, NC * 512], F32), ("ol_s2fm1", [128, NC * 512], F32),
            ("ol_s2n", [128, D], F32), ("ol_s2n1", [128, D], F32),
            ("ol_gt", [128, D], F32), ("ol_gt1", [128, D], F32),
            ("ol_xt", [128, D], F32), ("ol_xt1", [128, D], F32),
            ("ol_out", [128, D], F32), ("ol_out1", [128, D], F32))
        pp0, pp1, pn_0, pn_1 = ps_alloc(nc, _st, ("ol_pp0", [128, 512], F32),
                                ("ol_pp1", [128, 512], F32),
                                ("ol_pn", [128, 1024], F32),
                                ("ol_pn1", [128, 1024], F32))
        pps = [pp0, pp1]
        for tb in range(T // 512):
            t0 = tb * 512
            ob, s2fm = (ob0, s2fm0) if tb % 2 == 0 else (ob1, s2fm1)
            load_fm_block(pr, "y", ob, d["o_fm"], t0, 512)
            for do in range(NC):
                pp = pps[do % 2]
                for c in range(NC):
                    pr.do("t", "matmul", pp[:],
                          wo[:, c * D + do * 128: c * D + do * 128 + 128],
                          ob[:, c * 512:(c + 1) * 512], ins=[wo, ob], outs=[pp],
                          start=(c == 0), stop=(c == NC - 1))
                pr.do("v", "tensor_copy", s2fm[:, do * 512:(do + 1) * 512], pp[:],
                      ins=[pp], outs=[s2fm])
            for sub in range(4):
                pn = pn_0 if sub % 2 == 0 else pn_1
                s2n, gt, xt, outt = (s2n0, gt0, xt0, out0) if sub % 2 == 0 \
                    else (s2n1, gt1, xt1, out1)
                for do in range(NC):
                    pr.do("t", "transpose", pn[:, do * 128:(do + 1) * 128],
                          s2fm[:, do * 512 + sub * 128: do * 512 + sub * 128 + 128],
                          C.eye_f32[:], ins=[s2fm, C.eye_f32], outs=[pn])
                pr.do("v", "tensor_copy", s2n[:], pn[:], ins=[pn], outs=[s2n])
                tt = t0 + sub * 128
                pr.dma("y", gt[:], d["g_nat"][tt:tt + 128, :])
                pr.dma("y", xt[:], d["xres_nat"][tt:tt + 128, :])
                pr.do("v", "tensor_mul", s2n[:], s2n[:], gt[:],
                      ins=[s2n, gt], outs=[s2n])
                pr.do("v", "tensor_add", outt[:], xt[:], s2n[:],
                      ins=[xt, s2n], outs=[outt])
                pr.dma("y", d["out"][tt:tt + 128, :], outt[:])

def build_program(host, phases=None):
    phases = phases if phases is not None else\
        ["p0", "p1", "p2", "p3", "p4", "p5", "p6", "p7", "p8"]
    nc = bass.Bass("TRN2", target_bir_lowering=False)
    host_shapes = {k: (v.shape, BF16 if v.dtype == ml_dtypes.bfloat16 else F32)
                   for k, v in host.items()}
    d = declare_io(nc, host_shapes)
    pr = Prog(nc)
    C = Consts(pr, d)
    if "p0" in phases:
        emit_ln_to_fm(pr, C, d["x0"], d["xn_fm"], tag="0")
    if "p1" in phases:
        emit_gffn_l0(pr, C, d)
    if "p2" in phases:
        emit_attention(pr, C, d, 0)
    if "p3" in phases:
        emit_proj_ln1(pr, C, d)
    if "p4" in phases:
        emit_ffn_combine_l0(pr, C, d)
    if "p5" in phases:
        emit_ln_to_fm(pr, C, d["xres_nat"], d["xn_fm"], tag="2")
    if "p6" in phases:
        emit_moe(pr, C, d)
    if "p7" in phases:
        emit_attention(pr, C, d, 1)
    if "p8" in phases:
        emit_out_l1(pr, C, d)
    for ek in ("y", "g", "t", "v", "s"):
        pr.barrier_on(ek)
    return nc


def kernel(**inputs):
    host = prep_host(inputs)
    nc = build_program(host, PHASES)
    toks = np.asarray(inputs["L0_attn_tokens"])
    in_maps = []
    for i in range(8):
        m = {"x0": _f32(toks[i])}
        m.update(host)
        in_maps.append(m)
    res = run_bass_kernel_spmd(nc, in_maps, core_ids=list(range(8)))
    return np.stack([np.asarray(res.results[i]["out"]) for i in range(8)])


def emit_attention(pr, C, d, layer):
    nc = pr.nc
    pr.fence()
    NQG = T // 512
    with (
        pr.guard(nc.sbuf_tensor("at_xn", [128, NC * T], BF16, allow_name_mangling=True)) as xn_sb,
        pr.guard(nc.sbuf_tensor("at_frow", [1, 2 * T], BF16, allow_name_mangling=True)) as frow,
        pr.guard(nc.sbuf_tensor("at_eu", [128, 64], F32, allow_name_mangling=True)) as eu,
        pr.guard(nc.sbuf_tensor("at_fb", [128, 16 * 512], BF16, allow_name_mangling=True)) as fb,
        pr.guard(nc.sbuf_tensor("at_dsv", [1, 4], F32, allow_name_mangling=True)) as dsv,
        pr.guard(nc.sbuf_tensor("at_dsb", [128, 2], F32, allow_name_mangling=True)) as dsb,
        pr.guard(nc.sbuf_tensor("at_bq8", [128, NC], F32, allow_name_mangling=True)) as bq8,
        pr.guard(nc.sbuf_tensor("at_bk", [128, NC], F32, allow_name_mangling=True)) as bk_sb,
        pr.guard(nc.sbuf_tensor("at_bv", [128, NC], F32, allow_name_mangling=True)) as bv_sb,
    ):
        load_fm_block(pr, "y", xn_sb, d["xn_fm"], 0, T)
        if layer == 0:
            pr.dma("y", dsv[:, 0:1], d["L0_ds"])
            pr.dma("y", bq8[:], d["L0_bq"])
            pr.dma("y", bk_sb[:], d["L0_bk"])
            pr.dma("y", bv_sb[:], d["L0_bv"])
            pr.do("v", "tensor_scalar_mul", bq8[:], bq8[:], 0.125,
                  ins=[bq8], outs=[bq8])
            pr.do("v", "tensor_scalar_mul", dsv[:, 1:2], dsv[:, 0:1], 0.125,
                  ins=[dsv], outs=[dsv])
            pr.do("v", "tensor_scalar_mul", dsv[:, 2:3], dsv[:, 1:2], -1.0,
                  ins=[dsv], outs=[dsv])
            with pr.guard(nc.psum_tensor("at_pds", [128, 512], F32, allow_name_mangling=True)) as pds:
                pr.do("t", "matmul", pds[:, 0:2], C.ones_row_f32[:], dsv[:, 1:3],
                      ins=[C.ones_row_f32, dsv], outs=[pds], start=True, stop=True)
                pr.do("v", "tensor_copy", dsb[:], pds[:, 0:2], ins=[pds], outs=[dsb])
            pr.do("v", "tensor_scalar_mul", eu[:, 0:32], C.upos[:], dsb[:, 0:1],
                  ins=[C.upos, dsb], outs=[eu])
            pr.do("v", "tensor_scalar_mul", eu[:, 32:64], C.upos[:], dsb[:, 1:2],
                  ins=[C.upos, dsb], outs=[eu])
            pr.do("s", "activation", eu[:], eu[:], AF.Exp, ins=[eu], outs=[eu])
            pr.do("v", "tensor_single_scalar", frow[0:1, 0:T], C.pos_row[:], dsv[:, 2:3],
                  ALU.mult, ins=[C.pos_row, dsv], outs=[frow])
            pr.do("v", "tensor_single_scalar", frow[0:1, T:2 * T], C.pos_row[:], dsv[:, 1:2],
                  ALU.mult, ins=[C.pos_row, dsv], outs=[frow])
            pr.do("s", "activation", frow[:], frow[:], AF.Exp, ins=[frow], outs=[frow])
            with pr.guard(nc.psum_tensor("at_pfb", [128, 512], F32, allow_name_mangling=True)) as pfb:
                for qg in range(NQG):
                    for cls in range(2):
                        pr.do("t", "matmul", pfb[:], C.ones_row_bf[:],
                              frow[0:1, cls * T + qg * 512: cls * T + (qg + 1) * 512],
                              ins=[C.ones_row_bf, frow], outs=[pfb],
                              start=True, stop=True)
                        pr.do("v", "tensor_copy",
                              fb[:, (cls * NQG + qg) * 512:(cls * NQG + qg + 1) * 512],
                              pfb[:], ins=[pfb], outs=[fb])
        pr.fence()

        shared = dict(xn_sb=xn_sb, frow=frow, eu=eu, fb=fb, dsv=dsv,
                      dsb=dsb, bq8=bq8, bk_sb=bk_sb, bv_sb=bv_sb)
        for p in range(NC):
            _attn_pair(pr, C, d, layer, p, shared)
            pr.fence()


def _attn_pair(pr, C, d, layer, p, sh):
    nc = pr.nc
    NQG = T // 512
    VW = 130
    xn_sb, frow, eu, fb = sh["xn_sb"], sh["frow"], sh["eu"], sh["fb"]
    dsv, dsb, bq8, bk_sb, bv_sb = sh["dsv"], sh["dsb"], sh["bq8"], sh["bk_sb"], sh["bv_sb"]
    with pr.scope() as _st:
        wsl, qt, kt, v_sb, vl_sb, vu_sb = sb_alloc(
            nc, _st,
            ("at_wsl", [128, 3 * NC * 128], BF16), ("at_qt", [128, T], BF16),
            ("at_kt", [128, T], BF16),
            ("at_v", [128, 32 * VW], BF16), ("at_vl", [128, 32 * VW], BF16),
            ("at_vu", [128, 32 * VW], BF16))
        _stv = _st.enter_context(pr.scope())
        (vt,) = sb_alloc(nc, _stv, ("at_vt", [128, T], BF16))
        if layer == 0:
            for wi, wname in enumerate(("L0_Wq", "L0_Wk", "L0_Wv")):
                pr.dma("g", wsl[:, wi * NC * 128:(wi + 1) * NC * 128]
                       .rearrange("p (c n) -> p c n", c=NC),
                       d[wname].rearrange("(c p) n -> p c n", p=128)
                       [:, :, p * 128:(p + 1) * 128])
        else:
            qkv = d["L1_wqkv"].rearrange("(c p) n -> p c n", p=128)
            for wi in range(3):
                pr.dma("g", wsl[:, wi * NC * 128:(wi + 1) * NC * 128]
                       .rearrange("p (c n) -> p c n", c=NC),
                       qkv[:, :, wi * D + p * 128: wi * D + p * 128 + 128])
        with pr.guard(nc.psum_tensor("at_pj", [128, 512], F32, allow_name_mangling=True)) as pj:
            for wi, dst in ((0, qt), (1, kt), (2, vt)):
                for tb in range(T // 512):
                    t0 = tb * 512
                    for c in range(NC):
                        pr.do("t", "matmul", pj[:],
                              wsl[:, wi * NC * 128 + c * 128:
                                  wi * NC * 128 + (c + 1) * 128],
                              xn_sb[:, c * T + t0: c * T + t0 + 512],
                              ins=[wsl, xn_sb], outs=[pj],
                              start=(c == 0), stop=(c == NC - 1))
                    if layer == 0:
                        bias = (bq8, bk_sb, bv_sb)[wi][:, p:p + 1]
                        sc = 0.125 if wi == 0 else 1.0
                        pr.do("v", "tensor_scalar", dst[:, t0:t0 + 512], pj[:],
                              sc, bias, ALU.mult, ALU.add,
                              ins=[pj, bq8, bk_sb, bv_sb], outs=[dst])
                    elif wi == 0:
                        pr.do("s", "activation", dst[:, t0:t0 + 512], pj[:],
                              AF.Copy, scale=0.125, ins=[pj], outs=[dst])
                    else:
                        pr.do("v", "tensor_copy", dst[:, t0:t0 + 512], pj[:],
                              ins=[pj], outs=[dst])
        pr.fence()
        pr.do("v", "memset", v_sb[:].rearrange("p (t g w) -> p t g w", g=2, w=65)
              [:, :, :, 64:65], 1.0, ins=[], outs=[v_sb])
        with pr.guard(nc.psum_tensor("at_pv", [128, 1024], BF16, allow_name_mangling=True)) as pv:
            for tc in range(T // 128):
                pr.do("t", "transpose", pv[:, 0:128], vt[:, tc * 128:(tc + 1) * 128],
                      C.eye_bf[:], ins=[vt, C.eye_bf], outs=[pv])
                pr.do("v", "tensor_copy",
                      v_sb[:, tc * VW: (tc + 1) * VW]
                      .rearrange("p (g w) -> p g w", g=2)[:, :, 0:64],
                      pv[:, 0:128].rearrange("p (g w) -> p g w", g=2),
                      ins=[pv], outs=[v_sb])
            if layer == 0:
                for tc in range(T // 128):
                    pr.do("v", "tensor_scalar_mul",
                          vl_sb[:, tc * VW:(tc + 1) * VW],
                          v_sb[:, tc * VW:(tc + 1) * VW],
                          eu[:, tc:tc + 1], ins=[v_sb, eu], outs=[vl_sb])
                    pr.do("v", "tensor_scalar_mul",
                          vu_sb[:, tc * VW:(tc + 1) * VW],
                          v_sb[:, tc * VW:(tc + 1) * VW],
                          eu[:, 32 + tc:33 + tc], ins=[v_sb, eu], outs=[vu_sb])
        pr.fence()
        _stv.close()
        with pr.scope() as _st2:
            s00, s01, s10, s11 = ps_alloc(
                nc, _st2,
                ("at_s00", [128, 512], F32), ("at_s01", [128, 512], F32),
                ("at_s10", [128, 512], F32), ("at_s11", [128, 512], F32))
            accs = {}
            for cls in (("L", "U") if layer == 0 else ("L",)):
                for hh in range(2):
                    accs[(cls, hh)] = _st2.enter_context(pr.guard(nc.psum_tensor(
                        f"at_acc{cls}{hh}", [65, 512], F32,
                        allow_name_mangling=True)))
            p00, p01, p10, p11, ep1, ep2, dh, rb_sb, rb2_sb, ofin = sb_alloc(
                nc, _st2,
                ("at_p00", [128, 512], BF16), ("at_p01", [128, 512], BF16),
                ("at_p10", [128, 512], BF16), ("at_p11", [128, 512], BF16),
                ("at_ep1", [128, 512], F32), ("at_ep2", [128, 512], F32),
                ("at_dh", [1, 4 * 512], F32), ("at_rb", [128, 512], F32),
                ("at_rb2", [128, 512], F32),
                ("at_of", [128, 512], BF16))
            S = [[s00, s01], [s10, s11]]
            P = [[p00, p01], [p10, p11]]
            for qg in range(NQG):
                q0 = qg * 512
                n_uc = 32 if layer == 0 else 4 * qg + 4

                def classify(uc):
                    if layer == 0:
                        cls = "L" if uc <= 4 * qg + 3 else "U"
                        crossing = 4 * qg <= uc <= 4 * qg + 3
                        k = 0
                    else:
                        cls = "L"
                        crossing = uc >= 4 * qg
                        k = (uc - 4 * qg) if crossing else 0
                    if layer == 0 and cls == "L":
                        first, last = uc == 0, uc == min(4 * qg + 3, n_uc - 1)
                    elif layer == 0:
                        first, last = uc == 4 * qg + 4, uc == 31
                    else:
                        first, last = uc == 0, uc == n_uc - 1
                    return cls, crossing, k, first, last

                def emit_s(uc):
                    cls, crossing, k, first, last = classify(uc)
                    c0 = 128 * k
                    par = uc % 2
                    for hh in range(2):
                        st = S[hh][par]
                        r0, r1 = hh * 64, hh * 64 + 64
                        pr.do("t", "matmul", st[:, c0:512],
                              kt[r0:r1, uc * 128:(uc + 1) * 128],
                              qt[r0:r1, q0 + c0: q0 + 512],
                              ins=[kt, qt], outs=[st], start=True, stop=True,
                              tile_position=(hh * 64, 0), skip_group_check=True)
                        if layer == 0 and crossing:
                            di = uc - 4 * qg
                            pr.do("v", "scalar_tensor_tensor", st[:],
                                  C.cdelta[:, di * 512:(di + 1) * 512],
                                  dsb[:, 1:2], st[:], ALU.mult, ALU.add,
                                  ins=[C.cdelta, dsb, st], outs=[st])
                        if layer == 0 and uc == 0:
                            hg = 2 * p + hh
                            pr.do("v", "tensor_scalar_add", st[0:16, :],
                                  st[0:16, :], C.headmask[0:16, hg:hg + 1],
                                  ins=[st, C.headmask], outs=[st])
                        if layer == 1 and crossing:
                            pr.do("v", "tensor_tensor", st[:, c0:c0 + 128],
                                  st[:, c0:c0 + 128], C.tri[:], ALU.add,
                                  ins=[st, C.tri], outs=[st])

                def emit_exp(uc):
                    cls, crossing, k, first, last = classify(uc)
                    c0 = 128 * k
                    par = uc % 2
                    for hh in range(2):
                        pr.do("s", "activation", P[hh][par][:, c0:512],
                              S[hh][par][:, c0:512],
                              AF.Exp, ins=[S[hh][par]], outs=[P[hh][par]])

                def emit_dv(uc):
                    cls, crossing, k, first, last = classify(uc)
                    c0 = 128 * k
                    par = uc % 2
                    vx = (vl_sb if cls == "L" else vu_sb) if layer == 0 else v_sb
                    for hh in range(2):
                        pt = P[hh][par]
                        acc = accs[(cls, hh)]
                        pr.do("t", "matmul", acc[0:65, c0:512],
                              vx[:, uc * VW + hh * 65: uc * VW + hh * 65 + 65],
                              pt[:, c0:512],
                              ins=[vl_sb, vu_sb, v_sb, pt], outs=[acc],
                              start=first, stop=last,
                              tile_position=(0, 0), skip_group_check=True)

                for uc in range(n_uc):
                    emit_s(uc)
                    if uc > 1:
                        emit_dv(uc - 2)
                    emit_exp(uc)
                emit_dv(n_uc - 2)
                emit_dv(n_uc - 1)
                pep = s00
                for hh in range(2):
                    r0, r1 = hh * 64, hh * 64 + 64
                    o = hh * 512
                    al = accs[("L", hh)]
                    if layer == 0:
                        fbl = fb[:, qg * 512:(qg + 1) * 512]
                        fbu = fb[:, (NQG + qg) * 512:(NQG + qg + 1) * 512]
                        au = accs[("U", hh)]
                        pr.do("v", "tensor_tensor", ep1[r0:r1, :], al[0:64, :],
                              fbl[r0:r1, :], ALU.mult, ins=[al, fb], outs=[ep1])
                        pr.do("v", "tensor_tensor", dh[0:1, o:o + 512], al[64:65, :],
                              frow[0:1, q0:q0 + 512], ALU.mult,
                              ins=[al, frow], outs=[dh])
                        if qg < NQG - 1:
                            pr.do("v", "tensor_tensor", ep2[r0:r1, :], au[0:64, :],
                                  fbu[r0:r1, :], ALU.mult, ins=[au, fb], outs=[ep2])
                            pr.do("v", "tensor_add", ep1[r0:r1, :], ep1[r0:r1, :],
                                  ep2[r0:r1, :], ins=[ep1, ep2], outs=[ep1])
                            pr.do("v", "tensor_tensor",
                                  dh[0:1, 1024 + o:1024 + o + 512], au[64:65, :],
                                  frow[0:1, T + q0:T + q0 + 512], ALU.mult,
                                  ins=[au, frow], outs=[dh])
                            pr.do("v", "tensor_add", dh[0:1, o:o + 512],
                                  dh[0:1, o:o + 512],
                                  dh[0:1, 1024 + o:1024 + o + 512],
                                  ins=[dh], outs=[dh])
                    else:
                        pr.do("v", "tensor_copy", ep1[r0:r1, :], al[0:64, :],
                              ins=[al], outs=[ep1])
                        pr.do("v", "tensor_copy", dh[0:1, o:o + 512], al[64:65, :],
                              ins=[al], outs=[dh])
                pr.do("t", "matmul", pep[:], C.maskAf[:], dh[0:1, 0:512],
                      ins=[C.maskAf, dh], outs=[pep], start=True, stop=False,
                      skip_group_check=True)
                pr.do("t", "matmul", pep[:], C.maskBf[:], dh[0:1, 512:1024],
                      ins=[C.maskBf, dh], outs=[pep], start=False, stop=True,
                      skip_group_check=True)
                pr.do("v", "tensor_copy", rb_sb[:], pep[:], ins=[pep], outs=[rb_sb])
                pr.do("v", "reciprocal", rb2_sb[:], rb_sb[:], ins=[rb_sb],
                      outs=[rb2_sb])
                pr.do("v", "tensor_mul", ofin[:], ep1[:], rb2_sb[:],
                      ins=[ep1, rb2_sb], outs=[ofin])
                pr.dma("y", fm3(d["o_fm"], q0, 512, p, 1)[:, 0, :], ofin[:])


def emit_proj_ln1(pr, C, d):
    nc = pr.nc
    pr.fence()
    with (
        pr.guard(nc.sbuf_tensor("pl_wo", [128, NC * D], BF16, allow_name_mangling=True)) as wo,
        pr.guard(nc.sbuf_tensor("pl_bo", [128, NC], F32, allow_name_mangling=True)) as bo,
        pr.guard(nc.sbuf_tensor("pl_g", [128, NC], F32, allow_name_mangling=True)) as lg,
        pr.guard(nc.sbuf_tensor("pl_b", [128, NC], F32, allow_name_mangling=True)) as lb,
    ):
        pr.dma("g", wo[:].rearrange("p (c n) -> p c n", c=NC),
               d["L0_Wo"].rearrange("(c p) n -> p c n", p=128))
        pr.dma("y", bo[:], d["L0_bo"])
        pr.dma("y", lg[:], d["L0_ln1g"])
        pr.dma("y", lb[:], d["L0_ln1b"])
        with pr.scope() as _stx:
            ob0, ob1, xnb0, xnb1, sum_sb, nat0, nat1, stt0, stt1, sq0, sq1, xnn0, xnn1, x1t0, x1t1 = sb_alloc(nc, _stx, ("pl_ob", [128, NC * 512], BF16), ("pl_ob1", [128, NC * 512], BF16), ("pl_xnb", [128, NC * 512], BF16), ("pl_xnb1", [128, NC * 512], BF16), ("pl_sum", [128, NC * 512], F32), ("pl_nat", [128, D], F32), ("pl_nat1", [128, D], F32), ("pl_st", [128, 8], F32), ("pl_st1", [128, 8], F32), ("pl_sq", [128, D], F32), ("pl_sq1", [128, D], F32), ("pl_xnn", [128, D], BF16), ("pl_xnn1", [128, D], BF16), ("pl_x1", [128, 1024], BF16), ("pl_x11", [128, 1024], BF16))
            pp0, pp1, pn0, pn1, pf = ps_alloc(nc, _stx, ("pl_pp0", [128, 512], F32), ("pl_pp1", [128, 512], F32), ("pl_pn", [128, 1024], F32), ("pl_pn1", [128, 1024], F32), ("pl_pf", [128, 1024], BF16))
            pps = [pp0, pp1]
            for tb in range(T // 512):
                t0 = tb * 512
                ob, xnb = (ob0, xnb0) if tb % 2 == 0 else (ob1, xnb1)
                load_fm_block(pr, "y", ob, d["o_fm"], t0, 512)
                load_fm_block(pr, "g", xnb, d["xn_fm"], t0, 512)
                for do in range(NC):
                    pp = pps[do % 2]
                    for c in range(NC):
                        pr.do("t", "matmul", pp[:],
                              wo[:, c * D + do * 128: c * D + do * 128 + 128],
                              ob[:, c * 512:(c + 1) * 512], ins=[wo, ob], outs=[pp],
                              start=(c == 0), stop=(c == NC - 1))
                    pr.do("v", "scalar_tensor_tensor",
                          sum_sb[:, do * 512:(do + 1) * 512], pp[:], bo[:, do:do + 1],
                          xnb[:, do * 512:(do + 1) * 512], ALU.add, ALU.add,
                          ins=[pp, bo, xnb], outs=[sum_sb])
                for sub in range(4):
                    nat, stt, sq, xnn, x1t = (nat0, stt0, sq0, xnn0, x1t0) \
                        if sub % 2 == 0 else (nat1, stt1, sq1, xnn1, x1t1)
                    pn = pn0 if sub % 2 == 0 else pn1
                    for do in range(NC):
                        pr.do("t", "transpose", pn[:, do * 128:(do + 1) * 128],
                              sum_sb[:, do * 512 + sub * 128: do * 512 + sub * 128 + 128],
                              C.eye_f32[:], ins=[sum_sb, C.eye_f32], outs=[pn])
                    pr.do("v", "tensor_copy", nat[:], pn[:], ins=[pn], outs=[nat])
                    pr.do("v", "reduce_sum", stt[:, 0:1], nat[:], ins=[nat], outs=[stt],
                          axis=AX.X)
                    pr.do("s", "activation", sq[:], nat[:], AF.Square,
                          ins=[nat], outs=[sq, stt], accum_out=stt[:, 1:2])
                    pr.do("v", "tensor_scalar", stt[:, 2:4], stt[:, 0:2], 1.0 / D, None,
                          ALU.mult, ins=[stt, sq], outs=[stt])
                    pr.do("v", "tensor_mul", stt[:, 4:5], stt[:, 2:3], stt[:, 2:3],
                          ins=[stt], outs=[stt])
                    pr.do("v", "tensor_sub", stt[:, 4:5], stt[:, 3:4], stt[:, 4:5],
                          ins=[stt], outs=[stt])
                    pr.do("s", "activation", stt[:, 5:6], stt[:, 4:5], AF.Sqrt,
                          bias=C.epscol[:], ins=[stt, C.epscol], outs=[stt])
                    pr.do("v", "reciprocal", stt[:, 6:7], stt[:, 5:6], ins=[stt],
                          outs=[stt])
                    pr.do("v", "tensor_scalar", xnn[:], nat[:], stt[:, 2:3],
                          stt[:, 6:7], ALU.subtract, ALU.mult, ins=[nat, stt],
                          outs=[xnn])
                    for cc in range(NC):
                        pr.do("t", "transpose", pf[:, cc * 128:(cc + 1) * 128],
                              xnn[:, cc * 128:(cc + 1) * 128], C.eye_bf[:],
                              ins=[xnn, C.eye_bf], outs=[pf])
                    for cc in range(NC):
                        pr.do("v", "tensor_scalar", x1t[:, cc * 128:(cc + 1) * 128],
                              pf[:, cc * 128:(cc + 1) * 128], lg[:, cc:cc + 1],
                              lb[:, cc:cc + 1], ALU.mult, ALU.add,
                              ins=[pf, lg, lb], outs=[x1t])
                    pr.dma("y", fm3(d["x1_fm"], t0 + sub * 128, 128),
                           x1t[:].rearrange("p (c t) -> p c t", c=NC))


def emit_ffn_combine_l0(pr, C, d):
    nc = pr.nc
    pr.fence()
    NDC = 4
    with (
        pr.guard(nc.sbuf_tensor("ff_b1", [128, DFF // 128], F32, allow_name_mangling=True)) as b1,
        pr.guard(nc.sbuf_tensor("ff_b2", [128, NC], F32, allow_name_mangling=True)) as b2,
        pr.guard(nc.sbuf_tensor("ff_yacc", [128, NC * 2048], F32, allow_name_mangling=True)) as yacc,
    ):
        pr.dma("y", b1[:], d["L0_b1"])
        pr.dma("y", b2[:], d["L0_b2"])
        for th in range(2):
            h0 = th * 2048
            with (
                pr.guard(nc.sbuf_tensor("ff_w1", [128, NC * 1024], BF16, allow_name_mangling=True)) as w1c,
                pr.guard(nc.sbuf_tensor("ff_w2", [128, NC * 1024], BF16, allow_name_mangling=True)) as w2c,
                pr.guard(nc.sbuf_tensor("ff_x1", [128, NC * 512], BF16, allow_name_mangling=True)) as x1b0,
                pr.guard(nc.sbuf_tensor("ff_x1b", [128, NC * 512], BF16, allow_name_mangling=True)) as x1b1,
                pr.guard(nc.sbuf_tensor("ff_h", [128, NC * 512], BF16, allow_name_mangling=True)) as hb,
                pr.guard(nc.psum_tensor("ff_ph", [128, 512], F32, allow_name_mangling=True)) as ph_0,
                pr.guard(nc.psum_tensor("ff_ph1", [128, 512], F32, allow_name_mangling=True)) as ph_1,
                pr.guard(nc.psum_tensor("ff_py0", [128, 512], F32, allow_name_mangling=True)) as py0,
                pr.guard(nc.psum_tensor("ff_py1", [128, 512], F32, allow_name_mangling=True)) as py1,
            ):
                pys = [py0, py1]
                for dc in range(NDC):
                    pr.dma("g", w1c[:].rearrange("p (c n) -> p c n", c=NC),
                           d["L0_W1"].rearrange("(c p) n -> p c n", p=128)
                           [:, :, dc * 1024:(dc + 1) * 1024])
                    pr.dma("g", w2c[:].rearrange("p (c n) -> p c n", c=NC),
                           d["L0_W2"].rearrange("(c p) n -> p c n", p=128)
                           [:, dc * NC:(dc + 1) * NC, :])
                    for tb in range(4):
                        t0 = h0 + tb * 512
                        x1b = x1b0 if tb % 2 == 0 else x1b1
                        load_fm_block(pr, "y", x1b, d["x1_fm"], t0, 512)
                        for sub in range(NC):
                            ph = ph_0 if sub % 2 == 0 else ph_1
                            for c in range(NC):
                                pr.do("t", "matmul", ph[:],
                                      w1c[:, c * 1024 + sub * 128:
                                          c * 1024 + sub * 128 + 128],
                                      x1b[:, c * 512:(c + 1) * 512],
                                      ins=[w1c, x1b], outs=[ph],
                                      start=(c == 0), stop=(c == NC - 1))
                            pr.do("s", "activation", hb[:, sub * 512:(sub + 1) * 512],
                                  ph[:], AF.Relu, bias=b1[:, dc * NC + sub:
                                                          dc * NC + sub + 1],
                                  ins=[ph, b1], outs=[hb])
                        for do in range(NC):
                            py = pys[do % 2]
                            for sub in range(NC):
                                pr.do("t", "matmul", py[:],
                                      w2c[:, sub * 1024 + do * 128:
                                          sub * 1024 + do * 128 + 128],
                                      hb[:, sub * 512:(sub + 1) * 512],
                                      ins=[w2c, hb], outs=[py],
                                      start=(sub == 0), stop=(sub == NC - 1))
                            ysl = yacc[:, do * 2048 + tb * 512:
                                       do * 2048 + (tb + 1) * 512]
                            if dc == 0:
                                pr.do("v", "tensor_copy", ysl, py[:],
                                      ins=[py], outs=[yacc])
                            else:
                                pr.do("v", "tensor_tensor", ysl, ysl, py[:], ALU.add,
                                      ins=[py, yacc], outs=[yacc])
            pr.fence()
            with pr.scope() as _stx:
                x1b0, x1b1, sum_sb0, sum_sb1, nat0, nat1, stt0, stt1, sq0, sq1, s_nat0, s_nat1, gb, bb, gt0, gt1, xt0, xt1, outt0, outt1 = sb_alloc(nc, _stx, ("cb_x1", [128, NC * 512], BF16), ("cb_x12", [128, NC * 512], BF16), ("cb_sum", [128, NC * 128], F32), ("cb_sum1", [128, NC * 128], F32), ("cb_nat", [128, D], F32), ("cb_nat1", [128, D], F32), ("cb_st", [128, 8], F32), ("cb_st1", [128, 8], F32), ("cb_sq", [128, D], F32), ("cb_sq1", [128, D], F32), ("cb_s", [128, D], F32), ("cb_s1", [128, D], F32), ("cb_gb", [128, D], F32), ("cb_bb", [128, D], F32), ("cb_gt", [128, D], F32), ("cb_gt1", [128, D], F32), ("cb_xt", [128, D], F32), ("cb_xt1", [128, D], F32), ("cb_out", [128, D], F32), ("cb_out1", [128, D], F32))
                pn_0, pn_1, pb_ = ps_alloc(nc, _stx, ("cb_pn", [128, 1024], F32), ("cb_pn1", [128, 1024], F32), ("cb_pb", [128, 1024], F32))
                with pr.guard(nc.sbuf_tensor("cb_grow", [1, D], F32, allow_name_mangling=True)) as grow, \
                     pr.guard(nc.sbuf_tensor("cb_brow", [1, D], F32, allow_name_mangling=True)) as brow:
                    pr.dma("y", grow[:], d["L0_ln2g_row"])
                    pr.dma("y", brow[:], d["L0_ln2b_row"])
                    for hv in range(2):
                        pr.do("t", "matmul", pb_[:, hv * 512:(hv + 1) * 512],
                              C.ones_row_f32[:], grow[0:1, hv * 512:(hv + 1) * 512],
                              ins=[C.ones_row_f32, grow], outs=[pb_],
                              start=True, stop=True)
                    pr.do("v", "tensor_copy", gb[:], pb_[:], ins=[pb_], outs=[gb])
                    for hv in range(2):
                        pr.do("t", "matmul", pb_[:, hv * 512:(hv + 1) * 512],
                              C.ones_row_f32[:], brow[0:1, hv * 512:(hv + 1) * 512],
                              ins=[C.ones_row_f32, brow], outs=[pb_],
                              start=True, stop=True)
                    pr.do("v", "tensor_copy", bb[:], pb_[:], ins=[pb_], outs=[bb])
                for tc in range(16):
                    t0 = h0 + tc * 128
                    sum_sb, nat, stt, sq, s_nat = \
                        (sum_sb0, nat0, stt0, sq0, s_nat0) if tc % 2 == 0 \
                        else (sum_sb1, nat1, stt1, sq1, s_nat1)
                    pn = pn_0 if tc % 2 == 0 else pn_1
                    gt, xt, outt = (gt0, xt0, outt0) if tc % 2 == 0 else (gt1, xt1, outt1)
                    x1b = x1b0 if tc % 2 == 0 else x1b1
                    load_fm_block(pr, "y", x1b[:, 0:NC * 128], d["x1_fm"], t0, 128)
                    for do in range(NC):
                        pr.do("v", "scalar_tensor_tensor",
                              sum_sb[:, do * 128:(do + 1) * 128],
                              yacc[:, do * 2048 + tc * 128: do * 2048 + tc * 128 + 128],
                              b2[:, do:do + 1], x1b[:, do * 128:(do + 1) * 128],
                              ALU.add, ALU.add, ins=[yacc, b2, x1b], outs=[sum_sb])
                    for do in range(NC):
                        pr.do("t", "transpose", pn[:, do * 128:(do + 1) * 128],
                              sum_sb[:, do * 128:(do + 1) * 128], C.eye_f32[:],
                              ins=[sum_sb, C.eye_f32], outs=[pn])
                    pr.do("v", "tensor_copy", nat[:], pn[:], ins=[pn], outs=[nat])
                    pr.do("v", "reduce_sum", stt[:, 0:1], nat[:], ins=[nat],
                          outs=[stt], axis=AX.X)
                    pr.do("s", "activation", sq[:], nat[:], AF.Square,
                          ins=[nat], outs=[sq, stt], accum_out=stt[:, 1:2])
                    pr.do("v", "tensor_scalar", stt[:, 2:4], stt[:, 0:2], 1.0 / D,
                          None, ALU.mult, ins=[stt, sq], outs=[stt])
                    pr.do("v", "tensor_mul", stt[:, 4:5], stt[:, 2:3], stt[:, 2:3],
                          ins=[stt], outs=[stt])
                    pr.do("v", "tensor_sub", stt[:, 4:5], stt[:, 3:4], stt[:, 4:5],
                          ins=[stt], outs=[stt])
                    pr.do("s", "activation", stt[:, 5:6], stt[:, 4:5], AF.Sqrt,
                          bias=C.epscol[:], ins=[stt, C.epscol], outs=[stt])
                    pr.do("v", "reciprocal", stt[:, 6:7], stt[:, 5:6], ins=[stt],
                          outs=[stt])
                    pr.do("v", "tensor_scalar", s_nat[:], nat[:], stt[:, 2:3],
                          stt[:, 6:7], ALU.subtract, ALU.mult, ins=[nat, stt],
                          outs=[s_nat])
                    pr.do("v", "tensor_mul", s_nat[:], s_nat[:], gb[:],
                          ins=[s_nat, gb], outs=[s_nat])
                    pr.do("v", "tensor_add", s_nat[:], s_nat[:], bb[:],
                          ins=[s_nat, bb], outs=[s_nat])
                    pr.dma("y", gt[:], d["g_nat"][t0:t0 + 128, :])
                    pr.dma("y", xt[:], d["x0"][t0:t0 + 128, :])
                    pr.do("v", "tensor_mul", s_nat[:], s_nat[:], gt[:],
                          ins=[s_nat, gt], outs=[s_nat])
                    pr.do("v", "tensor_add", outt[:], xt[:], s_nat[:],
                          ins=[xt, s_nat], outs=[outt])
                    pr.dma("y", d["xres_nat"][t0:t0 + 128, :], outt[:])
            pr.fence()



# version 18
# speedup vs baseline: 1.1501x; 1.1501x over previous
import contextlib
import numpy as np
import ml_dtypes
import concourse.bass as bass
import concourse.mybir as mybir
from concourse.bass_utils import run_bass_kernel_spmd

F32 = mybir.dt.float32
BF16 = mybir.dt.bfloat16
AF = mybir.ActivationFunctionType
ALU = mybir.AluOpType
AX = mybir.AxisListType

T = 4096
D = 1024
NC = 8
H = 16
DH = 64
DFF = 4096
NEXP = 8
HID = 1408
NHC = 11
LN_EPS = 1e-5
MAGIC = 12582912.0
NEG = -3.0e38

SCRATCH_KIND = "Internal"
PHASES = None


class Prog:

    ENG = ("t", "v", "s", "g", "y")

    def __init__(self, nc):
        self.nc = nc
        self.eng = {"t": nc.tensor, "v": nc.vector, "s": nc.scalar,
                    "g": nc.gpsimd, "y": nc.sync}
        self.semkeys = ["t", "v", "s", "g", "yd", "gd", "sd"]
        self.sem = {k: nc.alloc_semaphore("sem_" + k) for k in self.semkeys}
        self.cnt = {k: 0 for k in self.semkeys}
        self.writer = {}
        self.readers = {}
        self.hwm = {e: {} for e in self.ENG}
        self.fence_vals = None

    def guard(self, cm):
        pr = self

        class _G:
            def __enter__(self):
                return cm.__enter__()

            def __exit__(self, *e):
                r = cm.__exit__(*e)
                pr.fence()
                return r

        return _G()

    def scope(self):
        pr = self

        class _S(contextlib.ExitStack):
            def __exit__(self, *exc):
                r = super().__exit__(*exc)
                pr.fence()
                return r

        return _S()

    def fence(self):
        self.fence_vals = dict(self.cnt)

    @staticmethod
    def key(ap):
        t = getattr(ap, "tensor", ap)
        return getattr(t, "name", str(t))

    def _wait(self, ek, semkey, val):
        if val <= 0:
            return
        if self.hwm[ek].get(semkey, 0) >= val:
            return
        self.eng[ek].wait_ge(self.sem[semkey], val)
        self.hwm[ek][semkey] = val

    def _dval(self, sk, v):
        return self.cnt[sk] if sk.endswith("d") else v

    def _edges(self, ek, ins, outs, skip_tensor_internal):
        if self.fence_vals is not None:
            for sk, v in self.fence_vals.items():
                self._wait(ek, sk, v)
        for ap in ins:
            w = self.writer.get(self.key(ap))
            if w and not (skip_tensor_internal and w[0] == "t"):
                self._wait(ek, w[0], self._dval(w[0], w[1]))
        for ap in outs:
            k = self.key(ap)
            for sk, v in self.readers.get(k, {}).items():
                if skip_tensor_internal and sk == "t":
                    continue
                self._wait(ek, sk, self._dval(sk, v))
            w = self.writer.get(k)
            if w and not (skip_tensor_internal and w[0] == "t"):
                self._wait(ek, w[0], self._dval(w[0], w[1]))

    def _update(self, semkey, val, ins, outs):
        for ap in outs:
            self.writer[self.key(ap)] = (semkey, val)
            self.readers[self.key(ap)] = {}
        for ap in ins:
            self.readers.setdefault(self.key(ap), {})[semkey] = val

    def do(self, ek, opname, *args, ins=(), outs=(), **kw):
        skip = ek == "t"
        self._edges(ek, ins, outs, skip)
        inst = getattr(self.eng[ek], opname)(*args, **kw)
        self.cnt[ek] += 1
        inst.then_inc(self.sem[ek], 1)
        v = self.cnt[ek]
        self._update(ek, v, ins, outs)
        return inst

    def dma(self, qk, out, in_, **kw):
        semkey = qk + "d"
        self._edges(qk, [in_], [out], False)
        inst = self.eng[qk].dma_start(out=out, in_=in_, **kw)
        self.cnt[semkey] += 16
        inst.then_inc(self.sem[semkey], 16)
        self._update(semkey, self.cnt[semkey], [in_], [out])
        return inst

    def barrier_on(self, ek, semkeys=None):
        for sk in (semkeys or self.semkeys):
            self._wait(ek, sk, self.cnt[sk])



def sb_alloc(nc, stack, *specs):
    return tuple(stack.enter_context(nc.sbuf_tensor(n, s, t, allow_name_mangling=True)) for n, s, t in specs)


def ps_alloc(nc, stack, *specs):
    return tuple(stack.enter_context(nc.psum_tensor(n, s, t, allow_name_mangling=True)) for n, s, t in specs)


def _bf(x):
    return np.ascontiguousarray(x, dtype=ml_dtypes.bfloat16)


def _f32(x):
    return np.ascontiguousarray(x, dtype=np.float32)


def _fmvec(v, nchunk=NC):
    return _f32(np.asarray(v, np.float32).reshape(nchunk, 128).T)


def host_constants():
    c = {}
    c["eye_bf"] = _bf(np.eye(128))
    c["eye_f32"] = _f32(np.eye(128))
    c["ones_row_bf"] = _bf(np.ones((1, 128)))
    c["ones_row_f32"] = _f32(np.ones((1, 128)))
    c["ones_col_bf"] = _bf(np.ones((128, 1)))
    c["pos_row"] = _f32(np.arange(T, dtype=np.float32)[None, :])
    c["upos"] = _f32(np.arange(128)[:, None] + 128 * np.arange(32)[None, :])
    ul = np.arange(128)[:, None]
    j = np.arange(512)[None, :]
    c["cdelta"] = _f32(np.stack([np.maximum(0.0, ul - j + d0) * 2.0
                                 for d0 in (0, 128, 256, 384)]))
    jj = np.arange(128)[None, :]
    c["tri"] = _f32(np.where(ul > jj, NEG, 0.0))
    hh = np.arange(16)[None, :]
    c["headmask"] = _f32(np.where(ul < hh, NEG, 0.0))
    mA = np.zeros((1, 128), np.float32); mA[0, :64] = 1.0
    mB = np.zeros((1, 128), np.float32); mB[0, 64:] = 1.0
    c["epscol"] = _f32(np.full((128, 1), 1e-5))
    c["negbig"] = _f32(np.full((128, 1), -1e30))
    c["maskA_bf"] = _bf(mA)
    c["maskB_bf"] = _bf(mB)
    c["maskA_f32"] = _f32(mA)
    c["maskB_f32"] = _f32(mB)
    return c


def prep_host(inputs):
    a = {k: np.asarray(v) for k, v in inputs.items()}
    h = {}
    for w in ("Wq", "Wk", "Wv", "Wo"):
        h["L0_" + w] = _f32(a["L0_attn_" + w])
    for b in ("bq", "bk", "bv", "bo"):
        h["L0_" + b] = _fmvec(a["L0_attn_" + b])
    h["L0_ds"] = _f32(a["L0_attn_dist_scale"].reshape(1, 1))
    h["L0_W1"] = _f32(a["L0_attn_W1"])
    h["L0_b1"] = _fmvec(a["L0_attn_b1"], DFF // 128)
    h["L0_W2"] = _f32(a["L0_attn_W2"])
    h["L0_b2"] = _fmvec(a["L0_attn_b2"])
    h["L0_ln1g"] = _fmvec(a["L0_attn_ln1_g"])
    h["L0_ln1b"] = _fmvec(a["L0_attn_ln1_b"])
    h["L0_ln2g"] = _fmvec(a["L0_attn_ln2_g"])
    h["L0_ln2b"] = _fmvec(a["L0_attn_ln2_b"])
    h["L0_ln2g_row"] = _f32(np.asarray(a["L0_attn_ln2_g"], np.float32).reshape(1, D))
    h["L0_ln2b_row"] = _f32(np.asarray(a["L0_attn_ln2_b"], np.float32).reshape(1, D))
    h["L0_fk"] = _f32(a["L0_ffn_kernel"])
    h["L0_fb"] = _fmvec(a["L0_ffn_bias"])
    h["L0_fa"] = _f32(a["L0_ffn_a"])
    h["L0_fbb"] = _f32(a["L0_ffn_b"])
    h["L1_wqkv"] = _f32(a["L1_attn_w_qkv"].T)
    h["L1_wout"] = _f32(a["L1_attn_w_out"].T)
    h["L1_gate"] = _f32(a["L1_ffn_gate_w"].T)
    h["L1_ebias"] = _f32(a["L1_ffn_expert_bias"].reshape(1, 8))
    h["L1_wup"] = _f32(np.ascontiguousarray(a["L1_ffn_w_up"].transpose(0, 2, 1)))
    h["L1_wdn"] = _f32(np.ascontiguousarray(a["L1_ffn_w_down"].transpose(0, 2, 1)))
    h["L1_swup"] = _f32(a["L1_ffn_shared_w_up"].T)
    h["L1_swdn"] = _f32(a["L1_ffn_shared_w_down"].T)
    h.update(host_constants())
    return h


HOST_SPECS = None


def declare_io(nc, host_shapes):
    d = {}
    d["x0"] = nc.dram_tensor("x0", [T, D], F32, kind="ExternalInput").ap()
    for name, (shape, dt) in host_shapes.items():
        d[name] = nc.dram_tensor(name, list(shape), dt, kind="ExternalInput").ap()
    d["out"] = nc.dram_tensor("out", [T, D], F32, kind="ExternalOutput").ap()
    sk = SCRATCH_KIND
    for name, shape, dt in [
        ("xn_fm", [D, T], BF16),
        ("g_nat", [T, D], F32),
        ("o_fm", [D, T], BF16),
        ("x1_fm", [D, T], BF16),
        ("xres_nat", [T, D], F32),
        ("w2t", [8, T], F32),
    ]:
        d[name] = nc.dram_tensor(name, shape, dt, kind=sk).ap()
    return d


def fm3(ap_dram, t0, tw, c0=0, ncs=NC):
    return ap_dram.rearrange("(c p) t -> p c t", p=128)[:, c0:c0 + ncs, t0:t0 + tw]


class Consts:

    def __init__(self, pr, d):
        nc = pr.nc
        self.eye_bf = nc.alloc_sbuf_tensor("c_eye_bf", [128, 128], BF16)
        self.eye_f32 = nc.alloc_sbuf_tensor("c_eye_f32", [128, 128], F32)
        self.ones_row_bf = nc.alloc_sbuf_tensor("c_ones_row", [1, 128], BF16)
        self.ones_row_f32 = nc.alloc_sbuf_tensor("c_ones_row_f32", [1, 128], F32)
        self.ones_col_bf = nc.alloc_sbuf_tensor("c_ones_col", [128, 1], BF16)
        self.pos_row = nc.alloc_sbuf_tensor("c_pos_row", [1, T], F32)
        self.upos = nc.alloc_sbuf_tensor("c_upos", [128, 32], F32)
        self.cdelta = nc.alloc_sbuf_tensor("c_cdelta", [128, 4 * 512], F32)
        self.tri = nc.alloc_sbuf_tensor("c_tri", [128, 128], F32)
        self.headmask = nc.alloc_sbuf_tensor("c_headmask", [128, 16], F32)
        self.epscol = nc.alloc_sbuf_tensor("c_epscol", [128, 1], F32)
        self.negbig = nc.alloc_sbuf_tensor("c_negbig", [128, 1], F32)
        self.maskA = nc.alloc_sbuf_tensor("c_maskA", [1, 128], BF16)
        self.maskB = nc.alloc_sbuf_tensor("c_maskB", [1, 128], BF16)
        self.maskAf = nc.alloc_sbuf_tensor("c_maskAf", [1, 128], F32)
        self.maskBf = nc.alloc_sbuf_tensor("c_maskBf", [1, 128], F32)
        pr.dma("y", self.eye_bf[:], d["eye_bf"])
        pr.dma("y", self.eye_f32[:], d["eye_f32"])
        pr.dma("y", self.ones_row_bf[:], d["ones_row_bf"])
        pr.dma("y", self.ones_row_f32[:], d["ones_row_f32"])
        pr.dma("y", self.ones_col_bf[:], d["ones_col_bf"])
        pr.dma("y", self.pos_row[:], d["pos_row"])
        pr.dma("y", self.upos[:], d["upos"])
        for k in range(4):
            pr.dma("y", self.cdelta[:, k * 512:(k + 1) * 512], d["cdelta"][k])
        pr.dma("y", self.tri[:], d["tri"])
        pr.dma("y", self.headmask[:], d["headmask"])
        pr.dma("y", self.epscol[:], d["epscol"])
        pr.dma("y", self.negbig[:], d["negbig"])
        pr.dma("y", self.maskA[:], d["maskA_bf"])
        pr.dma("y", self.maskB[:], d["maskB_bf"])
        pr.dma("y", self.maskAf[:], d["maskA_f32"])
        pr.dma("y", self.maskBf[:], d["maskB_f32"])


def emit_ln_to_fm(pr, C, src_nat_dram, dst_fm_dram, gain_sb=None, bias_sb=None,
                  src2_info=None, tag=""):
    pr.fence()
    """dst_fm = LN(src_nat [+ src2]) transposed to FM (bf16), plain LN with
    optional per-feature affine applied in FM after the transpose.

    gain_sb/bias_sb: SBUF [128, NC] FM vectors or None.
    src2_info: unused for now.
    """
    nc = pr.nc
    with (
        pr.guard(nc.sbuf_tensor(f"ln{tag}_x0", [128, D], F32, allow_name_mangling=True)) as xt0,
        pr.guard(nc.sbuf_tensor(f"ln{tag}_x1", [128, D], F32, allow_name_mangling=True)) as xt1,
        pr.guard(nc.sbuf_tensor(f"ln{tag}_xn0", [128, D], BF16, allow_name_mangling=True)) as xn0,
        pr.guard(nc.sbuf_tensor(f"ln{tag}_xn1", [128, D], BF16, allow_name_mangling=True)) as xn1,
        pr.guard(nc.sbuf_tensor(f"ln{tag}_st", [128, 8], F32, allow_name_mangling=True)) as stt0,
        pr.guard(nc.sbuf_tensor(f"ln{tag}_st1", [128, 8], F32, allow_name_mangling=True)) as stt1,
        pr.guard(nc.sbuf_tensor(f"ln{tag}_sq", [128, D], F32, allow_name_mangling=True)) as sq0,
        pr.guard(nc.sbuf_tensor(f"ln{tag}_sq1", [128, D], F32, allow_name_mangling=True)) as sq1,
        pr.guard(nc.sbuf_tensor(f"ln{tag}_o0", [128, 1024], BF16, allow_name_mangling=True)) as o0,
        pr.guard(nc.sbuf_tensor(f"ln{tag}_o1", [128, 1024], BF16, allow_name_mangling=True)) as o1,
        pr.guard(nc.psum_tensor(f"ln{tag}_p0", [128, 1024], BF16, allow_name_mangling=True)) as p0,
        pr.guard(nc.psum_tensor(f"ln{tag}_p1", [128, 1024], BF16, allow_name_mangling=True)) as p1,
    ):
        xts, xns, os_, ps = [xt0, xt1], [xn0, xn1], [o0, o1], [p0, p1]
        for i in range(T // 128):
            t0 = i * 128
            xt, xn, ot, pt = xts[i % 2], xns[i % 2], os_[i % 2], ps[i % 2]
            stt, sq = (stt0, sq0) if i % 2 == 0 else (stt1, sq1)
            pr.dma("y", xt[:], src_nat_dram[t0:t0 + 128, :])
            pr.do("v", "reduce_sum", stt[:, 0:1], xt[:], ins=[xt], outs=[stt], axis=AX.X)
            pr.do("s", "activation", sq[:], xt[:], AF.Square, ins=[xt], outs=[sq, stt],
                  accum_out=stt[:, 1:2])
            pr.do("v", "tensor_scalar", stt[:, 2:4], stt[:, 0:2], 1.0 / D, None,
                  ALU.mult, ins=[stt, sq], outs=[stt])
            pr.do("v", "tensor_mul", stt[:, 4:5], stt[:, 2:3], stt[:, 2:3],
                  ins=[stt], outs=[stt])
            pr.do("v", "tensor_sub", stt[:, 4:5], stt[:, 3:4], stt[:, 4:5],
                  ins=[stt], outs=[stt])
            pr.do("s", "activation", stt[:, 5:6], stt[:, 4:5], AF.Sqrt,
                  bias=C.epscol[:], ins=[stt, C.epscol], outs=[stt])
            pr.do("v", "reciprocal", stt[:, 6:7], stt[:, 5:6], ins=[stt], outs=[stt])
            pr.do("v", "tensor_scalar", xn[:], xt[:], stt[:, 2:3], stt[:, 6:7],
                  ALU.subtract, ALU.mult, ins=[xt, stt], outs=[xn])
            for c in range(NC):
                pr.do("t", "transpose", pt[:, c * 128:(c + 1) * 128],
                      xn[:, c * 128:(c + 1) * 128], C.eye_bf[:],
                      ins=[xn, C.eye_bf], outs=[pt])
            if gain_sb is not None:
                for c in range(NC):
                    pr.do("v", "tensor_scalar", ot[:, c * 128:(c + 1) * 128],
                          pt[:, c * 128:(c + 1) * 128],
                          gain_sb[:, c:c + 1], bias_sb[:, c:c + 1],
                          ALU.mult, ALU.add, ins=[pt, gain_sb, bias_sb], outs=[ot])
            else:
                pr.do("v", "tensor_copy", ot[:], pt[:], ins=[pt], outs=[ot])
            pr.dma("y", fm3(dst_fm_dram, t0, 128),
                   ot[:].rearrange("p (c t) -> p c t", c=NC))
    return


def load_fm_block(pr, qk, sb, fm_dram, t0, tw, ncs=NC, c0=0):
    pr.dma(qk, sb[:].rearrange("p (c t) -> p c t", c=ncs), fm3(fm_dram, t0, tw, c0, ncs))


def emit_quant(pr, C, d, wq_sb):
    pr.fence()
    """Quantize L0_ffn kernel int4-style into wq_sb [128, NC*1024] bf16."""
    nc = pr.nc
    with (
        pr.guard(nc.sbuf_tensor("qk_w", [128, NC * 1024], F32, allow_name_mangling=True)) as wk,
        pr.guard(nc.sbuf_tensor("qk_am", [128, 9], F32, allow_name_mangling=True)) as am,
        pr.guard(nc.sbuf_tensor("qk_amr", [1, 128], F32, allow_name_mangling=True)) as amr,
        pr.guard(nc.sbuf_tensor("qk_sc", [1, 4], F32, allow_name_mangling=True)) as sc,
        pr.guard(nc.sbuf_tensor("qk_scb", [128, 2], F32, allow_name_mangling=True)) as scb,
        pr.guard(nc.sbuf_tensor("qk_q", [128, 1024], F32, allow_name_mangling=True)) as qq,
        pr.guard(nc.psum_tensor("qk_p", [128, 512], F32, allow_name_mangling=True)) as pp,
        pr.guard(nc.psum_tensor("qk_pb", [128, 512], F32, allow_name_mangling=True)) as pb,
    ):
        load_fm_block(pr, "y", wk, d["L0_fk"], 0, 1024)
        for c in range(NC):
            pr.do("v", "reduce_max", am[:, c:c + 1], wk[:, c * 1024:(c + 1) * 1024],
                  ins=[wk], outs=[am], axis=AX.X, apply_absolute_value=True)
        pr.do("v", "reduce_max", am[:, 8:9], am[:, 0:8], ins=[am], outs=[am],
              axis=AX.X, apply_absolute_value=True)
        pr.do("t", "transpose", pp[0:1, 0:128], am[:, 8:9],
              C.eye_f32[:], ins=[am, C.eye_f32], outs=[pp])
        pr.do("v", "reduce_max", sc[:, 0:1], pp[0:1, 0:128], ins=[pp], outs=[sc],
              axis=AX.X, apply_absolute_value=True)
        pr.do("v", "tensor_scalar_mul", sc[:, 1:2], sc[:, 0:1], 1.0 / 7.0,
              ins=[sc], outs=[sc])
        pr.do("v", "reciprocal", sc[:, 2:3], sc[:, 1:2], ins=[sc], outs=[sc])
        pr.do("t", "matmul", pb[:, 0:2], C.ones_row_f32[:], sc[:, 1:3],
              ins=[C.ones_row_f32, sc], outs=[pb], start=True, stop=True)
        pr.do("v", "tensor_copy", scb[:], pb[:, 0:2], ins=[pb], outs=[scb])
        for c in range(NC):
            w_c = wk[:, c * 1024:(c + 1) * 1024]
            pr.do("v", "tensor_scalar", qq[:], w_c, scb[:, 1:2], MAGIC,
                  ALU.mult, ALU.add, ins=[wk, scb], outs=[qq])
            pr.do("v", "tensor_scalar", wq_sb[:, c * 1024:(c + 1) * 1024], qq[:],
                  MAGIC, scb[:, 0:1], ALU.subtract, ALU.mult,
                  ins=[qq, scb], outs=[wq_sb])


def emit_gffn_l0(pr, C, d):
    pr.fence()
    """g = silu(xn @ quant(kernel) + bias + (xn@a)@b), FM f32 -> g_fm dram."""
    nc = pr.nc
    with (
        pr.guard(nc.sbuf_tensor("gf_wq", [128, NC * 1024], BF16, allow_name_mangling=True)) as wq,
        pr.guard(nc.sbuf_tensor("gf_a", [128, NC * 16], BF16, allow_name_mangling=True)) as a_sb,
        pr.guard(nc.sbuf_tensor("gf_b", [16, 1024], BF16, allow_name_mangling=True)) as b_sb,
        pr.guard(nc.sbuf_tensor("gf_bias", [128, NC], F32, allow_name_mangling=True)) as bias_sb,
    ):
        emit_quant(pr, C, d, wq)
        pr.fence()
        pr.dma("g", a_sb[:].rearrange("p (c n) -> p c n", c=NC),
               d["L0_fa"].rearrange("(c p) n -> p c n", p=128))
        pr.dma("g", b_sb[:], d["L0_fbb"])
        pr.dma("y", bias_sb[:], d["L0_fb"])
        with (
            pr.guard(nc.sbuf_tensor("gf_xn0", [128, NC * 512], BF16, allow_name_mangling=True)) as xnb0,
            pr.guard(nc.sbuf_tensor("gf_xn1", [128, NC * 512], BF16, allow_name_mangling=True)) as xnb1,
            pr.guard(nc.sbuf_tensor("gf_l1", [16, 512], BF16, allow_name_mangling=True)) as l1,
            pr.guard(nc.sbuf_tensor("gf_gfm", [128, NC * 512], F32, allow_name_mangling=True)) as gfm,
            pr.guard(nc.sbuf_tensor("gf_gn0", [128, 1024], F32, allow_name_mangling=True)) as gn0,
            pr.guard(nc.sbuf_tensor("gf_gn1", [128, 1024], F32, allow_name_mangling=True)) as gn1,
            pr.guard(nc.psum_tensor("gf_pl", [16, 512], F32, allow_name_mangling=True)) as pl,
            pr.guard(nc.psum_tensor("gf_p0", [128, 512], F32, allow_name_mangling=True)) as pp0,
            pr.guard(nc.psum_tensor("gf_p1", [128, 512], F32, allow_name_mangling=True)) as pp1,
            pr.guard(nc.psum_tensor("gf_pn0", [128, 1024], F32, allow_name_mangling=True)) as pn0,
            pr.guard(nc.psum_tensor("gf_pn1", [128, 1024], F32, allow_name_mangling=True)) as pn1,
        ):
            xnbs, pps = [xnb0, xnb1], [pp0, pp1]
            gns, pns = [gn0, gn1], [pn0, pn1]
            for it in range(T // 512):
                t0 = it * 512
                xnb = xnbs[it % 2]
                load_fm_block(pr, "y", xnb, d["xn_fm"], t0, 512)
                for c in range(NC):
                    pr.do("t", "matmul", pl[:], a_sb[:, c * 16:(c + 1) * 16],
                          xnb[:, c * 512:(c + 1) * 512], ins=[a_sb, xnb], outs=[pl],
                          start=(c == 0), stop=(c == NC - 1))
                pr.do("v", "tensor_copy", l1[:], pl[:], ins=[pl], outs=[l1])
                for do in range(NC):
                    pp = pps[do % 2]
                    for c in range(NC):
                        pr.do("t", "matmul", pp[:],
                              wq[:, c * 1024 + do * 128: c * 1024 + do * 128 + 128],
                              xnb[:, c * 512:(c + 1) * 512], ins=[wq, xnb], outs=[pp],
                              start=(c == 0), stop=False)
                    pr.do("t", "matmul", pp[:], b_sb[:, do * 128:(do + 1) * 128],
                          l1[:], ins=[b_sb, l1], outs=[pp], start=False, stop=True)
                    pr.do("s", "activation", gfm[:, do * 512:(do + 1) * 512], pp[:],
                          AF.Silu, bias=bias_sb[:, do:do + 1], ins=[pp, bias_sb],
                          outs=[gfm])
                for sub in range(4):
                    pn = pns[sub % 2]
                    for do in range(NC):
                        pr.do("t", "transpose", pn[:, do * 128:(do + 1) * 128],
                              gfm[:, do * 512 + sub * 128: do * 512 + sub * 128 + 128],
                              C.eye_f32[:], ins=[gfm, C.eye_f32], outs=[pn])
                    gn = gns[sub % 2]
                    pr.do("v", "tensor_copy", gn[:], pn[:], ins=[pn], outs=[gn])
                    pr.dma("g", d["g_nat"][t0 + sub * 128: t0 + sub * 128 + 128, :], gn[:])



def emit_moe(pr, C, d):
    nc = pr.nc
    pr.fence()
    with pr.scope() as _st:
        with pr.scope() as _sg:
            gsb, ebb, xnt, scn, bn, b2n, m1, msk, msk2, w8 = sb_alloc(
                nc, _sg,
                ("mo_gsb", [128, NC * 8], BF16), ("mo_ebb", [128, 8], F32),
                ("mo_xnt", [128, NC * 128], BF16), ("mo_scn", [128, 8], F32),
                ("mo_bn", [128, 8], F32), ("mo_b2n", [128, 8], F32),
                ("mo_m1", [128, 2], F32), ("mo_msk", [128, 8], F32),
                ("mo_msk2", [128, 8], F32), ("mo_w8", [8, 128], F32))
            (ebr,) = sb_alloc(nc, _sg, ("mo_ebr", [1, 8], F32))
            pg, ptr, peb = ps_alloc(nc, _sg, ("mo_pg", [128, 512], F32),
                                    ("mo_ptr", [8, 512], F32),
                                    ("mo_peb", [128, 512], F32))
            pr.dma("g", gsb[:].rearrange("p (c n) -> p c n", c=NC),
                   d["L1_gate"].rearrange("(c p) n -> p c n", p=128))
            pr.dma("y", ebr[:], d["L1_ebias"])
            pr.do("t", "matmul", peb[:, 0:8], C.ones_row_f32[:], ebr[:],
                  ins=[C.ones_row_f32, ebr], outs=[peb], start=True, stop=True)
            pr.do("v", "tensor_copy", ebb[:], peb[:, 0:8], ins=[peb], outs=[ebb])
            for tc in range(T // 128):
                t0 = tc * 128
                load_fm_block(pr, "y", xnt, d["xn_fm"], t0, 128)
                for c in range(NC):
                    pr.do("t", "matmul", pg[:, 0:8], xnt[:, c * 128:(c + 1) * 128],
                          gsb[:, c * 8:(c + 1) * 8], ins=[xnt, gsb], outs=[pg],
                          start=(c == 0), stop=(c == NC - 1))
                pr.do("s", "activation", scn[:], pg[:, 0:8], AF.Sigmoid,
                      ins=[pg], outs=[scn])
                pr.do("v", "tensor_add", bn[:], scn[:], ebb[:],
                      ins=[scn, ebb], outs=[bn])
                pr.do("v", "reduce_max", m1[:, 0:1], bn[:], ins=[bn], outs=[m1],
                      axis=AX.X)
                pr.do("v", "tensor_scalar", msk[:], bn[:], m1[:, 0:1], None,
                      ALU.is_ge, ins=[bn, m1], outs=[msk])
                pr.do("v", "scalar_tensor_tensor", b2n[:], msk[:], C.negbig[:],
                      bn[:], ALU.mult, ALU.add, ins=[msk, C.negbig, bn], outs=[b2n])
                pr.do("v", "reduce_max", m1[:, 1:2], b2n[:], ins=[b2n], outs=[m1],
                      axis=AX.X)
                pr.do("v", "tensor_scalar", msk2[:], b2n[:], m1[:, 1:2], None,
                      ALU.is_ge, ins=[b2n, m1], outs=[msk2])
                pr.do("v", "tensor_add", msk[:], msk[:], msk2[:],
                      ins=[msk, msk2], outs=[msk])
                pr.do("v", "tensor_mul", msk[:], msk[:], scn[:],
                      ins=[msk, scn], outs=[msk])
                pr.do("v", "tensor_mul", msk[:], msk[:], msk[:],
                      ins=[msk], outs=[msk])
                pr.do("t", "transpose", ptr[0:8, 0:128], msk[:], C.eye_f32[:],
                      ins=[msk, C.eye_f32], outs=[ptr])
                pr.do("v", "tensor_copy", w8[:], ptr[0:8, 0:128],
                      ins=[ptr], outs=[w8])
                pr.dma("g", d["w2t"][:, t0:t0 + 128], w8[:])
        pr.fence()
        with pr.scope() as _se:
            (macc,) = sb_alloc(nc, _se, ("mo_macc", [128, NC * 2048], F32))
            for th in range(2):
                h0 = th * 2048
                with pr.scope() as _se2:
                    wup0, wup1, wdn, xnb0, xnb1, h2, rr0, rr1, tt_, wb, w2r = sb_alloc(
                        nc, _se2,
                        ("mo_wup", [128, NC * HID], BF16),
                        ("mo_wup1", [128, NC * HID], BF16),
                        ("mo_wdn", [128, NHC * 1024], BF16),
                        ("mo_xnb0", [128, NC * 512], BF16),
                        ("mo_xnb1", [128, NC * 512], BF16),
                        ("mo_h2", [128, NHC * 512], BF16),
                        ("mo_rr0", [128, 512], BF16), ("mo_rr1", [128, 512], BF16),
                        ("mo_tt", [128, 512], BF16), ("mo_wb", [128, 512], BF16),
                        ("mo_w2r", [1, 512], F32))
                    pwb, ph0, ph1, py0, py1 = ps_alloc(
                        nc, _se2, ("mo_pwb", [128, 512], F32),
                        ("mo_ph0", [128, 512], F32), ("mo_ph1", [128, 512], F32),
                        ("mo_py0", [128, 512], F32), ("mo_py1", [128, 512], F32))
                    xnbs, phs, pys, rrs = [xnb0, xnb1], [ph0, ph1], [py0, py1], [rr0, rr1]
                    for e in range(NEXP + 1):
                        shared = e == NEXP
                        wup = wup0 if e % 2 == 0 else wup1
                        if shared:
                            up_src = d["L1_swup"].rearrange("(c p) n -> p c n", p=128)
                            dn_src = d["L1_swdn"].rearrange("(c p) n -> p c n", p=128)
                        else:
                            up_src = d["L1_wup"][e].rearrange("(c p) n -> p c n", p=128)
                            dn_src = d["L1_wdn"][e].rearrange("(c p) n -> p c n", p=128)
                        pr.dma("g", wup[:].rearrange("p (c n) -> p c n", c=NC), up_src)
                        pr.dma("g", wdn[:].rearrange("p (c n) -> p c n", c=NHC), dn_src)
                        for tb in range(4):
                            t0 = h0 + tb * 512
                            xnb = xnbs[tb % 2]
                            load_fm_block(pr, "y", xnb, d["xn_fm"], t0, 512)
                            if not shared:
                                pr.dma("y", w2r[:], d["w2t"][e:e + 1, t0:t0 + 512])
                                pr.do("t", "matmul", pwb[:], C.ones_row_f32[:],
                                      w2r[:], ins=[C.ones_row_f32, w2r], outs=[pwb],
                                      start=True, stop=True)
                                pr.do("v", "tensor_copy", wb[:], pwb[:],
                                      ins=[pwb], outs=[wb])
                            for ns in range(NHC):
                                ph = phs[ns % 2]
                                rr = rrs[ns % 2]
                                for c in range(NC):
                                    pr.do("t", "matmul", ph[:],
                                          wup[:, c * HID + ns * 128:
                                              c * HID + ns * 128 + 128],
                                          xnb[:, c * 512:(c + 1) * 512],
                                          ins=[wup, xnb], outs=[ph],
                                          start=(c == 0), stop=(c == NC - 1))
                                pr.do("s", "activation", rr[:], ph[:], AF.Relu,
                                      ins=[ph], outs=[rr])
                                h2s = h2[:, ns * 512:(ns + 1) * 512]
                                if shared:
                                    pr.do("v", "tensor_mul", h2s, rr[:], rr[:],
                                          ins=[rr], outs=[h2])
                                else:
                                    pr.do("v", "tensor_mul", tt_[:], rr[:], wb[:],
                                          ins=[rr, wb], outs=[tt_])
                                    pr.do("v", "tensor_mul", h2s, tt_[:], rr[:],
                                          ins=[tt_, rr], outs=[h2])
                            for do in range(NC):
                                py = pys[do % 2]
                                for ns in range(NHC):
                                    pr.do("t", "matmul", py[:],
                                          wdn[:, ns * 1024 + do * 128:
                                              ns * 1024 + do * 128 + 128],
                                          h2[:, ns * 512:(ns + 1) * 512],
                                          ins=[wdn, h2], outs=[py],
                                          start=(ns == 0), stop=(ns == NHC - 1))
                                msl = macc[:, do * 2048 + tb * 512:
                                           do * 2048 + (tb + 1) * 512]
                                if e == 0:
                                    pr.do("v", "tensor_copy", msl, py[:],
                                          ins=[py], outs=[macc])
                                else:
                                    pr.do("v", "tensor_tensor", msl, msl, py[:],
                                          ALU.add, ins=[py, macc], outs=[macc])
                pr.fence()
                with pr.scope() as _so:
                    (gn0, gn1) = sb_alloc(nc, _so, ("mo_gn0", [128, D], F32),
                                          ("mo_gn1", [128, D], F32))
                    pn0, pn1 = ps_alloc(nc, _so, ("mo_pn0", [128, 1024], F32),
                                        ("mo_pn1", [128, 1024], F32))
                    for do in range(NC):
                        sl = macc[:, do * 2048:(do + 1) * 2048]
                        pr.do("s", "activation", sl, sl, AF.Silu,
                              ins=[macc], outs=[macc])
                    for tc in range(16):
                        t0 = h0 + tc * 128
                        pn, gn = (pn0, gn0) if tc % 2 == 0 else (pn1, gn1)
                        for do in range(NC):
                            pr.do("t", "transpose", pn[:, do * 128:(do + 1) * 128],
                                  macc[:, do * 2048 + tc * 128:
                                       do * 2048 + tc * 128 + 128],
                                  C.eye_f32[:], ins=[macc, C.eye_f32], outs=[pn])
                        pr.do("v", "tensor_copy", gn[:], pn[:], ins=[pn], outs=[gn])
                        pr.dma("g", d["g_nat"][t0:t0 + 128, :], gn[:])
                pr.fence()


def emit_out_l1(pr, C, d):
    nc = pr.nc
    pr.fence()
    with pr.scope() as _st:
        (wo,) = sb_alloc(nc, _st, ("ol_wo", [128, NC * D], BF16))
        pr.dma("g", wo[:].rearrange("p (c n) -> p c n", c=NC),
               d["L1_wout"].rearrange("(c p) n -> p c n", p=128))
        ob0, ob1, s2fm0, s2fm1, s2n0, s2n1, gt0, gt1, xt0, xt1, out0, out1 = sb_alloc(
            nc, _st,
            ("ol_ob", [128, NC * 512], BF16), ("ol_ob1", [128, NC * 512], BF16),
            ("ol_s2fm", [128, NC * 512], F32), ("ol_s2fm1", [128, NC * 512], F32),
            ("ol_s2n", [128, D], F32), ("ol_s2n1", [128, D], F32),
            ("ol_gt", [128, D], F32), ("ol_gt1", [128, D], F32),
            ("ol_xt", [128, D], F32), ("ol_xt1", [128, D], F32),
            ("ol_out", [128, D], F32), ("ol_out1", [128, D], F32))
        pp0, pp1, pn_0, pn_1 = ps_alloc(nc, _st, ("ol_pp0", [128, 512], F32),
                                ("ol_pp1", [128, 512], F32),
                                ("ol_pn", [128, 1024], F32),
                                ("ol_pn1", [128, 1024], F32))
        pps = [pp0, pp1]
        for tb in range(T // 512):
            t0 = tb * 512
            ob, s2fm = (ob0, s2fm0) if tb % 2 == 0 else (ob1, s2fm1)
            load_fm_block(pr, "y", ob, d["o_fm"], t0, 512)
            for do in range(NC):
                pp = pps[do % 2]
                for c in range(NC):
                    pr.do("t", "matmul", pp[:],
                          wo[:, c * D + do * 128: c * D + do * 128 + 128],
                          ob[:, c * 512:(c + 1) * 512], ins=[wo, ob], outs=[pp],
                          start=(c == 0), stop=(c == NC - 1))
                pr.do("v", "tensor_copy", s2fm[:, do * 512:(do + 1) * 512], pp[:],
                      ins=[pp], outs=[s2fm])
            for sub in range(4):
                pn = pn_0 if sub % 2 == 0 else pn_1
                s2n, gt, xt, outt = (s2n0, gt0, xt0, out0) if sub % 2 == 0 \
                    else (s2n1, gt1, xt1, out1)
                for do in range(NC):
                    pr.do("t", "transpose", pn[:, do * 128:(do + 1) * 128],
                          s2fm[:, do * 512 + sub * 128: do * 512 + sub * 128 + 128],
                          C.eye_f32[:], ins=[s2fm, C.eye_f32], outs=[pn])
                pr.do("v", "tensor_copy", s2n[:], pn[:], ins=[pn], outs=[s2n])
                tt = t0 + sub * 128
                pr.dma("y", gt[:], d["g_nat"][tt:tt + 128, :])
                pr.dma("y", xt[:], d["xres_nat"][tt:tt + 128, :])
                pr.do("v", "tensor_mul", s2n[:], s2n[:], gt[:],
                      ins=[s2n, gt], outs=[s2n])
                pr.do("v", "tensor_add", outt[:], xt[:], s2n[:],
                      ins=[xt, s2n], outs=[outt])
                pr.dma("y", d["out"][tt:tt + 128, :], outt[:])

def build_program(host, phases=None):
    phases = phases if phases is not None else\
        ["p0", "p1", "p2", "p3", "p4", "p5", "p6", "p7", "p8"]
    nc = bass.Bass("TRN2", target_bir_lowering=False)
    host_shapes = {k: (v.shape, BF16 if v.dtype == ml_dtypes.bfloat16 else F32)
                   for k, v in host.items()}
    d = declare_io(nc, host_shapes)
    pr = Prog(nc)
    C = Consts(pr, d)
    if "p0" in phases:
        emit_ln_to_fm(pr, C, d["x0"], d["xn_fm"], tag="0")
    if "p1" in phases:
        emit_gffn_l0(pr, C, d)
    if "p2" in phases:
        emit_attention(pr, C, d, 0)
    if "p3" in phases:
        emit_proj_ln1(pr, C, d)
    if "p4" in phases:
        emit_ffn_combine_l0(pr, C, d)
    if "p5" in phases:
        emit_ln_to_fm(pr, C, d["xres_nat"], d["xn_fm"], tag="2")
    if "p6" in phases:
        emit_moe(pr, C, d)
    if "p7" in phases:
        emit_attention(pr, C, d, 1)
    if "p8" in phases:
        emit_out_l1(pr, C, d)
    for ek in ("y", "g", "t", "v", "s"):
        pr.barrier_on(ek)
    return nc


def kernel(**inputs):
    host = prep_host(inputs)
    nc = build_program(host, PHASES)
    toks = np.asarray(inputs["L0_attn_tokens"])
    in_maps = []
    for i in range(8):
        m = {"x0": _f32(toks[i])}
        m.update(host)
        in_maps.append(m)
    res = run_bass_kernel_spmd(nc, in_maps, core_ids=list(range(8)))
    return np.stack([np.asarray(res.results[i]["out"]) for i in range(8)])


def emit_attention(pr, C, d, layer):
    nc = pr.nc
    pr.fence()
    NQG = T // 512
    with (
        pr.guard(nc.sbuf_tensor("at_xn", [128, NC * T], BF16, allow_name_mangling=True)) as xn_sb,
        pr.guard(nc.sbuf_tensor("at_frow", [1, 2 * T], BF16, allow_name_mangling=True)) as frow,
        pr.guard(nc.sbuf_tensor("at_eu", [128, 64], F32, allow_name_mangling=True)) as eu,
        pr.guard(nc.sbuf_tensor("at_fb", [128, 16 * 512], BF16, allow_name_mangling=True)) as fb,
        pr.guard(nc.sbuf_tensor("at_dsv", [1, 4], F32, allow_name_mangling=True)) as dsv,
        pr.guard(nc.sbuf_tensor("at_dsb", [128, 2], F32, allow_name_mangling=True)) as dsb,
        pr.guard(nc.sbuf_tensor("at_bq8", [128, NC], F32, allow_name_mangling=True)) as bq8,
        pr.guard(nc.sbuf_tensor("at_bk", [128, NC], F32, allow_name_mangling=True)) as bk_sb,
        pr.guard(nc.sbuf_tensor("at_bv", [128, NC], F32, allow_name_mangling=True)) as bv_sb,
    ):
        load_fm_block(pr, "y", xn_sb, d["xn_fm"], 0, T)
        if layer == 0:
            pr.dma("y", dsv[:, 0:1], d["L0_ds"])
            pr.dma("y", bq8[:], d["L0_bq"])
            pr.dma("y", bk_sb[:], d["L0_bk"])
            pr.dma("y", bv_sb[:], d["L0_bv"])
            pr.do("v", "tensor_scalar_mul", bq8[:], bq8[:], 0.125,
                  ins=[bq8], outs=[bq8])
            pr.do("v", "tensor_scalar_mul", dsv[:, 1:2], dsv[:, 0:1], 0.125,
                  ins=[dsv], outs=[dsv])
            pr.do("v", "tensor_scalar_mul", dsv[:, 2:3], dsv[:, 1:2], -1.0,
                  ins=[dsv], outs=[dsv])
            with pr.guard(nc.psum_tensor("at_pds", [128, 512], F32, allow_name_mangling=True)) as pds:
                pr.do("t", "matmul", pds[:, 0:2], C.ones_row_f32[:], dsv[:, 1:3],
                      ins=[C.ones_row_f32, dsv], outs=[pds], start=True, stop=True)
                pr.do("v", "tensor_copy", dsb[:], pds[:, 0:2], ins=[pds], outs=[dsb])
            pr.do("v", "tensor_scalar_mul", eu[:, 0:32], C.upos[:], dsb[:, 0:1],
                  ins=[C.upos, dsb], outs=[eu])
            pr.do("v", "tensor_scalar_mul", eu[:, 32:64], C.upos[:], dsb[:, 1:2],
                  ins=[C.upos, dsb], outs=[eu])
            pr.do("s", "activation", eu[:], eu[:], AF.Exp, ins=[eu], outs=[eu])
            pr.do("v", "tensor_single_scalar", frow[0:1, 0:T], C.pos_row[:], dsv[:, 2:3],
                  ALU.mult, ins=[C.pos_row, dsv], outs=[frow])
            pr.do("v", "tensor_single_scalar", frow[0:1, T:2 * T], C.pos_row[:], dsv[:, 1:2],
                  ALU.mult, ins=[C.pos_row, dsv], outs=[frow])
            pr.do("s", "activation", frow[:], frow[:], AF.Exp, ins=[frow], outs=[frow])
            with pr.guard(nc.psum_tensor("at_pfb", [128, 512], F32, allow_name_mangling=True)) as pfb:
                for qg in range(NQG):
                    for cls in range(2):
                        pr.do("t", "matmul", pfb[:], C.ones_row_bf[:],
                              frow[0:1, cls * T + qg * 512: cls * T + (qg + 1) * 512],
                              ins=[C.ones_row_bf, frow], outs=[pfb],
                              start=True, stop=True)
                        pr.do("v", "tensor_copy",
                              fb[:, (cls * NQG + qg) * 512:(cls * NQG + qg + 1) * 512],
                              pfb[:], ins=[pfb], outs=[fb])
        pr.fence()

        shared = dict(xn_sb=xn_sb, frow=frow, eu=eu, fb=fb, dsv=dsv,
                      dsb=dsb, bq8=bq8, bk_sb=bk_sb, bv_sb=bv_sb)
        for p in range(NC):
            _attn_pair(pr, C, d, layer, p, shared)
            pr.fence()


def _attn_pair(pr, C, d, layer, p, sh):
    nc = pr.nc
    NQG = T // 512
    VW = 130
    xn_sb, frow, eu, fb = sh["xn_sb"], sh["frow"], sh["eu"], sh["fb"]
    dsv, dsb, bq8, bk_sb, bv_sb = sh["dsv"], sh["dsb"], sh["bq8"], sh["bk_sb"], sh["bv_sb"]
    with pr.scope() as _st:
        wsl, qt, kt, v_sb, vl_sb, vu_sb = sb_alloc(
            nc, _st,
            ("at_wsl", [128, 3 * NC * 128], BF16), ("at_qt", [128, T], BF16),
            ("at_kt", [128, T], BF16),
            ("at_v", [128, 32 * VW], BF16), ("at_vl", [128, 32 * VW], BF16),
            ("at_vu", [128, 32 * VW], BF16))
        _stv = _st.enter_context(pr.scope())
        (vt,) = sb_alloc(nc, _stv, ("at_vt", [128, T], BF16))
        if layer == 0:
            for wi, wname in enumerate(("L0_Wq", "L0_Wk", "L0_Wv")):
                pr.dma("g", wsl[:, wi * NC * 128:(wi + 1) * NC * 128]
                       .rearrange("p (c n) -> p c n", c=NC),
                       d[wname].rearrange("(c p) n -> p c n", p=128)
                       [:, :, p * 128:(p + 1) * 128])
        else:
            qkv = d["L1_wqkv"].rearrange("(c p) n -> p c n", p=128)
            for wi in range(3):
                pr.dma("g", wsl[:, wi * NC * 128:(wi + 1) * NC * 128]
                       .rearrange("p (c n) -> p c n", c=NC),
                       qkv[:, :, wi * D + p * 128: wi * D + p * 128 + 128])
        with pr.guard(nc.psum_tensor("at_pj", [128, 512], F32, allow_name_mangling=True)) as pj:
            for wi, dst in ((0, qt), (1, kt), (2, vt)):
                for tb in range(T // 512):
                    t0 = tb * 512
                    for c in range(NC):
                        pr.do("t", "matmul", pj[:],
                              wsl[:, wi * NC * 128 + c * 128:
                                  wi * NC * 128 + (c + 1) * 128],
                              xn_sb[:, c * T + t0: c * T + t0 + 512],
                              ins=[wsl, xn_sb], outs=[pj],
                              start=(c == 0), stop=(c == NC - 1))
                    if layer == 0:
                        bias = (bq8, bk_sb, bv_sb)[wi][:, p:p + 1]
                        sc = 0.125 if wi == 0 else 1.0
                        pr.do("v", "tensor_scalar", dst[:, t0:t0 + 512], pj[:],
                              sc, bias, ALU.mult, ALU.add,
                              ins=[pj, bq8, bk_sb, bv_sb], outs=[dst])
                    elif wi == 0:
                        pr.do("s", "activation", dst[:, t0:t0 + 512], pj[:],
                              AF.Copy, scale=0.125, ins=[pj], outs=[dst])
                    else:
                        pr.do("v", "tensor_copy", dst[:, t0:t0 + 512], pj[:],
                              ins=[pj], outs=[dst])
        pr.fence()
        pr.do("v", "memset", v_sb[:].rearrange("p (t g w) -> p t g w", g=2, w=65)
              [:, :, :, 64:65], 1.0, ins=[], outs=[v_sb])
        with pr.guard(nc.psum_tensor("at_pv", [128, 1024], BF16, allow_name_mangling=True)) as pv:
            for tc in range(T // 128):
                pr.do("t", "transpose", pv[:, 0:128], vt[:, tc * 128:(tc + 1) * 128],
                      C.eye_bf[:], ins=[vt, C.eye_bf], outs=[pv])
                pr.do("v", "tensor_copy",
                      v_sb[:, tc * VW: (tc + 1) * VW]
                      .rearrange("p (g w) -> p g w", g=2)[:, :, 0:64],
                      pv[:, 0:128].rearrange("p (g w) -> p g w", g=2),
                      ins=[pv], outs=[v_sb])
            if layer == 0:
                for tc in range(T // 128):
                    pr.do("v", "tensor_scalar_mul",
                          vl_sb[:, tc * VW:(tc + 1) * VW],
                          v_sb[:, tc * VW:(tc + 1) * VW],
                          eu[:, tc:tc + 1], ins=[v_sb, eu], outs=[vl_sb])
                    pr.do("v", "tensor_scalar_mul",
                          vu_sb[:, tc * VW:(tc + 1) * VW],
                          v_sb[:, tc * VW:(tc + 1) * VW],
                          eu[:, 32 + tc:33 + tc], ins=[v_sb, eu], outs=[vu_sb])
        pr.fence()
        _stv.close()
        with pr.scope() as _st2:
            s00, s01, s10, s11 = ps_alloc(
                nc, _st2,
                ("at_s00", [128, 512], F32), ("at_s01", [128, 512], F32),
                ("at_s10", [128, 512], F32), ("at_s11", [128, 512], F32))
            accs = {}
            for cls in (("L", "U") if layer == 0 else ("L",)):
                for hh in range(2):
                    accs[(cls, hh)] = _st2.enter_context(pr.guard(nc.psum_tensor(
                        f"at_acc{cls}{hh}", [65, 512], F32,
                        allow_name_mangling=True)))
            p00, p01, p10, p11, ep1, ep2, dh, rb_sb, rb2_sb, ofin = sb_alloc(
                nc, _st2,
                ("at_p00", [128, 512], BF16), ("at_p01", [128, 512], BF16),
                ("at_p10", [128, 512], BF16), ("at_p11", [128, 512], BF16),
                ("at_ep1", [128, 512], F32), ("at_ep2", [128, 512], F32),
                ("at_dh", [1, 4 * 512], F32), ("at_rb", [128, 512], F32),
                ("at_rb2", [128, 512], F32),
                ("at_of", [128, 512], BF16))
            S = [[s00, s01], [s10, s11]]
            P = [[p00, p01], [p10, p11]]
            for qg in range(NQG):
                q0 = qg * 512
                n_uc = 32 if layer == 0 else 4 * qg + 4

                def classify(uc):
                    if layer == 0:
                        cls = "L" if uc <= 4 * qg + 3 else "U"
                        crossing = 4 * qg <= uc <= 4 * qg + 3
                        k = 0
                    else:
                        cls = "L"
                        crossing = uc >= 4 * qg
                        k = (uc - 4 * qg) if crossing else 0
                    if layer == 0 and cls == "L":
                        first, last = uc == 0, uc == min(4 * qg + 3, n_uc - 1)
                    elif layer == 0:
                        first, last = uc == 4 * qg + 4, uc == 31
                    else:
                        first, last = uc == 0, uc == n_uc - 1
                    return cls, crossing, k, first, last

                def emit_s(uc):
                    cls, crossing, k, first, last = classify(uc)
                    c0 = 128 * k
                    par = uc % 2
                    for hh in range(2):
                        st = S[hh][par]
                        r0, r1 = hh * 64, hh * 64 + 64
                        pr.do("t", "matmul", st[:, c0:512],
                              kt[r0:r1, uc * 128:(uc + 1) * 128],
                              qt[r0:r1, q0 + c0: q0 + 512],
                              ins=[kt, qt], outs=[st], start=True, stop=True,
                              tile_position=(hh * 64, 0), skip_group_check=True)
                        if layer == 0 and crossing:
                            di = uc - 4 * qg
                            pr.do("v", "scalar_tensor_tensor", st[:],
                                  C.cdelta[:, di * 512:(di + 1) * 512],
                                  dsb[:, 1:2], st[:], ALU.mult, ALU.add,
                                  ins=[C.cdelta, dsb, st], outs=[st])
                        if layer == 0 and uc == 0:
                            hg = 2 * p + hh
                            pr.do("v", "tensor_scalar_add", st[0:16, :],
                                  st[0:16, :], C.headmask[0:16, hg:hg + 1],
                                  ins=[st, C.headmask], outs=[st])
                        if layer == 1 and crossing:
                            pr.do("v", "tensor_tensor", st[:, c0:c0 + 128],
                                  st[:, c0:c0 + 128], C.tri[:], ALU.add,
                                  ins=[st, C.tri], outs=[st])

                def emit_exp(uc):
                    cls, crossing, k, first, last = classify(uc)
                    c0 = 128 * k
                    par = uc % 2
                    for hh in range(2):
                        pr.do("s", "activation", P[hh][par][:, c0:512],
                              S[hh][par][:, c0:512],
                              AF.Exp, ins=[S[hh][par]], outs=[P[hh][par]])

                def emit_dv(uc):
                    cls, crossing, k, first, last = classify(uc)
                    c0 = 128 * k
                    par = uc % 2
                    vx = (vl_sb if cls == "L" else vu_sb) if layer == 0 else v_sb
                    for hh in range(2):
                        pt = P[hh][par]
                        acc = accs[(cls, hh)]
                        pr.do("t", "matmul", acc[0:65, c0:512],
                              vx[:, uc * VW + hh * 65: uc * VW + hh * 65 + 65],
                              pt[:, c0:512],
                              ins=[vl_sb, vu_sb, v_sb, pt], outs=[acc],
                              start=first, stop=last,
                              tile_position=(0, 0), skip_group_check=True)

                for uc in range(n_uc):
                    emit_s(uc)
                    if uc > 1:
                        emit_dv(uc - 2)
                    emit_exp(uc)
                emit_dv(n_uc - 2)
                emit_dv(n_uc - 1)
                pep = s00
                for hh in range(2):
                    r0, r1 = hh * 64, hh * 64 + 64
                    o = hh * 512
                    al = accs[("L", hh)]
                    if layer == 0:
                        fbl = fb[:, qg * 512:(qg + 1) * 512]
                        fbu = fb[:, (NQG + qg) * 512:(NQG + qg + 1) * 512]
                        au = accs[("U", hh)]
                        pr.do("v", "tensor_tensor", ep1[r0:r1, :], al[0:64, :],
                              fbl[r0:r1, :], ALU.mult, ins=[al, fb], outs=[ep1])
                        pr.do("v", "tensor_tensor", dh[0:1, o:o + 512], al[64:65, :],
                              frow[0:1, q0:q0 + 512], ALU.mult,
                              ins=[al, frow], outs=[dh])
                        if qg < NQG - 1:
                            pr.do("v", "tensor_tensor", ep2[r0:r1, :], au[0:64, :],
                                  fbu[r0:r1, :], ALU.mult, ins=[au, fb], outs=[ep2])
                            pr.do("v", "tensor_add", ep1[r0:r1, :], ep1[r0:r1, :],
                                  ep2[r0:r1, :], ins=[ep1, ep2], outs=[ep1])
                            pr.do("v", "tensor_tensor",
                                  dh[0:1, 1024 + o:1024 + o + 512], au[64:65, :],
                                  frow[0:1, T + q0:T + q0 + 512], ALU.mult,
                                  ins=[au, frow], outs=[dh])
                            pr.do("v", "tensor_add", dh[0:1, o:o + 512],
                                  dh[0:1, o:o + 512],
                                  dh[0:1, 1024 + o:1024 + o + 512],
                                  ins=[dh], outs=[dh])
                    else:
                        pr.do("v", "tensor_copy", ep1[r0:r1, :], al[0:64, :],
                              ins=[al], outs=[ep1])
                        pr.do("v", "tensor_copy", dh[0:1, o:o + 512], al[64:65, :],
                              ins=[al], outs=[dh])
                pr.do("t", "matmul", pep[:], C.maskAf[:], dh[0:1, 0:512],
                      ins=[C.maskAf, dh], outs=[pep], start=True, stop=False,
                      skip_group_check=True)
                pr.do("t", "matmul", pep[:], C.maskBf[:], dh[0:1, 512:1024],
                      ins=[C.maskBf, dh], outs=[pep], start=False, stop=True,
                      skip_group_check=True)
                pr.do("v", "tensor_copy", rb_sb[:], pep[:], ins=[pep], outs=[rb_sb])
                pr.do("v", "reciprocal", rb2_sb[:], rb_sb[:], ins=[rb_sb],
                      outs=[rb2_sb])
                pr.do("v", "tensor_mul", ofin[:], ep1[:], rb2_sb[:],
                      ins=[ep1, rb2_sb], outs=[ofin])
                pr.dma("y", fm3(d["o_fm"], q0, 512, p, 1)[:, 0, :], ofin[:])


def emit_proj_ln1(pr, C, d):
    nc = pr.nc
    pr.fence()
    with (
        pr.guard(nc.sbuf_tensor("pl_wo", [128, NC * D], BF16, allow_name_mangling=True)) as wo,
        pr.guard(nc.sbuf_tensor("pl_bo", [128, NC], F32, allow_name_mangling=True)) as bo,
        pr.guard(nc.sbuf_tensor("pl_g", [128, NC], F32, allow_name_mangling=True)) as lg,
        pr.guard(nc.sbuf_tensor("pl_b", [128, NC], F32, allow_name_mangling=True)) as lb,
    ):
        pr.dma("g", wo[:].rearrange("p (c n) -> p c n", c=NC),
               d["L0_Wo"].rearrange("(c p) n -> p c n", p=128))
        pr.dma("y", bo[:], d["L0_bo"])
        pr.dma("y", lg[:], d["L0_ln1g"])
        pr.dma("y", lb[:], d["L0_ln1b"])
        with pr.scope() as _stx:
            ob0, ob1, xnb0, xnb1, sum_sb, nat0, nat1, stt0, stt1, sq0, sq1, xnn0, xnn1, x1t0, x1t1 = sb_alloc(nc, _stx, ("pl_ob", [128, NC * 512], BF16), ("pl_ob1", [128, NC * 512], BF16), ("pl_xnb", [128, NC * 512], BF16), ("pl_xnb1", [128, NC * 512], BF16), ("pl_sum", [128, NC * 512], F32), ("pl_nat", [128, D], F32), ("pl_nat1", [128, D], F32), ("pl_st", [128, 8], F32), ("pl_st1", [128, 8], F32), ("pl_sq", [128, D], F32), ("pl_sq1", [128, D], F32), ("pl_xnn", [128, D], BF16), ("pl_xnn1", [128, D], BF16), ("pl_x1", [128, 1024], BF16), ("pl_x11", [128, 1024], BF16))
            pp0, pp1, pn0, pn1, pf = ps_alloc(nc, _stx, ("pl_pp0", [128, 512], F32), ("pl_pp1", [128, 512], F32), ("pl_pn", [128, 1024], F32), ("pl_pn1", [128, 1024], F32), ("pl_pf", [128, 1024], BF16))
            pps = [pp0, pp1]
            for tb in range(T // 512):
                t0 = tb * 512
                ob, xnb = (ob0, xnb0) if tb % 2 == 0 else (ob1, xnb1)
                load_fm_block(pr, "y", ob, d["o_fm"], t0, 512)
                load_fm_block(pr, "g", xnb, d["xn_fm"], t0, 512)
                for do in range(NC):
                    pp = pps[do % 2]
                    for c in range(NC):
                        pr.do("t", "matmul", pp[:],
                              wo[:, c * D + do * 128: c * D + do * 128 + 128],
                              ob[:, c * 512:(c + 1) * 512], ins=[wo, ob], outs=[pp],
                              start=(c == 0), stop=(c == NC - 1))
                    pr.do("v", "scalar_tensor_tensor",
                          sum_sb[:, do * 512:(do + 1) * 512], pp[:], bo[:, do:do + 1],
                          xnb[:, do * 512:(do + 1) * 512], ALU.add, ALU.add,
                          ins=[pp, bo, xnb], outs=[sum_sb])
                for sub in range(4):
                    nat, stt, sq, xnn, x1t = (nat0, stt0, sq0, xnn0, x1t0) \
                        if sub % 2 == 0 else (nat1, stt1, sq1, xnn1, x1t1)
                    pn = pn0 if sub % 2 == 0 else pn1
                    for do in range(NC):
                        pr.do("t", "transpose", pn[:, do * 128:(do + 1) * 128],
                              sum_sb[:, do * 512 + sub * 128: do * 512 + sub * 128 + 128],
                              C.eye_f32[:], ins=[sum_sb, C.eye_f32], outs=[pn])
                    pr.do("v", "tensor_copy", nat[:], pn[:], ins=[pn], outs=[nat])
                    pr.do("v", "reduce_sum", stt[:, 0:1], nat[:], ins=[nat], outs=[stt],
                          axis=AX.X)
                    pr.do("s", "activation", sq[:], nat[:], AF.Square,
                          ins=[nat], outs=[sq, stt], accum_out=stt[:, 1:2])
                    pr.do("v", "tensor_scalar", stt[:, 2:4], stt[:, 0:2], 1.0 / D, None,
                          ALU.mult, ins=[stt, sq], outs=[stt])
                    pr.do("v", "tensor_mul", stt[:, 4:5], stt[:, 2:3], stt[:, 2:3],
                          ins=[stt], outs=[stt])
                    pr.do("v", "tensor_sub", stt[:, 4:5], stt[:, 3:4], stt[:, 4:5],
                          ins=[stt], outs=[stt])
                    pr.do("s", "activation", stt[:, 5:6], stt[:, 4:5], AF.Sqrt,
                          bias=C.epscol[:], ins=[stt, C.epscol], outs=[stt])
                    pr.do("v", "reciprocal", stt[:, 6:7], stt[:, 5:6], ins=[stt],
                          outs=[stt])
                    pr.do("v", "tensor_scalar", xnn[:], nat[:], stt[:, 2:3],
                          stt[:, 6:7], ALU.subtract, ALU.mult, ins=[nat, stt],
                          outs=[xnn])
                    for cc in range(NC):
                        pr.do("t", "transpose", pf[:, cc * 128:(cc + 1) * 128],
                              xnn[:, cc * 128:(cc + 1) * 128], C.eye_bf[:],
                              ins=[xnn, C.eye_bf], outs=[pf])
                    for cc in range(NC):
                        pr.do("v", "tensor_scalar", x1t[:, cc * 128:(cc + 1) * 128],
                              pf[:, cc * 128:(cc + 1) * 128], lg[:, cc:cc + 1],
                              lb[:, cc:cc + 1], ALU.mult, ALU.add,
                              ins=[pf, lg, lb], outs=[x1t])
                    pr.dma("y", fm3(d["x1_fm"], t0 + sub * 128, 128),
                           x1t[:].rearrange("p (c t) -> p c t", c=NC))


def emit_ffn_combine_l0(pr, C, d):
    nc = pr.nc
    pr.fence()
    NDC = 4
    with (
        pr.guard(nc.sbuf_tensor("ff_b1", [128, DFF // 128], F32, allow_name_mangling=True)) as b1,
        pr.guard(nc.sbuf_tensor("ff_b2", [128, NC], F32, allow_name_mangling=True)) as b2,
        pr.guard(nc.sbuf_tensor("ff_yacc", [128, NC * 2048], F32, allow_name_mangling=True)) as yacc,
    ):
        pr.dma("y", b1[:], d["L0_b1"])
        pr.dma("y", b2[:], d["L0_b2"])
        for th in range(2):
            h0 = th * 2048
            with (
                pr.guard(nc.sbuf_tensor("ff_w1", [128, NC * 1024], BF16, allow_name_mangling=True)) as w1c,
                pr.guard(nc.sbuf_tensor("ff_w2", [128, NC * 1024], BF16, allow_name_mangling=True)) as w2c,
                pr.guard(nc.sbuf_tensor("ff_x1", [128, NC * 512], BF16, allow_name_mangling=True)) as x1b0,
                pr.guard(nc.sbuf_tensor("ff_x1b", [128, NC * 512], BF16, allow_name_mangling=True)) as x1b1,
                pr.guard(nc.sbuf_tensor("ff_h", [128, NC * 512], BF16, allow_name_mangling=True)) as hb,
                pr.guard(nc.psum_tensor("ff_ph", [128, 512], F32, allow_name_mangling=True)) as ph_0,
                pr.guard(nc.psum_tensor("ff_ph1", [128, 512], F32, allow_name_mangling=True)) as ph_1,
                pr.guard(nc.psum_tensor("ff_py0", [128, 512], F32, allow_name_mangling=True)) as py0,
                pr.guard(nc.psum_tensor("ff_py1", [128, 512], F32, allow_name_mangling=True)) as py1,
            ):
                pys = [py0, py1]
                for dc in range(NDC):
                    pr.dma("g", w1c[:].rearrange("p (c n) -> p c n", c=NC),
                           d["L0_W1"].rearrange("(c p) n -> p c n", p=128)
                           [:, :, dc * 1024:(dc + 1) * 1024])
                    pr.dma("g", w2c[:].rearrange("p (c n) -> p c n", c=NC),
                           d["L0_W2"].rearrange("(c p) n -> p c n", p=128)
                           [:, dc * NC:(dc + 1) * NC, :])
                    for tb in range(4):
                        t0 = h0 + tb * 512
                        x1b = x1b0 if tb % 2 == 0 else x1b1
                        load_fm_block(pr, "y", x1b, d["x1_fm"], t0, 512)
                        for sub in range(NC):
                            ph = ph_0 if sub % 2 == 0 else ph_1
                            for c in range(NC):
                                pr.do("t", "matmul", ph[:],
                                      w1c[:, c * 1024 + sub * 128:
                                          c * 1024 + sub * 128 + 128],
                                      x1b[:, c * 512:(c + 1) * 512],
                                      ins=[w1c, x1b], outs=[ph],
                                      start=(c == 0), stop=(c == NC - 1))
                            pr.do("s", "activation", hb[:, sub * 512:(sub + 1) * 512],
                                  ph[:], AF.Relu, bias=b1[:, dc * NC + sub:
                                                          dc * NC + sub + 1],
                                  ins=[ph, b1], outs=[hb])
                        for do in range(NC):
                            py = pys[do % 2]
                            for sub in range(NC):
                                pr.do("t", "matmul", py[:],
                                      w2c[:, sub * 1024 + do * 128:
                                          sub * 1024 + do * 128 + 128],
                                      hb[:, sub * 512:(sub + 1) * 512],
                                      ins=[w2c, hb], outs=[py],
                                      start=(sub == 0), stop=(sub == NC - 1))
                            ysl = yacc[:, do * 2048 + tb * 512:
                                       do * 2048 + (tb + 1) * 512]
                            if dc == 0:
                                pr.do("v", "tensor_copy", ysl, py[:],
                                      ins=[py], outs=[yacc])
                            else:
                                pr.do("v", "tensor_tensor", ysl, ysl, py[:], ALU.add,
                                      ins=[py, yacc], outs=[yacc])
            pr.fence()
            with pr.scope() as _stx:
                x1b0, x1b1, sum_sb0, sum_sb1, nat0, nat1, stt0, stt1, sq0, sq1, s_nat0, s_nat1, gb, bb, gt0, gt1, xt0, xt1, outt0, outt1 = sb_alloc(nc, _stx, ("cb_x1", [128, NC * 512], BF16), ("cb_x12", [128, NC * 512], BF16), ("cb_sum", [128, NC * 128], F32), ("cb_sum1", [128, NC * 128], F32), ("cb_nat", [128, D], F32), ("cb_nat1", [128, D], F32), ("cb_st", [128, 8], F32), ("cb_st1", [128, 8], F32), ("cb_sq", [128, D], F32), ("cb_sq1", [128, D], F32), ("cb_s", [128, D], F32), ("cb_s1", [128, D], F32), ("cb_gb", [128, D], F32), ("cb_bb", [128, D], F32), ("cb_gt", [128, D], F32), ("cb_gt1", [128, D], F32), ("cb_xt", [128, D], F32), ("cb_xt1", [128, D], F32), ("cb_out", [128, D], F32), ("cb_out1", [128, D], F32))
                pn_0, pn_1, pb_ = ps_alloc(nc, _stx, ("cb_pn", [128, 1024], F32), ("cb_pn1", [128, 1024], F32), ("cb_pb", [128, 1024], F32))
                with pr.guard(nc.sbuf_tensor("cb_grow", [1, D], F32, allow_name_mangling=True)) as grow, \
                     pr.guard(nc.sbuf_tensor("cb_brow", [1, D], F32, allow_name_mangling=True)) as brow:
                    pr.dma("y", grow[:], d["L0_ln2g_row"])
                    pr.dma("y", brow[:], d["L0_ln2b_row"])
                    for hv in range(2):
                        pr.do("t", "matmul", pb_[:, hv * 512:(hv + 1) * 512],
                              C.ones_row_f32[:], grow[0:1, hv * 512:(hv + 1) * 512],
                              ins=[C.ones_row_f32, grow], outs=[pb_],
                              start=True, stop=True)
                    pr.do("v", "tensor_copy", gb[:], pb_[:], ins=[pb_], outs=[gb])
                    for hv in range(2):
                        pr.do("t", "matmul", pb_[:, hv * 512:(hv + 1) * 512],
                              C.ones_row_f32[:], brow[0:1, hv * 512:(hv + 1) * 512],
                              ins=[C.ones_row_f32, brow], outs=[pb_],
                              start=True, stop=True)
                    pr.do("v", "tensor_copy", bb[:], pb_[:], ins=[pb_], outs=[bb])
                for tc in range(16):
                    t0 = h0 + tc * 128
                    sum_sb, nat, stt, sq, s_nat = \
                        (sum_sb0, nat0, stt0, sq0, s_nat0) if tc % 2 == 0 \
                        else (sum_sb1, nat1, stt1, sq1, s_nat1)
                    pn = pn_0 if tc % 2 == 0 else pn_1
                    gt, xt, outt = (gt0, xt0, outt0) if tc % 2 == 0 else (gt1, xt1, outt1)
                    x1b = x1b0 if tc % 2 == 0 else x1b1
                    load_fm_block(pr, "y", x1b[:, 0:NC * 128], d["x1_fm"], t0, 128)
                    for do in range(NC):
                        pr.do("v", "scalar_tensor_tensor",
                              sum_sb[:, do * 128:(do + 1) * 128],
                              yacc[:, do * 2048 + tc * 128: do * 2048 + tc * 128 + 128],
                              b2[:, do:do + 1], x1b[:, do * 128:(do + 1) * 128],
                              ALU.add, ALU.add, ins=[yacc, b2, x1b], outs=[sum_sb])
                    for do in range(NC):
                        pr.do("t", "transpose", pn[:, do * 128:(do + 1) * 128],
                              sum_sb[:, do * 128:(do + 1) * 128], C.eye_f32[:],
                              ins=[sum_sb, C.eye_f32], outs=[pn])
                    pr.do("v", "tensor_copy", nat[:], pn[:], ins=[pn], outs=[nat])
                    pr.do("v", "reduce_sum", stt[:, 0:1], nat[:], ins=[nat],
                          outs=[stt], axis=AX.X)
                    pr.do("s", "activation", sq[:], nat[:], AF.Square,
                          ins=[nat], outs=[sq, stt], accum_out=stt[:, 1:2])
                    pr.do("v", "tensor_scalar", stt[:, 2:4], stt[:, 0:2], 1.0 / D,
                          None, ALU.mult, ins=[stt, sq], outs=[stt])
                    pr.do("v", "tensor_mul", stt[:, 4:5], stt[:, 2:3], stt[:, 2:3],
                          ins=[stt], outs=[stt])
                    pr.do("v", "tensor_sub", stt[:, 4:5], stt[:, 3:4], stt[:, 4:5],
                          ins=[stt], outs=[stt])
                    pr.do("s", "activation", stt[:, 5:6], stt[:, 4:5], AF.Sqrt,
                          bias=C.epscol[:], ins=[stt, C.epscol], outs=[stt])
                    pr.do("v", "reciprocal", stt[:, 6:7], stt[:, 5:6], ins=[stt],
                          outs=[stt])
                    pr.do("v", "tensor_scalar", s_nat[:], nat[:], stt[:, 2:3],
                          stt[:, 6:7], ALU.subtract, ALU.mult, ins=[nat, stt],
                          outs=[s_nat])
                    pr.do("v", "tensor_mul", s_nat[:], s_nat[:], gb[:],
                          ins=[s_nat, gb], outs=[s_nat])
                    pr.do("v", "tensor_add", s_nat[:], s_nat[:], bb[:],
                          ins=[s_nat, bb], outs=[s_nat])
                    pr.dma("y", gt[:], d["g_nat"][t0:t0 + 128, :])
                    pr.dma("y", xt[:], d["x0"][t0:t0 + 128, :])
                    pr.do("v", "tensor_mul", s_nat[:], s_nat[:], gt[:],
                          ins=[s_nat, gt], outs=[s_nat])
                    pr.do("v", "tensor_add", outt[:], xt[:], s_nat[:],
                          ins=[xt, s_nat], outs=[outt])
                    pr.dma("y", d["xres_nat"][t0:t0 + 128, :], outt[:])
            pr.fence()



# version 19 (final)
# speedup vs baseline: 1.1991x; 1.0426x over previous
import contextlib
import numpy as np
import ml_dtypes
import concourse.bass as bass
import concourse.mybir as mybir
from concourse.bass_utils import run_bass_kernel_spmd

F32 = mybir.dt.float32
BF16 = mybir.dt.bfloat16
AF = mybir.ActivationFunctionType
ALU = mybir.AluOpType
AX = mybir.AxisListType

T = 4096
D = 1024
NC = 8
H = 16
DH = 64
DFF = 4096
NEXP = 8
HID = 1408
NHC = 11
LN_EPS = 1e-5
MAGIC = 12582912.0
NEG = -3.0e38

SCRATCH_KIND = "Internal"
PHASES = None


class Prog:

    ENG = ("t", "v", "s", "g", "y")

    def __init__(self, nc):
        self.nc = nc
        self.eng = {"t": nc.tensor, "v": nc.vector, "s": nc.scalar,
                    "g": nc.gpsimd, "y": nc.sync}
        self.semkeys = ["t", "v", "s", "g", "yd", "gd", "sd"]
        self.sem = {k: nc.alloc_semaphore("sem_" + k) for k in self.semkeys}
        self.cnt = {k: 0 for k in self.semkeys}
        self.writer = {}
        self.readers = {}
        self.hwm = {e: {} for e in self.ENG}
        self.fence_vals = None

    def guard(self, cm):
        pr = self

        class _G:
            def __enter__(self):
                return cm.__enter__()

            def __exit__(self, *e):
                r = cm.__exit__(*e)
                pr.fence()
                return r

        return _G()

    def scope(self):
        pr = self

        class _S(contextlib.ExitStack):
            def __exit__(self, *exc):
                r = super().__exit__(*exc)
                pr.fence()
                return r

        return _S()

    def fence(self):
        self.fence_vals = dict(self.cnt)

    @staticmethod
    def key(ap):
        t = getattr(ap, "tensor", ap)
        return getattr(t, "name", str(t))

    def _wait(self, ek, semkey, val):
        if val <= 0:
            return
        if self.hwm[ek].get(semkey, 0) >= val:
            return
        self.eng[ek].wait_ge(self.sem[semkey], val)
        self.hwm[ek][semkey] = val

    def _dval(self, sk, v):
        return self.cnt[sk] if sk.endswith("d") else v

    def _edges(self, ek, ins, outs, skip_tensor_internal):
        if self.fence_vals is not None:
            for sk, v in self.fence_vals.items():
                self._wait(ek, sk, v)
        for ap in ins:
            w = self.writer.get(self.key(ap))
            if w and not (skip_tensor_internal and w[0] == "t"):
                self._wait(ek, w[0], self._dval(w[0], w[1]))
        for ap in outs:
            k = self.key(ap)
            for sk, v in self.readers.get(k, {}).items():
                if skip_tensor_internal and sk == "t":
                    continue
                self._wait(ek, sk, self._dval(sk, v))
            w = self.writer.get(k)
            if w and not (skip_tensor_internal and w[0] == "t"):
                self._wait(ek, w[0], self._dval(w[0], w[1]))

    def _update(self, semkey, val, ins, outs):
        for ap in outs:
            self.writer[self.key(ap)] = (semkey, val)
            self.readers[self.key(ap)] = {}
        for ap in ins:
            self.readers.setdefault(self.key(ap), {})[semkey] = val

    def do(self, ek, opname, *args, ins=(), outs=(), **kw):
        skip = ek == "t"
        self._edges(ek, ins, outs, skip)
        inst = getattr(self.eng[ek], opname)(*args, **kw)
        self.cnt[ek] += 1
        inst.then_inc(self.sem[ek], 1)
        v = self.cnt[ek]
        self._update(ek, v, ins, outs)
        return inst

    def dma(self, qk, out, in_, **kw):
        semkey = qk + "d"
        self._edges(qk, [in_], [out], False)
        inst = self.eng[qk].dma_start(out=out, in_=in_, **kw)
        self.cnt[semkey] += 16
        inst.then_inc(self.sem[semkey], 16)
        self._update(semkey, self.cnt[semkey], [in_], [out])
        return inst

    def barrier_on(self, ek, semkeys=None):
        for sk in (semkeys or self.semkeys):
            self._wait(ek, sk, self.cnt[sk])



def sb_alloc(nc, stack, *specs):
    return tuple(stack.enter_context(nc.sbuf_tensor(n, s, t, allow_name_mangling=True)) for n, s, t in specs)


def ps_alloc(nc, stack, *specs):
    return tuple(stack.enter_context(nc.psum_tensor(n, s, t, allow_name_mangling=True)) for n, s, t in specs)


def _bf(x):
    return np.ascontiguousarray(x, dtype=ml_dtypes.bfloat16)


def _f32(x):
    return np.ascontiguousarray(x, dtype=np.float32)


def _fmvec(v, nchunk=NC):
    return _f32(np.asarray(v, np.float32).reshape(nchunk, 128).T)


def host_constants():
    c = {}
    c["eye_bf"] = _bf(np.eye(128))
    c["eye_f32"] = _f32(np.eye(128))
    c["ones_row_bf"] = _bf(np.ones((1, 128)))
    c["ones_row_f32"] = _f32(np.ones((1, 128)))
    c["ones_col_bf"] = _bf(np.ones((128, 1)))
    c["pos_row"] = _f32(np.arange(T, dtype=np.float32)[None, :])
    c["upos"] = _f32(np.arange(128)[:, None] + 128 * np.arange(32)[None, :])
    ul = np.arange(128)[:, None]
    j = np.arange(512)[None, :]
    c["cdelta"] = _f32(np.stack([np.maximum(0.0, ul - j + d0) * 2.0
                                 for d0 in (0, 128, 256, 384)]))
    jj = np.arange(128)[None, :]
    c["tri"] = _f32(np.where(ul > jj, NEG, 0.0))
    hh = np.arange(16)[None, :]
    c["headmask"] = _f32(np.where(ul < hh, NEG, 0.0))
    mA = np.zeros((1, 128), np.float32); mA[0, :64] = 1.0
    mB = np.zeros((1, 128), np.float32); mB[0, 64:] = 1.0
    c["epscol"] = _f32(np.full((128, 1), 1e-5))
    c["negbig"] = _f32(np.full((128, 1), -1e30))
    c["maskA_bf"] = _bf(mA)
    c["maskB_bf"] = _bf(mB)
    c["maskA_f32"] = _f32(mA)
    c["maskB_f32"] = _f32(mB)
    return c


def prep_host(inputs):
    a = {k: np.asarray(v) for k, v in inputs.items()}
    h = {}
    for w in ("Wq", "Wk", "Wv", "Wo"):
        h["L0_" + w] = _f32(a["L0_attn_" + w])
    for b in ("bq", "bk", "bv", "bo"):
        h["L0_" + b] = _fmvec(a["L0_attn_" + b])
    h["L0_ds"] = _f32(a["L0_attn_dist_scale"].reshape(1, 1))
    h["L0_W1"] = _f32(a["L0_attn_W1"])
    h["L0_b1"] = _fmvec(a["L0_attn_b1"], DFF // 128)
    h["L0_W2"] = _f32(a["L0_attn_W2"])
    h["L0_b2"] = _fmvec(a["L0_attn_b2"])
    h["L0_ln1g"] = _fmvec(a["L0_attn_ln1_g"])
    h["L0_ln1b"] = _fmvec(a["L0_attn_ln1_b"])
    h["L0_ln2g"] = _fmvec(a["L0_attn_ln2_g"])
    h["L0_ln2b"] = _fmvec(a["L0_attn_ln2_b"])
    h["L0_ln2g_row"] = _f32(np.asarray(a["L0_attn_ln2_g"], np.float32).reshape(1, D))
    h["L0_ln2b_row"] = _f32(np.asarray(a["L0_attn_ln2_b"], np.float32).reshape(1, D))
    h["L0_fk"] = _f32(a["L0_ffn_kernel"])
    h["L0_fb"] = _fmvec(a["L0_ffn_bias"])
    h["L0_fa"] = _f32(a["L0_ffn_a"])
    h["L0_fbb"] = _f32(a["L0_ffn_b"])
    h["L1_wqkv"] = _f32(a["L1_attn_w_qkv"].T)
    h["L1_wout"] = _f32(a["L1_attn_w_out"].T)
    h["L1_gate"] = _f32(a["L1_ffn_gate_w"].T)
    h["L1_ebias"] = _f32(a["L1_ffn_expert_bias"].reshape(1, 8))
    h["L1_wup"] = _f32(np.ascontiguousarray(a["L1_ffn_w_up"].transpose(0, 2, 1)))
    h["L1_wdn"] = _f32(np.ascontiguousarray(a["L1_ffn_w_down"].transpose(0, 2, 1)))
    h["L1_swup"] = _f32(a["L1_ffn_shared_w_up"].T)
    h["L1_swdn"] = _f32(a["L1_ffn_shared_w_down"].T)
    h.update(host_constants())
    return h


HOST_SPECS = None


def declare_io(nc, host_shapes):
    d = {}
    d["x0"] = nc.dram_tensor("x0", [T, D], F32, kind="ExternalInput").ap()
    for name, (shape, dt) in host_shapes.items():
        d[name] = nc.dram_tensor(name, list(shape), dt, kind="ExternalInput").ap()
    d["out"] = nc.dram_tensor("out", [T, D], F32, kind="ExternalOutput").ap()
    sk = SCRATCH_KIND
    for name, shape, dt in [
        ("xn_fm", [D, T], BF16),
        ("g_nat", [T, D], F32),
        ("o_fm", [D, T], BF16),
        ("x1_fm", [D, T], BF16),
        ("xres_nat", [T, D], F32),
        ("w2t", [8, T], F32),
    ]:
        d[name] = nc.dram_tensor(name, shape, dt, kind=sk).ap()
    return d


def fm3(ap_dram, t0, tw, c0=0, ncs=NC):
    return ap_dram.rearrange("(c p) t -> p c t", p=128)[:, c0:c0 + ncs, t0:t0 + tw]


class Consts:

    def __init__(self, pr, d):
        nc = pr.nc
        self.eye_bf = nc.alloc_sbuf_tensor("c_eye_bf", [128, 128], BF16)
        self.eye_f32 = nc.alloc_sbuf_tensor("c_eye_f32", [128, 128], F32)
        self.ones_row_bf = nc.alloc_sbuf_tensor("c_ones_row", [1, 128], BF16)
        self.ones_row_f32 = nc.alloc_sbuf_tensor("c_ones_row_f32", [1, 128], F32)
        self.ones_col_bf = nc.alloc_sbuf_tensor("c_ones_col", [128, 1], BF16)
        self.pos_row = nc.alloc_sbuf_tensor("c_pos_row", [1, T], F32)
        self.upos = nc.alloc_sbuf_tensor("c_upos", [128, 32], F32)
        self.cdelta = nc.alloc_sbuf_tensor("c_cdelta", [128, 4 * 512], F32)
        self.tri = nc.alloc_sbuf_tensor("c_tri", [128, 128], F32)
        self.headmask = nc.alloc_sbuf_tensor("c_headmask", [128, 16], F32)
        self.epscol = nc.alloc_sbuf_tensor("c_epscol", [128, 1], F32)
        self.negbig = nc.alloc_sbuf_tensor("c_negbig", [128, 1], F32)
        self.maskA = nc.alloc_sbuf_tensor("c_maskA", [1, 128], BF16)
        self.maskB = nc.alloc_sbuf_tensor("c_maskB", [1, 128], BF16)
        self.maskAf = nc.alloc_sbuf_tensor("c_maskAf", [1, 128], F32)
        self.maskBf = nc.alloc_sbuf_tensor("c_maskBf", [1, 128], F32)
        pr.dma("y", self.eye_bf[:], d["eye_bf"])
        pr.dma("y", self.eye_f32[:], d["eye_f32"])
        pr.dma("y", self.ones_row_bf[:], d["ones_row_bf"])
        pr.dma("y", self.ones_row_f32[:], d["ones_row_f32"])
        pr.dma("y", self.ones_col_bf[:], d["ones_col_bf"])
        pr.dma("y", self.pos_row[:], d["pos_row"])
        pr.dma("y", self.upos[:], d["upos"])
        for k in range(4):
            pr.dma("y", self.cdelta[:, k * 512:(k + 1) * 512], d["cdelta"][k])
        pr.dma("y", self.tri[:], d["tri"])
        pr.dma("y", self.headmask[:], d["headmask"])
        pr.dma("y", self.epscol[:], d["epscol"])
        pr.dma("y", self.negbig[:], d["negbig"])
        pr.dma("y", self.maskA[:], d["maskA_bf"])
        pr.dma("y", self.maskB[:], d["maskB_bf"])
        pr.dma("y", self.maskAf[:], d["maskA_f32"])
        pr.dma("y", self.maskBf[:], d["maskB_f32"])


def emit_ln_to_fm(pr, C, src_nat_dram, dst_fm_dram, gain_sb=None, bias_sb=None,
                  src2_info=None, tag=""):
    pr.fence()
    """dst_fm = LN(src_nat [+ src2]) transposed to FM (bf16), plain LN with
    optional per-feature affine applied in FM after the transpose.

    gain_sb/bias_sb: SBUF [128, NC] FM vectors or None.
    src2_info: unused for now.
    """
    nc = pr.nc
    with (
        pr.guard(nc.sbuf_tensor(f"ln{tag}_x0", [128, D], F32, allow_name_mangling=True)) as xt0,
        pr.guard(nc.sbuf_tensor(f"ln{tag}_x1", [128, D], F32, allow_name_mangling=True)) as xt1,
        pr.guard(nc.sbuf_tensor(f"ln{tag}_xn0", [128, D], BF16, allow_name_mangling=True)) as xn0,
        pr.guard(nc.sbuf_tensor(f"ln{tag}_xn1", [128, D], BF16, allow_name_mangling=True)) as xn1,
        pr.guard(nc.sbuf_tensor(f"ln{tag}_st", [128, 8], F32, allow_name_mangling=True)) as stt0,
        pr.guard(nc.sbuf_tensor(f"ln{tag}_st1", [128, 8], F32, allow_name_mangling=True)) as stt1,
        pr.guard(nc.sbuf_tensor(f"ln{tag}_sq", [128, D], F32, allow_name_mangling=True)) as sq0,
        pr.guard(nc.sbuf_tensor(f"ln{tag}_sq1", [128, D], F32, allow_name_mangling=True)) as sq1,
        pr.guard(nc.sbuf_tensor(f"ln{tag}_o0", [128, 1024], BF16, allow_name_mangling=True)) as o0,
        pr.guard(nc.sbuf_tensor(f"ln{tag}_o1", [128, 1024], BF16, allow_name_mangling=True)) as o1,
        pr.guard(nc.psum_tensor(f"ln{tag}_p0", [128, 1024], BF16, allow_name_mangling=True)) as p0,
        pr.guard(nc.psum_tensor(f"ln{tag}_p1", [128, 1024], BF16, allow_name_mangling=True)) as p1,
    ):
        xts, xns, os_, ps = [xt0, xt1], [xn0, xn1], [o0, o1], [p0, p1]
        for i in range(T // 128):
            t0 = i * 128
            xt, xn, ot, pt = xts[i % 2], xns[i % 2], os_[i % 2], ps[i % 2]
            stt, sq = (stt0, sq0) if i % 2 == 0 else (stt1, sq1)
            pr.dma("y", xt[:], src_nat_dram[t0:t0 + 128, :])
            pr.do("v", "reduce_sum", stt[:, 0:1], xt[:], ins=[xt], outs=[stt], axis=AX.X)
            pr.do("s", "activation", sq[:], xt[:], AF.Square, ins=[xt], outs=[sq, stt],
                  accum_out=stt[:, 1:2])
            pr.do("v", "tensor_scalar", stt[:, 2:4], stt[:, 0:2], 1.0 / D, None,
                  ALU.mult, ins=[stt, sq], outs=[stt])
            pr.do("v", "tensor_mul", stt[:, 4:5], stt[:, 2:3], stt[:, 2:3],
                  ins=[stt], outs=[stt])
            pr.do("v", "tensor_sub", stt[:, 4:5], stt[:, 3:4], stt[:, 4:5],
                  ins=[stt], outs=[stt])
            pr.do("s", "activation", stt[:, 5:6], stt[:, 4:5], AF.Sqrt,
                  bias=C.epscol[:], ins=[stt, C.epscol], outs=[stt])
            pr.do("v", "reciprocal", stt[:, 6:7], stt[:, 5:6], ins=[stt], outs=[stt])
            pr.do("v", "tensor_scalar", xn[:], xt[:], stt[:, 2:3], stt[:, 6:7],
                  ALU.subtract, ALU.mult, ins=[xt, stt], outs=[xn])
            for c in range(NC):
                pr.do("t", "transpose", pt[:, c * 128:(c + 1) * 128],
                      xn[:, c * 128:(c + 1) * 128], C.eye_bf[:],
                      ins=[xn, C.eye_bf], outs=[pt])
            if gain_sb is not None:
                for c in range(NC):
                    pr.do("v", "tensor_scalar", ot[:, c * 128:(c + 1) * 128],
                          pt[:, c * 128:(c + 1) * 128],
                          gain_sb[:, c:c + 1], bias_sb[:, c:c + 1],
                          ALU.mult, ALU.add, ins=[pt, gain_sb, bias_sb], outs=[ot])
            else:
                pr.do("v", "tensor_copy", ot[:], pt[:], ins=[pt], outs=[ot])
            pr.dma("y", fm3(dst_fm_dram, t0, 128),
                   ot[:].rearrange("p (c t) -> p c t", c=NC))
    return


def load_fm_block(pr, qk, sb, fm_dram, t0, tw, ncs=NC, c0=0):
    pr.dma(qk, sb[:].rearrange("p (c t) -> p c t", c=ncs), fm3(fm_dram, t0, tw, c0, ncs))


def emit_quant(pr, C, d, wq_sb):
    pr.fence()
    """Quantize L0_ffn kernel int4-style into wq_sb [128, NC*1024] bf16."""
    nc = pr.nc
    with (
        pr.guard(nc.sbuf_tensor("qk_w", [128, NC * 1024], F32, allow_name_mangling=True)) as wk,
        pr.guard(nc.sbuf_tensor("qk_am", [128, 9], F32, allow_name_mangling=True)) as am,
        pr.guard(nc.sbuf_tensor("qk_amr", [1, 128], F32, allow_name_mangling=True)) as amr,
        pr.guard(nc.sbuf_tensor("qk_sc", [1, 4], F32, allow_name_mangling=True)) as sc,
        pr.guard(nc.sbuf_tensor("qk_scb", [128, 2], F32, allow_name_mangling=True)) as scb,
        pr.guard(nc.sbuf_tensor("qk_q", [128, 1024], F32, allow_name_mangling=True)) as qq,
        pr.guard(nc.psum_tensor("qk_p", [128, 512], F32, allow_name_mangling=True)) as pp,
        pr.guard(nc.psum_tensor("qk_pb", [128, 512], F32, allow_name_mangling=True)) as pb,
    ):
        load_fm_block(pr, "y", wk, d["L0_fk"], 0, 1024)
        for c in range(NC):
            pr.do("v", "reduce_max", am[:, c:c + 1], wk[:, c * 1024:(c + 1) * 1024],
                  ins=[wk], outs=[am], axis=AX.X, apply_absolute_value=True)
        pr.do("v", "reduce_max", am[:, 8:9], am[:, 0:8], ins=[am], outs=[am],
              axis=AX.X, apply_absolute_value=True)
        pr.do("t", "transpose", pp[0:1, 0:128], am[:, 8:9],
              C.eye_f32[:], ins=[am, C.eye_f32], outs=[pp])
        pr.do("v", "reduce_max", sc[:, 0:1], pp[0:1, 0:128], ins=[pp], outs=[sc],
              axis=AX.X, apply_absolute_value=True)
        pr.do("v", "tensor_scalar_mul", sc[:, 1:2], sc[:, 0:1], 1.0 / 7.0,
              ins=[sc], outs=[sc])
        pr.do("v", "reciprocal", sc[:, 2:3], sc[:, 1:2], ins=[sc], outs=[sc])
        pr.do("t", "matmul", pb[:, 0:2], C.ones_row_f32[:], sc[:, 1:3],
              ins=[C.ones_row_f32, sc], outs=[pb], start=True, stop=True)
        pr.do("v", "tensor_copy", scb[:], pb[:, 0:2], ins=[pb], outs=[scb])
        for c in range(NC):
            w_c = wk[:, c * 1024:(c + 1) * 1024]
            pr.do("v", "tensor_scalar", qq[:], w_c, scb[:, 1:2], MAGIC,
                  ALU.mult, ALU.add, ins=[wk, scb], outs=[qq])
            pr.do("v", "tensor_scalar", wq_sb[:, c * 1024:(c + 1) * 1024], qq[:],
                  MAGIC, scb[:, 0:1], ALU.subtract, ALU.mult,
                  ins=[qq, scb], outs=[wq_sb])


def emit_gffn_l0(pr, C, d):
    pr.fence()
    """g = silu(xn @ quant(kernel) + bias + (xn@a)@b), FM f32 -> g_fm dram."""
    nc = pr.nc
    with (
        pr.guard(nc.sbuf_tensor("gf_wq", [128, NC * 1024], BF16, allow_name_mangling=True)) as wq,
        pr.guard(nc.sbuf_tensor("gf_a", [128, NC * 16], BF16, allow_name_mangling=True)) as a_sb,
        pr.guard(nc.sbuf_tensor("gf_b", [16, 1024], BF16, allow_name_mangling=True)) as b_sb,
        pr.guard(nc.sbuf_tensor("gf_bias", [128, NC], F32, allow_name_mangling=True)) as bias_sb,
    ):
        emit_quant(pr, C, d, wq)
        pr.fence()
        pr.dma("g", a_sb[:].rearrange("p (c n) -> p c n", c=NC),
               d["L0_fa"].rearrange("(c p) n -> p c n", p=128))
        pr.dma("g", b_sb[:], d["L0_fbb"])
        pr.dma("y", bias_sb[:], d["L0_fb"])
        with (
            pr.guard(nc.sbuf_tensor("gf_xn0", [128, NC * 512], BF16, allow_name_mangling=True)) as xnb0,
            pr.guard(nc.sbuf_tensor("gf_xn1", [128, NC * 512], BF16, allow_name_mangling=True)) as xnb1,
            pr.guard(nc.sbuf_tensor("gf_l1", [16, 512], BF16, allow_name_mangling=True)) as l1,
            pr.guard(nc.sbuf_tensor("gf_gfm", [128, NC * 512], F32, allow_name_mangling=True)) as gfm,
            pr.guard(nc.sbuf_tensor("gf_gn0", [128, 1024], F32, allow_name_mangling=True)) as gn0,
            pr.guard(nc.sbuf_tensor("gf_gn1", [128, 1024], F32, allow_name_mangling=True)) as gn1,
            pr.guard(nc.psum_tensor("gf_pl", [16, 512], F32, allow_name_mangling=True)) as pl,
            pr.guard(nc.psum_tensor("gf_p0", [128, 512], F32, allow_name_mangling=True)) as pp0,
            pr.guard(nc.psum_tensor("gf_p1", [128, 512], F32, allow_name_mangling=True)) as pp1,
            pr.guard(nc.psum_tensor("gf_pn0", [128, 1024], F32, allow_name_mangling=True)) as pn0,
            pr.guard(nc.psum_tensor("gf_pn1", [128, 1024], F32, allow_name_mangling=True)) as pn1,
        ):
            xnbs, pps = [xnb0, xnb1], [pp0, pp1]
            gns, pns = [gn0, gn1], [pn0, pn1]
            for it in range(T // 512):
                t0 = it * 512
                xnb = xnbs[it % 2]
                load_fm_block(pr, "y", xnb, d["xn_fm"], t0, 512)
                for c in range(NC):
                    pr.do("t", "matmul", pl[:], a_sb[:, c * 16:(c + 1) * 16],
                          xnb[:, c * 512:(c + 1) * 512], ins=[a_sb, xnb], outs=[pl],
                          start=(c == 0), stop=(c == NC - 1))
                pr.do("v", "tensor_copy", l1[:], pl[:], ins=[pl], outs=[l1])
                for do in range(NC):
                    pp = pps[do % 2]
                    for c in range(NC):
                        pr.do("t", "matmul", pp[:],
                              wq[:, c * 1024 + do * 128: c * 1024 + do * 128 + 128],
                              xnb[:, c * 512:(c + 1) * 512], ins=[wq, xnb], outs=[pp],
                              start=(c == 0), stop=False)
                    pr.do("t", "matmul", pp[:], b_sb[:, do * 128:(do + 1) * 128],
                          l1[:], ins=[b_sb, l1], outs=[pp], start=False, stop=True)
                    pr.do("s", "activation", gfm[:, do * 512:(do + 1) * 512], pp[:],
                          AF.Silu, bias=bias_sb[:, do:do + 1], ins=[pp, bias_sb],
                          outs=[gfm])
                for sub in range(4):
                    pn = pns[sub % 2]
                    for do in range(NC):
                        pr.do("t", "transpose", pn[:, do * 128:(do + 1) * 128],
                              gfm[:, do * 512 + sub * 128: do * 512 + sub * 128 + 128],
                              C.eye_f32[:], ins=[gfm, C.eye_f32], outs=[pn])
                    gn = gns[sub % 2]
                    pr.do("v", "tensor_copy", gn[:], pn[:], ins=[pn], outs=[gn])
                    pr.dma("g", d["g_nat"][t0 + sub * 128: t0 + sub * 128 + 128, :], gn[:])



def emit_moe(pr, C, d):
    nc = pr.nc
    pr.fence()
    with pr.scope() as _st:
        with pr.scope() as _sg:
            gsb, ebb, xnt, scn, bn, b2n, m1, msk, msk2, w8 = sb_alloc(
                nc, _sg,
                ("mo_gsb", [128, NC * 8], BF16), ("mo_ebb", [128, 8], F32),
                ("mo_xnt", [128, NC * 128], BF16), ("mo_scn", [128, 8], F32),
                ("mo_bn", [128, 8], F32), ("mo_b2n", [128, 8], F32),
                ("mo_m1", [128, 2], F32), ("mo_msk", [128, 8], F32),
                ("mo_msk2", [128, 8], F32), ("mo_w8", [8, 128], F32))
            (ebr,) = sb_alloc(nc, _sg, ("mo_ebr", [1, 8], F32))
            pg, ptr, peb = ps_alloc(nc, _sg, ("mo_pg", [128, 512], F32),
                                    ("mo_ptr", [8, 512], F32),
                                    ("mo_peb", [128, 512], F32))
            pr.dma("g", gsb[:].rearrange("p (c n) -> p c n", c=NC),
                   d["L1_gate"].rearrange("(c p) n -> p c n", p=128))
            pr.dma("y", ebr[:], d["L1_ebias"])
            pr.do("t", "matmul", peb[:, 0:8], C.ones_row_f32[:], ebr[:],
                  ins=[C.ones_row_f32, ebr], outs=[peb], start=True, stop=True)
            pr.do("v", "tensor_copy", ebb[:], peb[:, 0:8], ins=[peb], outs=[ebb])
            for tc in range(T // 128):
                t0 = tc * 128
                load_fm_block(pr, "y", xnt, d["xn_fm"], t0, 128)
                for c in range(NC):
                    pr.do("t", "matmul", pg[:, 0:8], xnt[:, c * 128:(c + 1) * 128],
                          gsb[:, c * 8:(c + 1) * 8], ins=[xnt, gsb], outs=[pg],
                          start=(c == 0), stop=(c == NC - 1))
                pr.do("s", "activation", scn[:], pg[:, 0:8], AF.Sigmoid,
                      ins=[pg], outs=[scn])
                pr.do("v", "tensor_add", bn[:], scn[:], ebb[:],
                      ins=[scn, ebb], outs=[bn])
                pr.do("v", "reduce_max", m1[:, 0:1], bn[:], ins=[bn], outs=[m1],
                      axis=AX.X)
                pr.do("v", "tensor_scalar", msk[:], bn[:], m1[:, 0:1], None,
                      ALU.is_ge, ins=[bn, m1], outs=[msk])
                pr.do("v", "scalar_tensor_tensor", b2n[:], msk[:], C.negbig[:],
                      bn[:], ALU.mult, ALU.add, ins=[msk, C.negbig, bn], outs=[b2n])
                pr.do("v", "reduce_max", m1[:, 1:2], b2n[:], ins=[b2n], outs=[m1],
                      axis=AX.X)
                pr.do("v", "tensor_scalar", msk2[:], b2n[:], m1[:, 1:2], None,
                      ALU.is_ge, ins=[b2n, m1], outs=[msk2])
                pr.do("v", "tensor_add", msk[:], msk[:], msk2[:],
                      ins=[msk, msk2], outs=[msk])
                pr.do("v", "tensor_mul", msk[:], msk[:], scn[:],
                      ins=[msk, scn], outs=[msk])
                pr.do("v", "tensor_mul", msk[:], msk[:], msk[:],
                      ins=[msk], outs=[msk])
                pr.do("t", "transpose", ptr[0:8, 0:128], msk[:], C.eye_f32[:],
                      ins=[msk, C.eye_f32], outs=[ptr])
                pr.do("v", "tensor_copy", w8[:], ptr[0:8, 0:128],
                      ins=[ptr], outs=[w8])
                pr.dma("g", d["w2t"][:, t0:t0 + 128], w8[:])
        pr.fence()
        with pr.scope() as _se:
            (macc,) = sb_alloc(nc, _se, ("mo_macc", [128, NC * 2048], F32))
            for th in range(2):
                h0 = th * 2048
                with pr.scope() as _se2:
                    wup0, wup1, wdn, xnb0, xnb1, h2, rr0, rr1, tt_, wb, w2r = sb_alloc(
                        nc, _se2,
                        ("mo_wup", [128, NC * HID], BF16),
                        ("mo_wup1", [128, NC * HID], BF16),
                        ("mo_wdn", [128, NHC * 1024], BF16),
                        ("mo_xnb0", [128, NC * 512], BF16),
                        ("mo_xnb1", [128, NC * 512], BF16),
                        ("mo_h2", [128, NHC * 512], BF16),
                        ("mo_rr0", [128, 512], BF16), ("mo_rr1", [128, 512], BF16),
                        ("mo_tt", [128, 512], BF16), ("mo_wb", [128, 512], BF16),
                        ("mo_w2r", [1, 512], F32))
                    pwb, ph0, ph1, py0, py1 = ps_alloc(
                        nc, _se2, ("mo_pwb", [128, 512], F32),
                        ("mo_ph0", [128, 512], F32), ("mo_ph1", [128, 512], F32),
                        ("mo_py0", [128, 512], F32), ("mo_py1", [128, 512], F32))
                    xnbs, phs, pys, rrs = [xnb0, xnb1], [ph0, ph1], [py0, py1], [rr0, rr1]
                    for e in range(NEXP + 1):
                        shared = e == NEXP
                        wup = wup0 if e % 2 == 0 else wup1
                        if shared:
                            up_src = d["L1_swup"].rearrange("(c p) n -> p c n", p=128)
                            dn_src = d["L1_swdn"].rearrange("(c p) n -> p c n", p=128)
                        else:
                            up_src = d["L1_wup"][e].rearrange("(c p) n -> p c n", p=128)
                            dn_src = d["L1_wdn"][e].rearrange("(c p) n -> p c n", p=128)
                        pr.dma("g", wup[:].rearrange("p (c n) -> p c n", c=NC), up_src)
                        pr.dma("g", wdn[:].rearrange("p (c n) -> p c n", c=NHC), dn_src)
                        for tb in range(4):
                            t0 = h0 + tb * 512
                            xnb = xnbs[tb % 2]
                            load_fm_block(pr, "y", xnb, d["xn_fm"], t0, 512)
                            if not shared:
                                pr.dma("y", w2r[:], d["w2t"][e:e + 1, t0:t0 + 512])
                                pr.do("t", "matmul", pwb[:], C.ones_row_f32[:],
                                      w2r[:], ins=[C.ones_row_f32, w2r], outs=[pwb],
                                      start=True, stop=True)
                                pr.do("v", "tensor_copy", wb[:], pwb[:],
                                      ins=[pwb], outs=[wb])
                            for ns in range(NHC):
                                ph = phs[ns % 2]
                                rr = rrs[ns % 2]
                                for c in range(NC):
                                    pr.do("t", "matmul", ph[:],
                                          wup[:, c * HID + ns * 128:
                                              c * HID + ns * 128 + 128],
                                          xnb[:, c * 512:(c + 1) * 512],
                                          ins=[wup, xnb], outs=[ph],
                                          start=(c == 0), stop=(c == NC - 1))
                                pr.do("s", "activation", rr[:], ph[:], AF.Relu,
                                      ins=[ph], outs=[rr])
                                h2s = h2[:, ns * 512:(ns + 1) * 512]
                                if shared:
                                    pr.do("v", "tensor_mul", h2s, rr[:], rr[:],
                                          ins=[rr], outs=[h2])
                                else:
                                    pr.do("v", "tensor_mul", tt_[:], rr[:], wb[:],
                                          ins=[rr, wb], outs=[tt_])
                                    pr.do("v", "tensor_mul", h2s, tt_[:], rr[:],
                                          ins=[tt_, rr], outs=[h2])
                            for do in range(NC):
                                py = pys[do % 2]
                                for ns in range(NHC):
                                    pr.do("t", "matmul", py[:],
                                          wdn[:, ns * 1024 + do * 128:
                                              ns * 1024 + do * 128 + 128],
                                          h2[:, ns * 512:(ns + 1) * 512],
                                          ins=[wdn, h2], outs=[py],
                                          start=(ns == 0), stop=(ns == NHC - 1))
                                msl = macc[:, do * 2048 + tb * 512:
                                           do * 2048 + (tb + 1) * 512]
                                if e == 0:
                                    pr.do("v", "tensor_copy", msl, py[:],
                                          ins=[py], outs=[macc])
                                else:
                                    pr.do("v", "tensor_tensor", msl, msl, py[:],
                                          ALU.add, ins=[py, macc], outs=[macc])
                pr.fence()
                with pr.scope() as _so:
                    (gn0, gn1) = sb_alloc(nc, _so, ("mo_gn0", [128, D], F32),
                                          ("mo_gn1", [128, D], F32))
                    pn0, pn1 = ps_alloc(nc, _so, ("mo_pn0", [128, 1024], F32),
                                        ("mo_pn1", [128, 1024], F32))
                    for do in range(NC):
                        sl = macc[:, do * 2048:(do + 1) * 2048]
                        pr.do("s", "activation", sl, sl, AF.Silu,
                              ins=[macc], outs=[macc])
                    for tc in range(16):
                        t0 = h0 + tc * 128
                        pn, gn = (pn0, gn0) if tc % 2 == 0 else (pn1, gn1)
                        for do in range(NC):
                            pr.do("t", "transpose", pn[:, do * 128:(do + 1) * 128],
                                  macc[:, do * 2048 + tc * 128:
                                       do * 2048 + tc * 128 + 128],
                                  C.eye_f32[:], ins=[macc, C.eye_f32], outs=[pn])
                        pr.do("v", "tensor_copy", gn[:], pn[:], ins=[pn], outs=[gn])
                        pr.dma("g", d["g_nat"][t0:t0 + 128, :], gn[:])
                pr.fence()


def emit_out_l1(pr, C, d):
    nc = pr.nc
    pr.fence()
    with pr.scope() as _st:
        (wo,) = sb_alloc(nc, _st, ("ol_wo", [128, NC * D], BF16))
        pr.dma("g", wo[:].rearrange("p (c n) -> p c n", c=NC),
               d["L1_wout"].rearrange("(c p) n -> p c n", p=128))
        ob0, ob1, s2fm0, s2fm1, s2n0, s2n1, gt0, gt1, xt0, xt1, out0, out1 = sb_alloc(
            nc, _st,
            ("ol_ob", [128, NC * 512], BF16), ("ol_ob1", [128, NC * 512], BF16),
            ("ol_s2fm", [128, NC * 512], F32), ("ol_s2fm1", [128, NC * 512], F32),
            ("ol_s2n", [128, D], F32), ("ol_s2n1", [128, D], F32),
            ("ol_gt", [128, D], F32), ("ol_gt1", [128, D], F32),
            ("ol_xt", [128, D], F32), ("ol_xt1", [128, D], F32),
            ("ol_out", [128, D], F32), ("ol_out1", [128, D], F32))
        pp0, pp1, pn_0, pn_1 = ps_alloc(nc, _st, ("ol_pp0", [128, 512], F32),
                                ("ol_pp1", [128, 512], F32),
                                ("ol_pn", [128, 1024], F32),
                                ("ol_pn1", [128, 1024], F32))
        pps = [pp0, pp1]
        for tb in range(T // 512):
            t0 = tb * 512
            ob, s2fm = (ob0, s2fm0) if tb % 2 == 0 else (ob1, s2fm1)
            load_fm_block(pr, "y", ob, d["o_fm"], t0, 512)
            for do in range(NC):
                pp = pps[do % 2]
                for c in range(NC):
                    pr.do("t", "matmul", pp[:],
                          wo[:, c * D + do * 128: c * D + do * 128 + 128],
                          ob[:, c * 512:(c + 1) * 512], ins=[wo, ob], outs=[pp],
                          start=(c == 0), stop=(c == NC - 1))
                pr.do("v", "tensor_copy", s2fm[:, do * 512:(do + 1) * 512], pp[:],
                      ins=[pp], outs=[s2fm])
            for sub in range(4):
                pn = pn_0 if sub % 2 == 0 else pn_1
                s2n, gt, xt, outt = (s2n0, gt0, xt0, out0) if sub % 2 == 0 \
                    else (s2n1, gt1, xt1, out1)
                for do in range(NC):
                    pr.do("t", "transpose", pn[:, do * 128:(do + 1) * 128],
                          s2fm[:, do * 512 + sub * 128: do * 512 + sub * 128 + 128],
                          C.eye_f32[:], ins=[s2fm, C.eye_f32], outs=[pn])
                pr.do("v", "tensor_copy", s2n[:], pn[:], ins=[pn], outs=[s2n])
                tt = t0 + sub * 128
                pr.dma("y", gt[:], d["g_nat"][tt:tt + 128, :])
                pr.dma("y", xt[:], d["xres_nat"][tt:tt + 128, :])
                pr.do("v", "tensor_mul", s2n[:], s2n[:], gt[:],
                      ins=[s2n, gt], outs=[s2n])
                pr.do("v", "tensor_add", outt[:], xt[:], s2n[:],
                      ins=[xt, s2n], outs=[outt])
                pr.dma("y", d["out"][tt:tt + 128, :], outt[:])

def build_program(host, phases=None):
    phases = phases if phases is not None else\
        ["p0", "p1", "p2", "p3", "p4", "p5", "p6", "p7", "p8"]
    nc = bass.Bass("TRN2", target_bir_lowering=False)
    host_shapes = {k: (v.shape, BF16 if v.dtype == ml_dtypes.bfloat16 else F32)
                   for k, v in host.items()}
    d = declare_io(nc, host_shapes)
    pr = Prog(nc)
    C = Consts(pr, d)
    if "p0" in phases:
        emit_ln_to_fm(pr, C, d["x0"], d["xn_fm"], tag="0")
    if "p1" in phases:
        emit_gffn_l0(pr, C, d)
    if "p2" in phases:
        emit_attention(pr, C, d, 0)
    if "p3" in phases:
        emit_proj_ln1(pr, C, d)
    if "p4" in phases:
        emit_ffn_combine_l0(pr, C, d)
    if "p5" in phases:
        emit_ln_to_fm(pr, C, d["xres_nat"], d["xn_fm"], tag="2")
    if "p6" in phases:
        emit_moe(pr, C, d)
    if "p7" in phases:
        emit_attention(pr, C, d, 1)
    if "p8" in phases:
        emit_out_l1(pr, C, d)
    for ek in ("y", "g", "t", "v", "s"):
        pr.barrier_on(ek)
    return nc


def kernel(**inputs):
    host = prep_host(inputs)
    nc = build_program(host, PHASES)
    toks = np.asarray(inputs["L0_attn_tokens"])
    in_maps = []
    for i in range(8):
        m = {"x0": _f32(toks[i])}
        m.update(host)
        in_maps.append(m)
    res = run_bass_kernel_spmd(nc, in_maps, core_ids=list(range(8)))
    return np.stack([np.asarray(res.results[i]["out"]) for i in range(8)])


def emit_attention(pr, C, d, layer):
    nc = pr.nc
    pr.fence()
    NQG = T // 512
    with (
        pr.guard(nc.sbuf_tensor("at_xn", [128, NC * T], BF16, allow_name_mangling=True)) as xn_sb,
        pr.guard(nc.sbuf_tensor("at_frow", [1, 2 * T], BF16, allow_name_mangling=True)) as frow,
        pr.guard(nc.sbuf_tensor("at_eu", [128, 64], F32, allow_name_mangling=True)) as eu,
        pr.guard(nc.sbuf_tensor("at_fb", [128, 16 * 512], BF16, allow_name_mangling=True)) as fb,
        pr.guard(nc.sbuf_tensor("at_dsv", [1, 4], F32, allow_name_mangling=True)) as dsv,
        pr.guard(nc.sbuf_tensor("at_dsb", [128, 2], F32, allow_name_mangling=True)) as dsb,
        pr.guard(nc.sbuf_tensor("at_bq8", [128, NC], F32, allow_name_mangling=True)) as bq8,
        pr.guard(nc.sbuf_tensor("at_bk", [128, NC], F32, allow_name_mangling=True)) as bk_sb,
        pr.guard(nc.sbuf_tensor("at_bv", [128, NC], F32, allow_name_mangling=True)) as bv_sb,
    ):
        load_fm_block(pr, "y", xn_sb, d["xn_fm"], 0, T)
        if layer == 0:
            pr.dma("y", dsv[:, 0:1], d["L0_ds"])
            pr.dma("y", bq8[:], d["L0_bq"])
            pr.dma("y", bk_sb[:], d["L0_bk"])
            pr.dma("y", bv_sb[:], d["L0_bv"])
            pr.do("v", "tensor_scalar_mul", bq8[:], bq8[:], 0.125,
                  ins=[bq8], outs=[bq8])
            pr.do("v", "tensor_scalar_mul", dsv[:, 1:2], dsv[:, 0:1], 0.125,
                  ins=[dsv], outs=[dsv])
            pr.do("v", "tensor_scalar_mul", dsv[:, 2:3], dsv[:, 1:2], -1.0,
                  ins=[dsv], outs=[dsv])
            with pr.guard(nc.psum_tensor("at_pds", [128, 512], F32, allow_name_mangling=True)) as pds:
                pr.do("t", "matmul", pds[:, 0:2], C.ones_row_f32[:], dsv[:, 1:3],
                      ins=[C.ones_row_f32, dsv], outs=[pds], start=True, stop=True)
                pr.do("v", "tensor_copy", dsb[:], pds[:, 0:2], ins=[pds], outs=[dsb])
            pr.do("v", "tensor_scalar_mul", eu[:, 0:32], C.upos[:], dsb[:, 0:1],
                  ins=[C.upos, dsb], outs=[eu])
            pr.do("v", "tensor_scalar_mul", eu[:, 32:64], C.upos[:], dsb[:, 1:2],
                  ins=[C.upos, dsb], outs=[eu])
            pr.do("s", "activation", eu[:], eu[:], AF.Exp, ins=[eu], outs=[eu])
            pr.do("v", "tensor_single_scalar", frow[0:1, 0:T], C.pos_row[:], dsv[:, 2:3],
                  ALU.mult, ins=[C.pos_row, dsv], outs=[frow])
            pr.do("v", "tensor_single_scalar", frow[0:1, T:2 * T], C.pos_row[:], dsv[:, 1:2],
                  ALU.mult, ins=[C.pos_row, dsv], outs=[frow])
            pr.do("s", "activation", frow[:], frow[:], AF.Exp, ins=[frow], outs=[frow])
            with pr.guard(nc.psum_tensor("at_pfb", [128, 512], F32, allow_name_mangling=True)) as pfb:
                for qg in range(NQG):
                    for cls in range(2):
                        pr.do("t", "matmul", pfb[:], C.ones_row_bf[:],
                              frow[0:1, cls * T + qg * 512: cls * T + (qg + 1) * 512],
                              ins=[C.ones_row_bf, frow], outs=[pfb],
                              start=True, stop=True)
                        pr.do("v", "tensor_copy",
                              fb[:, (cls * NQG + qg) * 512:(cls * NQG + qg + 1) * 512],
                              pfb[:], ins=[pfb], outs=[fb])
        pr.fence()

        shared = dict(xn_sb=xn_sb, frow=frow, eu=eu, fb=fb, dsv=dsv,
                      dsb=dsb, bq8=bq8, bk_sb=bk_sb, bv_sb=bv_sb)
        for p in range(NC):
            _attn_pair(pr, C, d, layer, p, shared)
            pr.fence()


def _attn_pair(pr, C, d, layer, p, sh):
    nc = pr.nc
    NQG = T // 512
    VW = 130
    xn_sb, frow, eu, fb = sh["xn_sb"], sh["frow"], sh["eu"], sh["fb"]
    dsv, dsb, bq8, bk_sb, bv_sb = sh["dsv"], sh["dsb"], sh["bq8"], sh["bk_sb"], sh["bv_sb"]
    with pr.scope() as _st:
        wsl, qt, kt, v_sb, vl_sb, vu_sb = sb_alloc(
            nc, _st,
            ("at_wsl", [128, 3 * NC * 128], BF16), ("at_qt", [128, T], BF16),
            ("at_kt", [128, T], BF16),
            ("at_v", [128, 32 * VW], BF16), ("at_vl", [128, 32 * VW], BF16),
            ("at_vu", [128, 32 * VW], BF16))
        _stv = _st.enter_context(pr.scope())
        (vt,) = sb_alloc(nc, _stv, ("at_vt", [128, T], BF16))
        if layer == 0:
            for wi, wname in enumerate(("L0_Wq", "L0_Wk", "L0_Wv")):
                pr.dma("g", wsl[:, wi * NC * 128:(wi + 1) * NC * 128]
                       .rearrange("p (c n) -> p c n", c=NC),
                       d[wname].rearrange("(c p) n -> p c n", p=128)
                       [:, :, p * 128:(p + 1) * 128])
        else:
            qkv = d["L1_wqkv"].rearrange("(c p) n -> p c n", p=128)
            for wi in range(3):
                pr.dma("g", wsl[:, wi * NC * 128:(wi + 1) * NC * 128]
                       .rearrange("p (c n) -> p c n", c=NC),
                       qkv[:, :, wi * D + p * 128: wi * D + p * 128 + 128])
        with pr.guard(nc.psum_tensor("at_pj", [128, 512], F32, allow_name_mangling=True)) as pj:
            for wi, dst in ((0, qt), (1, kt), (2, vt)):
                for tb in range(T // 512):
                    t0 = tb * 512
                    for c in range(NC):
                        pr.do("t", "matmul", pj[:],
                              wsl[:, wi * NC * 128 + c * 128:
                                  wi * NC * 128 + (c + 1) * 128],
                              xn_sb[:, c * T + t0: c * T + t0 + 512],
                              ins=[wsl, xn_sb], outs=[pj],
                              start=(c == 0), stop=(c == NC - 1))
                    if layer == 0:
                        bias = (bq8, bk_sb, bv_sb)[wi][:, p:p + 1]
                        sc = 0.125 if wi == 0 else 1.0
                        pr.do("v", "tensor_scalar", dst[:, t0:t0 + 512], pj[:],
                              sc, bias, ALU.mult, ALU.add,
                              ins=[pj, bq8, bk_sb, bv_sb], outs=[dst])
                    elif wi == 0:
                        pr.do("s", "activation", dst[:, t0:t0 + 512], pj[:],
                              AF.Copy, scale=0.125, ins=[pj], outs=[dst])
                    else:
                        pr.do("v", "tensor_copy", dst[:, t0:t0 + 512], pj[:],
                              ins=[pj], outs=[dst])
        pr.fence()
        pr.do("v", "memset", v_sb[:].rearrange("p (t g w) -> p t g w", g=2, w=65)
              [:, :, :, 64:65], 1.0, ins=[], outs=[v_sb])
        with pr.guard(nc.psum_tensor("at_pv", [128, 1024], BF16, allow_name_mangling=True)) as pv:
            for tc in range(T // 128):
                pr.do("t", "transpose", pv[:, 0:128], vt[:, tc * 128:(tc + 1) * 128],
                      C.eye_bf[:], ins=[vt, C.eye_bf], outs=[pv])
                pr.do("v", "tensor_copy",
                      v_sb[:, tc * VW: (tc + 1) * VW]
                      .rearrange("p (g w) -> p g w", g=2)[:, :, 0:64],
                      pv[:, 0:128].rearrange("p (g w) -> p g w", g=2),
                      ins=[pv], outs=[v_sb])
            if layer == 0:
                for tc in range(T // 128):
                    pr.do("v", "tensor_scalar_mul",
                          vl_sb[:, tc * VW:(tc + 1) * VW],
                          v_sb[:, tc * VW:(tc + 1) * VW],
                          eu[:, tc:tc + 1], ins=[v_sb, eu], outs=[vl_sb])
                    pr.do("v", "tensor_scalar_mul",
                          vu_sb[:, tc * VW:(tc + 1) * VW],
                          v_sb[:, tc * VW:(tc + 1) * VW],
                          eu[:, 32 + tc:33 + tc], ins=[v_sb, eu], outs=[vu_sb])
        pr.fence()
        _stv.close()
        with pr.scope() as _st2:
            s00, s01, s10, s11 = ps_alloc(
                nc, _st2,
                ("at_s00", [128, 512], F32), ("at_s01", [128, 512], F32),
                ("at_s10", [128, 512], F32), ("at_s11", [128, 512], F32))
            accs = {}
            for cls in (("L", "U") if layer == 0 else ("L",)):
                for hh in range(2):
                    accs[(cls, hh)] = _st2.enter_context(pr.guard(nc.psum_tensor(
                        f"at_acc{cls}{hh}", [65, 512], F32,
                        allow_name_mangling=True)))
            p00, p01, p10, p11, ep1, ep2, dh, rb_sb, rb2_sb, ofin = sb_alloc(
                nc, _st2,
                ("at_p00", [128, 512], BF16), ("at_p01", [128, 512], BF16),
                ("at_p10", [128, 512], BF16), ("at_p11", [128, 512], BF16),
                ("at_ep1", [128, 512], F32), ("at_ep2", [128, 512], F32),
                ("at_dh", [1, 4 * 512], F32), ("at_rb", [128, 512], F32),
                ("at_rb2", [128, 512], F32),
                ("at_of", [128, 512], BF16))
            S = [[s00, s01], [s10, s11]]
            P = [[p00, p01], [p10, p11]]
            pend_ep = [None]
            for qg in range(NQG):
                q0 = qg * 512
                n_uc = 32 if layer == 0 else 4 * qg + 4

                def classify(uc):
                    if layer == 0:
                        cls = "L" if uc <= 4 * qg + 3 else "U"
                        crossing = 4 * qg <= uc <= 4 * qg + 3
                        k = 0
                    else:
                        cls = "L"
                        crossing = uc >= 4 * qg
                        k = (uc - 4 * qg) if crossing else 0
                    if layer == 0 and cls == "L":
                        first, last = uc == 0, uc == min(4 * qg + 3, n_uc - 1)
                    elif layer == 0:
                        first, last = uc == 4 * qg + 4, uc == 31
                    else:
                        first, last = uc == 0, uc == n_uc - 1
                    return cls, crossing, k, first, last

                def emit_s(uc):
                    cls, crossing, k, first, last = classify(uc)
                    c0 = 128 * k
                    par = uc % 2
                    for hh in range(2):
                        st = S[hh][par]
                        r0, r1 = hh * 64, hh * 64 + 64
                        pr.do("t", "matmul", st[:, c0:512],
                              kt[r0:r1, uc * 128:(uc + 1) * 128],
                              qt[r0:r1, q0 + c0: q0 + 512],
                              ins=[kt, qt], outs=[st], start=True, stop=True,
                              tile_position=(hh * 64, 0), skip_group_check=True)
                        if layer == 0 and crossing:
                            di = uc - 4 * qg
                            pr.do("v", "scalar_tensor_tensor", st[:],
                                  C.cdelta[:, di * 512:(di + 1) * 512],
                                  dsb[:, 1:2], st[:], ALU.mult, ALU.add,
                                  ins=[C.cdelta, dsb, st], outs=[st])
                        if layer == 0 and uc == 0:
                            hg = 2 * p + hh
                            pr.do("v", "tensor_scalar_add", st[0:16, :],
                                  st[0:16, :], C.headmask[0:16, hg:hg + 1],
                                  ins=[st, C.headmask], outs=[st])
                        if layer == 1 and crossing:
                            pr.do("v", "tensor_tensor", st[:, c0:c0 + 128],
                                  st[:, c0:c0 + 128], C.tri[:], ALU.add,
                                  ins=[st, C.tri], outs=[st])

                def emit_exp(uc):
                    cls, crossing, k, first, last = classify(uc)
                    c0 = 128 * k
                    par = uc % 2
                    for hh in range(2):
                        pr.do("s", "activation", P[hh][par][:, c0:512],
                              S[hh][par][:, c0:512],
                              AF.Exp, ins=[S[hh][par]], outs=[P[hh][par]])

                def emit_dv(uc):
                    cls, crossing, k, first, last = classify(uc)
                    c0 = 128 * k
                    par = uc % 2
                    vx = (vl_sb if cls == "L" else vu_sb) if layer == 0 else v_sb
                    for hh in range(2):
                        pt = P[hh][par]
                        acc = accs[(cls, hh)]
                        pr.do("t", "matmul", acc[0:65, c0:512],
                              vx[:, uc * VW + hh * 65: uc * VW + hh * 65 + 65],
                              pt[:, c0:512],
                              ins=[vl_sb, vu_sb, v_sb, pt], outs=[acc],
                              start=first, stop=last,
                              tile_position=(0, 0), skip_group_check=True)

                for uc in range(n_uc):
                    emit_s(uc)
                    if uc == 0 and pend_ep[0] is not None:
                        pend_ep[0]()
                        pend_ep[0] = None
                    if uc > 1:
                        emit_dv(uc - 2)
                    emit_exp(uc)
                emit_dv(n_uc - 2)
                emit_dv(n_uc - 1)
                def _epilogue(qg=qg, q0=q0):
                    pep = s11
                    for hh in range(2):
                        r0, r1 = hh * 64, hh * 64 + 64
                        o = hh * 512
                        al = accs[("L", hh)]
                        if layer == 0:
                            fbl = fb[:, qg * 512:(qg + 1) * 512]
                            fbu = fb[:, (NQG + qg) * 512:(NQG + qg + 1) * 512]
                            au = accs[("U", hh)]
                            pr.do("v", "tensor_tensor", ep1[r0:r1, :], al[0:64, :],
                                  fbl[r0:r1, :], ALU.mult, ins=[al, fb], outs=[ep1])
                            pr.do("v", "tensor_tensor", dh[0:1, o:o + 512], al[64:65, :],
                                  frow[0:1, q0:q0 + 512], ALU.mult,
                                  ins=[al, frow], outs=[dh])
                            if qg < NQG - 1:
                                pr.do("v", "tensor_tensor", ep2[r0:r1, :], au[0:64, :],
                                      fbu[r0:r1, :], ALU.mult, ins=[au, fb], outs=[ep2])
                                pr.do("v", "tensor_add", ep1[r0:r1, :], ep1[r0:r1, :],
                                      ep2[r0:r1, :], ins=[ep1, ep2], outs=[ep1])
                                pr.do("v", "tensor_tensor",
                                      dh[0:1, 1024 + o:1024 + o + 512], au[64:65, :],
                                      frow[0:1, T + q0:T + q0 + 512], ALU.mult,
                                      ins=[au, frow], outs=[dh])
                                pr.do("v", "tensor_add", dh[0:1, o:o + 512],
                                      dh[0:1, o:o + 512],
                                      dh[0:1, 1024 + o:1024 + o + 512],
                                      ins=[dh], outs=[dh])
                        else:
                            pr.do("v", "tensor_copy", ep1[r0:r1, :], al[0:64, :],
                                  ins=[al], outs=[ep1])
                            pr.do("v", "tensor_copy", dh[0:1, o:o + 512], al[64:65, :],
                                  ins=[al], outs=[dh])
                    pr.do("t", "matmul", pep[:], C.maskAf[:], dh[0:1, 0:512],
                          ins=[C.maskAf, dh], outs=[pep], start=True, stop=False,
                          skip_group_check=True)
                    pr.do("t", "matmul", pep[:], C.maskBf[:], dh[0:1, 512:1024],
                          ins=[C.maskBf, dh], outs=[pep], start=False, stop=True,
                          skip_group_check=True)
                    pr.do("v", "tensor_copy", rb_sb[:], pep[:], ins=[pep], outs=[rb_sb])
                    pr.do("v", "reciprocal", rb2_sb[:], rb_sb[:], ins=[rb_sb],
                          outs=[rb2_sb])
                    pr.do("v", "tensor_mul", ofin[:], ep1[:], rb2_sb[:],
                          ins=[ep1, rb2_sb], outs=[ofin])
                    pr.dma("y", fm3(d["o_fm"], q0, 512, p, 1)[:, 0, :], ofin[:])
                pend_ep[0] = _epilogue
            if pend_ep[0] is not None:
                pend_ep[0]()
                pend_ep[0] = None


def emit_proj_ln1(pr, C, d):
    nc = pr.nc
    pr.fence()
    with (
        pr.guard(nc.sbuf_tensor("pl_wo", [128, NC * D], BF16, allow_name_mangling=True)) as wo,
        pr.guard(nc.sbuf_tensor("pl_bo", [128, NC], F32, allow_name_mangling=True)) as bo,
        pr.guard(nc.sbuf_tensor("pl_g", [128, NC], F32, allow_name_mangling=True)) as lg,
        pr.guard(nc.sbuf_tensor("pl_b", [128, NC], F32, allow_name_mangling=True)) as lb,
    ):
        pr.dma("g", wo[:].rearrange("p (c n) -> p c n", c=NC),
               d["L0_Wo"].rearrange("(c p) n -> p c n", p=128))
        pr.dma("y", bo[:], d["L0_bo"])
        pr.dma("y", lg[:], d["L0_ln1g"])
        pr.dma("y", lb[:], d["L0_ln1b"])
        with pr.scope() as _stx:
            ob0, ob1, xnb0, xnb1, sum_sb, nat0, nat1, stt0, stt1, sq0, sq1, xnn0, xnn1, x1t0, x1t1 = sb_alloc(nc, _stx, ("pl_ob", [128, NC * 512], BF16), ("pl_ob1", [128, NC * 512], BF16), ("pl_xnb", [128, NC * 512], BF16), ("pl_xnb1", [128, NC * 512], BF16), ("pl_sum", [128, NC * 512], F32), ("pl_nat", [128, D], F32), ("pl_nat1", [128, D], F32), ("pl_st", [128, 8], F32), ("pl_st1", [128, 8], F32), ("pl_sq", [128, D], F32), ("pl_sq1", [128, D], F32), ("pl_xnn", [128, D], BF16), ("pl_xnn1", [128, D], BF16), ("pl_x1", [128, 1024], BF16), ("pl_x11", [128, 1024], BF16))
            pp0, pp1, pn0, pn1, pf = ps_alloc(nc, _stx, ("pl_pp0", [128, 512], F32), ("pl_pp1", [128, 512], F32), ("pl_pn", [128, 1024], F32), ("pl_pn1", [128, 1024], F32), ("pl_pf", [128, 1024], BF16))
            pps = [pp0, pp1]
            for tb in range(T // 512):
                t0 = tb * 512
                ob, xnb = (ob0, xnb0) if tb % 2 == 0 else (ob1, xnb1)
                load_fm_block(pr, "y", ob, d["o_fm"], t0, 512)
                load_fm_block(pr, "g", xnb, d["xn_fm"], t0, 512)
                for do in range(NC):
                    pp = pps[do % 2]
                    for c in range(NC):
                        pr.do("t", "matmul", pp[:],
                              wo[:, c * D + do * 128: c * D + do * 128 + 128],
                              ob[:, c * 512:(c + 1) * 512], ins=[wo, ob], outs=[pp],
                              start=(c == 0), stop=(c == NC - 1))
                    pr.do("v", "scalar_tensor_tensor",
                          sum_sb[:, do * 512:(do + 1) * 512], pp[:], bo[:, do:do + 1],
                          xnb[:, do * 512:(do + 1) * 512], ALU.add, ALU.add,
                          ins=[pp, bo, xnb], outs=[sum_sb])
                for sub in range(4):
                    nat, stt, sq, xnn, x1t = (nat0, stt0, sq0, xnn0, x1t0) \
                        if sub % 2 == 0 else (nat1, stt1, sq1, xnn1, x1t1)
                    pn = pn0 if sub % 2 == 0 else pn1
                    for do in range(NC):
                        pr.do("t", "transpose", pn[:, do * 128:(do + 1) * 128],
                              sum_sb[:, do * 512 + sub * 128: do * 512 + sub * 128 + 128],
                              C.eye_f32[:], ins=[sum_sb, C.eye_f32], outs=[pn])
                    pr.do("v", "tensor_copy", nat[:], pn[:], ins=[pn], outs=[nat])
                    pr.do("v", "reduce_sum", stt[:, 0:1], nat[:], ins=[nat], outs=[stt],
                          axis=AX.X)
                    pr.do("s", "activation", sq[:], nat[:], AF.Square,
                          ins=[nat], outs=[sq, stt], accum_out=stt[:, 1:2])
                    pr.do("v", "tensor_scalar", stt[:, 2:4], stt[:, 0:2], 1.0 / D, None,
                          ALU.mult, ins=[stt, sq], outs=[stt])
                    pr.do("v", "tensor_mul", stt[:, 4:5], stt[:, 2:3], stt[:, 2:3],
                          ins=[stt], outs=[stt])
                    pr.do("v", "tensor_sub", stt[:, 4:5], stt[:, 3:4], stt[:, 4:5],
                          ins=[stt], outs=[stt])
                    pr.do("s", "activation", stt[:, 5:6], stt[:, 4:5], AF.Sqrt,
                          bias=C.epscol[:], ins=[stt, C.epscol], outs=[stt])
                    pr.do("v", "reciprocal", stt[:, 6:7], stt[:, 5:6], ins=[stt],
                          outs=[stt])
                    pr.do("v", "tensor_scalar", xnn[:], nat[:], stt[:, 2:3],
                          stt[:, 6:7], ALU.subtract, ALU.mult, ins=[nat, stt],
                          outs=[xnn])
                    for cc in range(NC):
                        pr.do("t", "transpose", pf[:, cc * 128:(cc + 1) * 128],
                              xnn[:, cc * 128:(cc + 1) * 128], C.eye_bf[:],
                              ins=[xnn, C.eye_bf], outs=[pf])
                    for cc in range(NC):
                        pr.do("v", "tensor_scalar", x1t[:, cc * 128:(cc + 1) * 128],
                              pf[:, cc * 128:(cc + 1) * 128], lg[:, cc:cc + 1],
                              lb[:, cc:cc + 1], ALU.mult, ALU.add,
                              ins=[pf, lg, lb], outs=[x1t])
                    pr.dma("y", fm3(d["x1_fm"], t0 + sub * 128, 128),
                           x1t[:].rearrange("p (c t) -> p c t", c=NC))


def emit_ffn_combine_l0(pr, C, d):
    nc = pr.nc
    pr.fence()
    NDC = 4
    with (
        pr.guard(nc.sbuf_tensor("ff_b1", [128, DFF // 128], F32, allow_name_mangling=True)) as b1,
        pr.guard(nc.sbuf_tensor("ff_b2", [128, NC], F32, allow_name_mangling=True)) as b2,
        pr.guard(nc.sbuf_tensor("ff_yacc", [128, NC * 2048], F32, allow_name_mangling=True)) as yacc,
    ):
        pr.dma("y", b1[:], d["L0_b1"])
        pr.dma("y", b2[:], d["L0_b2"])
        for th in range(2):
            h0 = th * 2048
            with (
                pr.guard(nc.sbuf_tensor("ff_w1", [128, NC * 1024], BF16, allow_name_mangling=True)) as w1c,
                pr.guard(nc.sbuf_tensor("ff_w2", [128, NC * 1024], BF16, allow_name_mangling=True)) as w2c,
                pr.guard(nc.sbuf_tensor("ff_x1", [128, NC * 512], BF16, allow_name_mangling=True)) as x1b0,
                pr.guard(nc.sbuf_tensor("ff_x1b", [128, NC * 512], BF16, allow_name_mangling=True)) as x1b1,
                pr.guard(nc.sbuf_tensor("ff_h", [128, NC * 512], BF16, allow_name_mangling=True)) as hb,
                pr.guard(nc.psum_tensor("ff_ph", [128, 512], F32, allow_name_mangling=True)) as ph_0,
                pr.guard(nc.psum_tensor("ff_ph1", [128, 512], F32, allow_name_mangling=True)) as ph_1,
                pr.guard(nc.psum_tensor("ff_py0", [128, 512], F32, allow_name_mangling=True)) as py0,
                pr.guard(nc.psum_tensor("ff_py1", [128, 512], F32, allow_name_mangling=True)) as py1,
            ):
                pys = [py0, py1]
                for dc in range(NDC):
                    pr.dma("g", w1c[:].rearrange("p (c n) -> p c n", c=NC),
                           d["L0_W1"].rearrange("(c p) n -> p c n", p=128)
                           [:, :, dc * 1024:(dc + 1) * 1024])
                    pr.dma("g", w2c[:].rearrange("p (c n) -> p c n", c=NC),
                           d["L0_W2"].rearrange("(c p) n -> p c n", p=128)
                           [:, dc * NC:(dc + 1) * NC, :])
                    for tb in range(4):
                        t0 = h0 + tb * 512
                        x1b = x1b0 if tb % 2 == 0 else x1b1
                        load_fm_block(pr, "y", x1b, d["x1_fm"], t0, 512)
                        for sub in range(NC):
                            ph = ph_0 if sub % 2 == 0 else ph_1
                            for c in range(NC):
                                pr.do("t", "matmul", ph[:],
                                      w1c[:, c * 1024 + sub * 128:
                                          c * 1024 + sub * 128 + 128],
                                      x1b[:, c * 512:(c + 1) * 512],
                                      ins=[w1c, x1b], outs=[ph],
                                      start=(c == 0), stop=(c == NC - 1))
                            pr.do("s", "activation", hb[:, sub * 512:(sub + 1) * 512],
                                  ph[:], AF.Relu, bias=b1[:, dc * NC + sub:
                                                          dc * NC + sub + 1],
                                  ins=[ph, b1], outs=[hb])
                        for do in range(NC):
                            py = pys[do % 2]
                            for sub in range(NC):
                                pr.do("t", "matmul", py[:],
                                      w2c[:, sub * 1024 + do * 128:
                                          sub * 1024 + do * 128 + 128],
                                      hb[:, sub * 512:(sub + 1) * 512],
                                      ins=[w2c, hb], outs=[py],
                                      start=(sub == 0), stop=(sub == NC - 1))
                            ysl = yacc[:, do * 2048 + tb * 512:
                                       do * 2048 + (tb + 1) * 512]
                            if dc == 0:
                                pr.do("v", "tensor_copy", ysl, py[:],
                                      ins=[py], outs=[yacc])
                            else:
                                pr.do("v", "tensor_tensor", ysl, ysl, py[:], ALU.add,
                                      ins=[py, yacc], outs=[yacc])
            pr.fence()
            with pr.scope() as _stx:
                x1b0, x1b1, sum_sb0, sum_sb1, nat0, nat1, stt0, stt1, sq0, sq1, s_nat0, s_nat1, gb, bb, gt0, gt1, xt0, xt1, outt0, outt1 = sb_alloc(nc, _stx, ("cb_x1", [128, NC * 512], BF16), ("cb_x12", [128, NC * 512], BF16), ("cb_sum", [128, NC * 128], F32), ("cb_sum1", [128, NC * 128], F32), ("cb_nat", [128, D], F32), ("cb_nat1", [128, D], F32), ("cb_st", [128, 8], F32), ("cb_st1", [128, 8], F32), ("cb_sq", [128, D], F32), ("cb_sq1", [128, D], F32), ("cb_s", [128, D], F32), ("cb_s1", [128, D], F32), ("cb_gb", [128, D], F32), ("cb_bb", [128, D], F32), ("cb_gt", [128, D], F32), ("cb_gt1", [128, D], F32), ("cb_xt", [128, D], F32), ("cb_xt1", [128, D], F32), ("cb_out", [128, D], F32), ("cb_out1", [128, D], F32))
                pn_0, pn_1, pb_ = ps_alloc(nc, _stx, ("cb_pn", [128, 1024], F32), ("cb_pn1", [128, 1024], F32), ("cb_pb", [128, 1024], F32))
                with pr.guard(nc.sbuf_tensor("cb_grow", [1, D], F32, allow_name_mangling=True)) as grow, \
                     pr.guard(nc.sbuf_tensor("cb_brow", [1, D], F32, allow_name_mangling=True)) as brow:
                    pr.dma("y", grow[:], d["L0_ln2g_row"])
                    pr.dma("y", brow[:], d["L0_ln2b_row"])
                    for hv in range(2):
                        pr.do("t", "matmul", pb_[:, hv * 512:(hv + 1) * 512],
                              C.ones_row_f32[:], grow[0:1, hv * 512:(hv + 1) * 512],
                              ins=[C.ones_row_f32, grow], outs=[pb_],
                              start=True, stop=True)
                    pr.do("v", "tensor_copy", gb[:], pb_[:], ins=[pb_], outs=[gb])
                    for hv in range(2):
                        pr.do("t", "matmul", pb_[:, hv * 512:(hv + 1) * 512],
                              C.ones_row_f32[:], brow[0:1, hv * 512:(hv + 1) * 512],
                              ins=[C.ones_row_f32, brow], outs=[pb_],
                              start=True, stop=True)
                    pr.do("v", "tensor_copy", bb[:], pb_[:], ins=[pb_], outs=[bb])
                for tc in range(16):
                    t0 = h0 + tc * 128
                    sum_sb, nat, stt, sq, s_nat = \
                        (sum_sb0, nat0, stt0, sq0, s_nat0) if tc % 2 == 0 \
                        else (sum_sb1, nat1, stt1, sq1, s_nat1)
                    pn = pn_0 if tc % 2 == 0 else pn_1
                    gt, xt, outt = (gt0, xt0, outt0) if tc % 2 == 0 else (gt1, xt1, outt1)
                    x1b = x1b0 if tc % 2 == 0 else x1b1
                    load_fm_block(pr, "y", x1b[:, 0:NC * 128], d["x1_fm"], t0, 128)
                    for do in range(NC):
                        pr.do("v", "scalar_tensor_tensor",
                              sum_sb[:, do * 128:(do + 1) * 128],
                              yacc[:, do * 2048 + tc * 128: do * 2048 + tc * 128 + 128],
                              b2[:, do:do + 1], x1b[:, do * 128:(do + 1) * 128],
                              ALU.add, ALU.add, ins=[yacc, b2, x1b], outs=[sum_sb])
                    for do in range(NC):
                        pr.do("t", "transpose", pn[:, do * 128:(do + 1) * 128],
                              sum_sb[:, do * 128:(do + 1) * 128], C.eye_f32[:],
                              ins=[sum_sb, C.eye_f32], outs=[pn])
                    pr.do("v", "tensor_copy", nat[:], pn[:], ins=[pn], outs=[nat])
                    pr.do("v", "reduce_sum", stt[:, 0:1], nat[:], ins=[nat],
                          outs=[stt], axis=AX.X)
                    pr.do("s", "activation", sq[:], nat[:], AF.Square,
                          ins=[nat], outs=[sq, stt], accum_out=stt[:, 1:2])
                    pr.do("v", "tensor_scalar", stt[:, 2:4], stt[:, 0:2], 1.0 / D,
                          None, ALU.mult, ins=[stt, sq], outs=[stt])
                    pr.do("v", "tensor_mul", stt[:, 4:5], stt[:, 2:3], stt[:, 2:3],
                          ins=[stt], outs=[stt])
                    pr.do("v", "tensor_sub", stt[:, 4:5], stt[:, 3:4], stt[:, 4:5],
                          ins=[stt], outs=[stt])
                    pr.do("s", "activation", stt[:, 5:6], stt[:, 4:5], AF.Sqrt,
                          bias=C.epscol[:], ins=[stt, C.epscol], outs=[stt])
                    pr.do("v", "reciprocal", stt[:, 6:7], stt[:, 5:6], ins=[stt],
                          outs=[stt])
                    pr.do("v", "tensor_scalar", s_nat[:], nat[:], stt[:, 2:3],
                          stt[:, 6:7], ALU.subtract, ALU.mult, ins=[nat, stt],
                          outs=[s_nat])
                    pr.do("v", "tensor_mul", s_nat[:], s_nat[:], gb[:],
                          ins=[s_nat, gb], outs=[s_nat])
                    pr.do("v", "tensor_add", s_nat[:], s_nat[:], bb[:],
                          ins=[s_nat, bb], outs=[s_nat])
                    pr.dma("y", gt[:], d["g_nat"][t0:t0 + 128, :])
                    pr.dma("y", xt[:], d["x0"][t0:t0 + 128, :])
                    pr.do("v", "tensor_mul", s_nat[:], s_nat[:], gt[:],
                          ins=[s_nat, gt], outs=[s_nat])
                    pr.do("v", "tensor_add", outt[:], xt[:], s_nat[:],
                          ins=[xt, s_nat], outs=[outt])
                    pr.dma("y", d["xres_nat"][t0:t0 + 128, :], outt[:])
            pr.fence()

